# version 03
# baseline (speedup 1.0000x reference)
import math
import numpy as np
import ml_dtypes
import concourse.bass as bass
import concourse.mybir as mybir
from concourse.bass_utils import run_bass_kernel_spmd

F32 = mybir.dt.float32
BF16 = mybir.dt.bfloat16
AF = mybir.ActivationFunctionType
ALU = mybir.AluOpType
AX = mybir.AxisListType

P = 128
T, C, NH, HD, RD, KVR, QR, IF = 2048, 1024, 16, 64, 32, 256, 512, 2816
TT_N = T // P
NB = 2
KB = 1024
RSC = 1.0 / math.sqrt(96.0)
AMGM_A = 12.0
FFN_CHUNK = [512] * 5 + [256]
FFN_OFF = [0, 512, 1024, 1536, 2048, 2560]


def _ap_span(ap):
    try:
        name = ap.tensor.name
        esz = {F32: 4, BF16: 2, mybir.dt.uint8: 1}.get(ap.dtype, 4)
        dims = list(ap.ap)
        reach = 1
        for st_, ct_ in dims[1:]:
            reach += (ct_ - 1) * abs(st_)
        lo = ap.offset * esz
        return (name, lo, lo + reach * esz)
    except Exception:
        return None


def _overlap(a, b):
    return a and b and a[0] == b[0] and a[1] < b[2] and b[1] < a[2]


class Ctx:
    ENGS = ("sync", "scalar", "vector", "tensor", "gpsimd")
    AUTO = {"vector": "V", "scalar": "S"}
    _DMAISH = ("dma_start", "dma_start_transpose", "nop")

    def __init__(self, nc):
        self.nc = nc
        self.streams = {e: [] for e in self.ENGS}
        self.sems = {}
        self.cnt = {}
        self._guards = []
        self.recent = {e: [] for e in self.ENGS}

    def sem(self, name):
        if name not in self.sems:
            g = self.nc.semaphore(name)
            h = g.__enter__()
            self._guards.append(g)
            self.sems[name] = h
            self.cnt[name] = 0
        return self.sems[name]

    def op(self, eng, method, *a, inc=None, **k):
        auto = self.AUTO.get(eng) if method not in self._DMAISH else None
        if auto is not None:
            aps = [x for x in a if isinstance(x, bass.AP)]
            aps += [x for x in k.values() if isinstance(x, bass.AP)]
            spans = [_ap_span(x) for x in aps]
            wspans = spans[0:1]
            rspans = spans[1:]
            need = 0
            for seq_after, ws, rs in self.recent[eng]:
                hit = any(_overlap(w, sp) for w in ws for sp in spans) or \
                      any(_overlap(w, r) for w in wspans for r in rs)
                if hit:
                    need = max(need, seq_after)
            if need > 0:
                self.wait(eng, auto, need)
            if inc is None:
                inc = (auto, 1)
            elif inc[0] != auto:
                raise AssertionError(f"op on {eng} with inc {inc}")
            self.sem(auto)
            seq_after = self.cnt[auto] + inc[1]
            self.recent[eng].append((seq_after, wspans, rspans))
            if len(self.recent[eng]) > 8:
                self.recent[eng].pop(0)
        semh, n = None, 0
        if inc is not None:
            name, n = inc
            semh = self.sem(name)
            self.cnt[name] += n

        def run(e, method=method, a=a, k=k, semh=semh, n=n):
            inst = getattr(e, method)(*a, **k)
            if semh is not None:
                inst.then_inc(semh, n)

        self.streams[eng].append(run)

    def wait(self, eng, name, val=None):
        semh = self.sem(name)
        if val is None:
            val = self.cnt[name]
        if val <= 0:
            return

        def run(e, semh=semh, val=val):
            e.wait_ge(semh, val)

        self.streams[eng].append(run)

    def now(self, name):
        self.sem(name)
        return self.cnt[name]

    def barrier(self):
        engs = self.ENGS
        marks = {}
        for e in engs:
            self.op(e, "nop", inc=("bar_" + e, 1))
            marks[e] = self.now("bar_" + e)
        for e in engs:
            for o in engs:
                if o != e:
                    self.wait(e, "bar_" + o, marks[o])


class K:

    def __init__(self, debug=False):
        self.debug = debug
        nc = bass.Bass("TRN2", target_bir_lowering=False, detect_race_conditions=False)
        self.nc = nc
        cx = Ctx(nc)
        self.cx = cx

        dri = lambda n, s, dt=F32: nc.dram_tensor(n, s, dt, kind="ExternalInput").ap()
        self.x_d = dri("x", [NB * T, C])
        self.wkv_d = dri("wkv", [C, KVR])
        self.gkv_d = dri("gkv", [P, 2])
        self.wkdec_d = dri("wkdec", [KVR, C])
        self.wvdec_d = dri("wvdec", [KVR, C])
        self.wq_d = dri("wq", [C, QR])
        self.wqdec_d = dri("wqdec", [QR, C])
        self.wkrope_d = dri("wkrope", [C, QR])
        self.wqrope_d = dri("wqrope", [QR, QR])
        self.wo_d = dri("wo", [C, C])
        self.wg_d = dri("wg", [IF, C])
        self.wu_d = dri("wu", [IF, C])
        self.wd_d = dri("wd", [C, IF])
        self.cos1_d = dri("cos1", [P, T])
        self.sin1_d = dri("sin1", [P, T])
        self.cos2_d = dri("cos2", [P, T])
        self.sin2_d = dri("sin2", [P, T])
        self.mask_d = dri("maskc", [P, 512], BF16)
        self.ident_d = dri("ident", [P, P], BF16)

        self.out_d = nc.dram_tensor("out", [NB * T, C], F32, kind="ExternalOutput").ap()
        self.y1_d = nc.dram_tensor("y1s", [T, C], F32, kind="Internal").ap()
        self.wgq_d = nc.dram_tensor("wgq", [C, IF], BF16, kind="Internal").ap()
        self.wuq_d = nc.dram_tensor("wuq", [C, IF], BF16, kind="Internal").ap()
        self.wdq_d = nc.dram_tensor("wdq", [IF, C], BF16, kind="Internal").ap()

        self.dbg = {}
        if debug:
            def dbg_t(n, s, dt=F32):
                self.dbg[n] = nc.dram_tensor("d_" + n, s, dt, kind="ExternalOutput").ap()
            dbg_t("xn_tm", [P, TT_N * C], BF16)
            dbg_t("kvn_tm", [P, TT_N * KVR], BF16)
            dbg_t("kvc0", [P, KVR], F32)
            dbg_t("sm12", [P, 4], F32)
            dbg_t("qc", [P, 4 * T], BF16)
            dbg_t("khat", [P, 4 * T], BF16)
            dbg_t("qhat", [P, 4 * T], BF16)
            dbg_t("kb0", [97, T], BF16)
            dbg_t("qb0", [97, T], BF16)
            dbg_t("vbuf", [P, TT_N * NH * 65], BF16)
            dbg_t("attn", [P, 8 * T], BF16)
            dbg_t("xq", [P, TT_N * C], BF16)
            dbg_t("hq", [P, IF], BF16)
            dbg_t("wgqd", [C, IF], BF16)

        AR = 202 * KB
        g = nc.sbuf_tensor("arena", [P, AR], mybir.dt.uint8)
        self.arena = g.__enter__().ap()
        cx._guards.append(g)

        self.pb = []
        for i in range(8):
            pg = nc.psum_tensor(f"pb{i}", [P, 512], F32)
            self.pb.append(pg.__enter__().ap())
            cx._guards.append(pg)

    def view(self, off, rows, shape, dt):
        esz = {F32: 4, BF16: 2}[dt]
        n = 1
        for s in shape:
            n *= s
        ap = self.arena[0:rows, off:off + n * esz].bitcast(dt)
        if len(shape) > 1:
            dims = " ".join(f"d{i}" for i in range(len(shape)))
            kw = {f"d{i}": shape[i] for i in range(1, len(shape))}
            ap = ap.rearrange(f"p ({dims}) -> p {dims}", **kw)
        return ap

    def sm(self, i):
        return self.arena[0:P, 1280 + 4 * i: 1280 + 4 * (i + 1)].bitcast(F32)

    @property
    def ident(self):
        return self.arena[0:P, 2560:2816].bitcast(BF16)

    def transpose_blocks(self, src_flat, dst_flat, nblk, banks, tsem="T", vsem="V"):
        cx = self.cx
        v0 = cx.now(vsem)
        for g in range(nblk):
            pbank = banks[g % 2]
            cx.wait("tensor", vsem, v0 + max(0, g - 1))
            cx.op("tensor", "transpose", pbank[:, 0:P],
                  src_flat[:, g * P:(g + 1) * P], self.ident, inc=(tsem, 1))
            cx.wait("vector", tsem)
            cx.op("vector", "tensor_copy", dst_flat[:, g * P:(g + 1) * P],
                  pbank[:, 0:P], inc=(vsem, 1))

    def quant(self, src, out, FD, scr):
        cx = self.cx
        NBLK = FD // 16
        v = self.view(scr, P, [FD], F32)
        t = self.view(scr + 4 * FD, P, [FD], F32)
        s1 = self.view(scr + 8 * FD, P, [FD], F32)
        s2 = self.view(scr + 12 * FD, P, [FD], F32)
        am = self.view(scr + 16 * FD, P, [NBLK], F32)
        sc = self.view(scr + 16 * FD + 4 * NBLK, P, [NBLK], F32)
        rs = self.view(scr + 16 * FD + 8 * NBLK, P, [NBLK], F32)
        t3 = t.rearrange("p (b k) -> p b k", k=16)
        v3 = v.rearrange("p (b k) -> p b k", k=16)
        rs_b = rs.unsqueeze(2).broadcast_to([P, NBLK, 16])
        sc_b = sc.unsqueeze(2).broadcast_to([P, NBLK, 16])
        s13 = s1.rearrange("p (b k) -> p b k", k=16)

        cx.wait("scalar", "V")
        cx.op("scalar", "activation", v[:], src, AF.Abs, inc=("S", 1))
        cx.wait("vector", "S")
        cx.op("vector", "reduce_max", am[:], v3, axis=AX.X)
        cx.op("vector", "tensor_scalar", sc[:], am[:], 1.0 / 6.0, 1e-12, ALU.mult,
              ALU.max)
        cx.op("vector", "reciprocal", rs[:], sc[:])
        cx.op("vector", "tensor_tensor", t3, v3, rs_b, ALU.mult)
        cx.op("vector", "tensor_single_scalar", s1[:], t[:], 0.25, ALU.is_gt)
        cx.op("vector", "scalar_tensor_tensor", s1[:], t[:], 0.75, s1[:], ALU.is_gt, ALU.add)
        cx.op("vector", "scalar_tensor_tensor", s1[:], t[:], 1.25, s1[:], ALU.is_gt, ALU.add)
        cx.op("vector", "scalar_tensor_tensor", s1[:], t[:], 1.75, s1[:], ALU.is_gt, ALU.add)
        cx.op("vector", "tensor_single_scalar", s2[:], t[:], 2.5, ALU.is_gt)
        cx.op("vector", "scalar_tensor_tensor", s2[:], t[:], 3.5, s2[:], ALU.is_gt, ALU.add)
        cx.op("vector", "scalar_tensor_tensor", s2[:], t[:], 5.0, s2[:], ALU.is_gt, ALU.add)
        cx.op("vector", "scalar_tensor_tensor", s2[:], t[:], 5.0, s2[:], ALU.is_gt, ALU.add)
        cx.op("vector", "scalar_tensor_tensor", s1[:], s2[:], 2.0, s1[:], ALU.mult, ALU.add)
        cx.op("vector", "tensor_tensor", s13, s13, sc_b, ALU.mult)
        cx.op("vector", "tensor_scalar", s2[:], src, 0.0, -0.5, ALU.is_gt, ALU.add)
        cx.op("vector", "tensor_tensor", out, s1[:], s2[:], ALU.mult, inc=("V", 1))

    def ln(self, xsrc, xout, FD, xsq):
        cx = self.cx
        sm = self.sm
        cx.op("vector", "tensor_tensor", xsq[:], xsrc, xsrc, ALU.mult)
        cx.op("vector", "reduce_sum", sm(0), xsrc, axis=AX.X)
        cx.op("vector", "reduce_sum", sm(1), xsq[:], axis=AX.X)
        cx.op("vector", "tensor_scalar", sm(0), sm(0), 1.0 / FD, None, ALU.mult)
        cx.op("vector", "tensor_tensor", sm(2), sm(0), sm(0), ALU.mult)
        cx.op("vector", "tensor_scalar", sm(1), sm(1), 1.0 / FD, None, ALU.mult)
        cx.op("vector", "tensor_tensor", sm(1), sm(1), sm(2), ALU.subtract)
        cx.op("vector", "tensor_scalar", sm(1), sm(1), 1e-5, None, ALU.add, inc=("V", 1))
        cx.wait("scalar", "V")
        cx.op("scalar", "activation", sm(1), sm(1), AF.Ln)
        cx.op("scalar", "activation", sm(2), sm(1), AF.Exp, scale=-0.5, inc=("S", 1))
        cx.wait("vector", "S")
        cx.op("vector", "tensor_scalar", xsq[:], xsrc, sm(0), None, ALU.subtract)
        cx.op("vector", "tensor_scalar", xout, xsq[:], sm(2), None, ALU.mult,
              inc=("V", 1))


O_MASK = 0
O_ONES = 1024
O_GKV = 1216
O_RB = 1536
O_W = 4 * KB
O_A = 24 * KB
O_B = 56 * KB
O_H = 88 * KB
O_F = 122 * KB
O_G = 138 * KB
O_J = 154 * KB
O_K = 170 * KB
O_L = 178 * KB
O_M = 186 * KB
F_WDT = 4 * KB
F_XQ = 48 * KB
F_TR = 80 * KB
F_STR = 120 * KB
F_HQ = 152 * KB
PR_STG = 60 * KB
PR_Q = 72 * KB
PR_TP = 78 * KB
PR_SCR = 84 * KB


def emit_prep(k):
    cx, view = k.cx, k.view
    cx.op("sync", "dma_start", out=view(O_MASK, P, [512], BF16), in_=k.mask_d,
          inc=("LX", 16))
    cx.op("sync", "dma_start", out=k.arena[0:P, O_GKV:O_GKV + 8].bitcast(F32),
          in_=k.gkv_d, inc=("LX", 16))
    ones64 = k.arena[0:1, O_ONES:O_ONES + 128].bitcast(BF16)
    ones96 = k.arena[0:96, 1184:1186].bitcast(BF16)
    cx.op("sync", "dma_start", out=k.ident, in_=k.ident_d, inc=("LX", 16))
    cx.wait("vector", "LX")
    cx.op("vector", "memset", ones64, 1.0)
    cx.op("vector", "memset", ones96, 1.0, inc=("V", 1))

    for wsrc, wdst, NT, FD in [(k.wg_d, k.wgq_d, 22, C), (k.wu_d, k.wuq_d, 22, C),
                               (k.wd_d, k.wdq_d, 8, IF)]:
        nstr = FD // P
        for it in range(NT):
            stg = view(PR_STG, P, [FD], F32)
            qo = view(PR_Q, P, [FD], BF16)
            tp = view(PR_TP, P, [nstr * P], BF16)[:]
            cx.wait("sync", "V")
            cx.wait("sync", "TP")
            cx.op("sync", "dma_start", out=stg[:], in_=wsrc[it * P:(it + 1) * P, :],
                  inc=("LX", 16))
            cx.wait("vector", "LX")
            cx.wait("scalar", "LX")
            if FD <= 1024:
                k.quant(stg[:], qo[:], FD, PR_SCR)
            else:
                for c0, c1 in ((0, 1024), (1024, 2048), (2048, 2816)):
                    k.quant(stg[:, c0:c1], qo[:, c0:c1], c1 - c0, PR_SCR)
            cx.wait("tensor", "V")
            cx.wait("tensor", "STP")
            k.transpose_blocks(qo[:], tp, nstr, (k.pb[2].bitcast(BF16), k.pb[3].bitcast(BF16)))
            cx.wait("sync", "V")
            od = wdst.rearrange("(s f) q -> f s q", f=P)[:, :, it * P:(it + 1) * P]
            cx.op("sync", "dma_start", out=od,
                  in_=tp.rearrange("f (s p) -> f s p", s=nstr), inc=("STP", 16))
    cx.wait("sync", "STP")
    if k.debug:
        cx.wait("gpsimd", "STP")
        cx.op("gpsimd", "dma_start", out=k.dbg["wgqd"], in_=k.wgq_d, inc=("DBG", 16))
        for e_ in ("vector", "scalar", "tensor", "sync"):
            cx.wait(e_, "DBG")


def emit_attn(k, b, upto="full"):
    cx, view, sm, pb = k.cx, k.view, k.sm, k.pb
    dbgon = k.debug and b == 0
    x0 = b * T

    wkv_sb = view(O_W, P, [8, KVR], BF16)
    wq_sb = view(O_W + 4 * KB, P, [8, QR], BF16)
    wkrope_sb = view(O_W + 12 * KB, P, [8, QR], BF16)
    wvdec_sb = view(O_W, P, [2, C], BF16)
    wqrope_sb = view(O_W + 12 * KB, P, [4, QR], BF16)
    wkdec_sb = view(O_W + 16 * KB, P, [2, C], BF16)
    wqdec_sb = view(O_W + 4 * KB, P, [4, C], BF16)
    wo_sb = view(O_A + 16 * KB, P, [8, C], BF16)

    xn_tm = view(O_A, P, [TT_N, C], BF16)
    tabs = [view(O_A + i * 8 * KB, P, [T], F32) for i in range(2)]
    ksq = view(O_A, P, [T], BF16)
    p_sb = [view(O_A + 4 * KB + i * KB, P, [512], BF16) for i in range(4)]
    mrow = view(O_A + 8 * KB, 1, [T], BF16)
    R_f32 = view(O_A + 12 * KB, 1, [512], F32)
    pvs = view(O_A + 14 * KB, 64, [512], BF16)
    xn_fm = view(O_B, P, [TT_N, 8, P], BF16)
    attn_sb = view(O_B, P, [8, T], BF16)
    rtmp_a = view(O_B, P, [T], F32)
    rtmp_b = view(O_B + 8 * KB, P, [T], F32)
    kraw = view(O_H, P, [4, T], F32)
    vbuf = view(O_H, P, [TT_N, NH * 65], BF16)
    xq_tm = view(O_H, P, [TT_N, C], BF16)
    khat = view(O_F, P, [4, T], BF16)
    qhat = view(O_G, P, [4, T], BF16)
    kqb = [view(O_J + i * 4 * KB, P, [T], BF16) for i in range(4)]
    ep_x = view(O_J, P, [C], F32)
    ep_y1 = view(O_J + 4 * KB, P, [C], F32)
    ep_xn2 = view(O_J + 8 * KB, P, [C], F32)
    xstage = view(O_K, P, [C], F32)
    xsq = view(O_J + 12 * KB, P, [C], F32)
    kvn_tm = view(O_K, P, [TT_N, KVR], BF16)
    kvn_fm = view(O_L, P, [TT_N, 2, P], BF16)
    qc_sb = view(O_M, P, [4, T], BF16)
    mask_sb = view(O_MASK, P, [512], BF16)
    ones64 = k.arena[0:1, O_ONES:O_ONES + 128].bitcast(BF16)
    ones96 = k.arena[0:96, 1184:1186].bitcast(BF16)
    R_bf = k.arena[0:1, O_RB:O_RB + 1024].bitcast(BF16)
    gkv_col = k.arena[0:P, O_GKV:O_GKV + 8].bitcast(F32)

    for wd_, sb_ in ((k.wkv_d, wkv_sb), (k.wq_d, wq_sb), (k.wkrope_d, wkrope_sb)):
        cx.op("gpsimd", "dma_start", out=sb_, in_=wd_.rearrange("(a p) n -> p a n", p=P),
              inc=("LW", 16))
    lw_a = cx.now("LW")

    for m in range(TT_N):
        cx.wait("sync", "V")
        cx.op("sync", "dma_start", out=xstage[:], in_=k.x_d[x0 + m * P: x0 + (m + 1) * P, :],
              inc=("LX", 16))
        cx.wait("vector", "LX")
        k.ln(xstage[:], xn_tm[:, m, :], C, xsq)
    v_ln = cx.now("V")
    if upto == "ln":
        return

    cx.wait("tensor", "V", v_ln)
    k.transpose_blocks(view(O_A, P, [TT_N * C], BF16)[:], view(O_B, P, [TT_N * C], BF16)[:],
                       TT_N * 8, (pb[2].bitcast(BF16), pb[3].bitcast(BF16)))
    v_xnf = cx.now("V")
    tp_xn = cx.now("T")
    if dbgon:
        cx.wait("gpsimd", "V", v_ln)
        cx.op("gpsimd", "dma_start", out=k.dbg["xn_tm"],
              in_=view(O_A, P, [TT_N * C], BF16)[:], inc=("DBG", 16))
        for e_ in ("vector", "scalar", "tensor", "sync"):
            cx.wait(e_, "DBG")

    cx.wait("tensor", "V", v_xnf)
    cx.wait("tensor", "LW", lw_a)
    for m in range(TT_N):
        bank = pb[m % 2]
        cx.wait("tensor", "V")
        for kk in range(8):
            cx.op("tensor", "matmul", bank[:, 0:KVR], xn_fm[:, m, kk, :],
                  wkv_sb[:, kk, :], start=(kk == 0), stop=(kk == 7),
                  inc=("T", 1) if kk == 7 else None)
        cx.wait("vector", "T")
        if dbgon and m == 0:
            cx.op("vector", "tensor_copy", view(O_A + 20 * KB, P, [KVR], F32)[:],
                  bank[:, 0:KVR])
        cx.op("vector", "tensor_copy", xsq[:, 0:KVR], bank[:, 0:KVR])
        cx.op("vector", "tensor_tensor", xsq[:, 0:KVR], xsq[:, 0:KVR],
              xsq[:, 0:KVR], ALU.mult)
        cx.op("vector", "reduce_sum", sm(0), xsq[:, 0:KVR], axis=AX.X)
        cx.op("vector", "tensor_scalar", sm(1), sm(0), 1.0 / KVR, 1e-6, ALU.mult,
              ALU.add, inc=("V", 1))
        cx.wait("scalar", "V")
        if dbgon and m == 0:
            cx.op("scalar", "activation", view(O_A + 21 * KB + 4, P, [1], F32)[:],
                  sm(1), AF.Identity)
        cx.op("scalar", "activation", sm(1), sm(1), AF.Ln)
        if dbgon and m == 0:
            cx.op("scalar", "activation", view(O_A + 21 * KB + 8, P, [1], F32)[:],
                  sm(1), AF.Identity)
        cx.op("scalar", "activation", sm(2), sm(1), AF.Exp, scale=-0.5, inc=("S", 1))
        cx.wait("vector", "S")
        if dbgon and m == 0:
            cx.op("vector", "tensor_copy", view(O_A + 21 * KB + 12, P, [1], F32)[:], sm(2))
        cx.op("vector", "tensor_scalar", kvn_tm[:, m, :], bank[:, 0:KVR], sm(2), None,
              ALU.mult, inc=("V", 1))
        if dbgon and m == 0:
            cx.wait("gpsimd", "V")
            cx.op("gpsimd", "dma_start", out=k.dbg["kvc0"], in_=view(O_A + 20 * KB, P, [KVR], F32)[:], inc=("DBG", 16))
            cx.op("gpsimd", "dma_start", out=k.dbg["sm12"], in_=view(O_A + 21 * KB, P, [4], F32)[:], inc=("DBG", 16))
            for e_ in ("vector", "scalar", "tensor", "sync"):
                cx.wait(e_, "DBG")
    v_kv = cx.now("V")
    cx.wait("tensor", "V", v_kv)
    k.transpose_blocks(view(O_K, P, [TT_N * KVR], BF16)[:], view(O_L, P, [TT_N * KVR], BF16)[:],
                       TT_N * 2, (pb[2].bitcast(BF16), pb[3].bitcast(BF16)))
    v_kvf = cx.now("V")
    if dbgon:
        cx.wait("gpsimd", "V", v_kv)
        cx.op("gpsimd", "dma_start", out=k.dbg["kvn_tm"],
              in_=view(O_K, P, [TT_N * KVR], BF16)[:], inc=("DBG", 16))
        for e_ in ("vector", "scalar", "tensor", "sync"):
            cx.wait(e_, "DBG")

    for mq in range(4):
        for nt in range(4):
            bank = pb[(mq * 4 + nt) % 2]
            cx.wait("tensor", "V")
            for kk in range(8):
                cx.op("tensor", "matmul",
                      bank[:, 0:512].rearrange("p (a b) -> p a b", a=4),
                      wq_sb[:, kk, mq * P:(mq + 1) * P],
                      xn_fm[:, nt * 4:(nt + 1) * 4, kk, :], start=(kk == 0),
                      stop=(kk == 7), inc=("T", 1) if kk == 7 else None)
            cx.wait("vector", "T")
            cx.op("vector", "tensor_scalar", qc_sb[:, mq, nt * 512:(nt + 1) * 512],
                  bank[:, 0:512], RSC, None, ALU.mult, inc=("V", 1))
    if dbgon:
        cx.wait("gpsimd", "V")
        cx.op("gpsimd", "dma_start", out=k.dbg["qc"], in_=view(O_M, P, [4 * T], BF16)[:],
              inc=("DBG", 16))
        for e_ in ("vector", "scalar", "tensor", "sync"):
            cx.wait(e_, "DBG")

    for mr in range(4):
        for nt in range(4):
            bank = pb[(mr * 4 + nt) % 2]
            cx.wait("tensor", "V")
            for kk in range(8):
                cx.op("tensor", "matmul",
                      bank[:, 0:512].rearrange("p (a b) -> p a b", a=4),
                      wkrope_sb[:, kk, mr * P:(mr + 1) * P],
                      xn_fm[:, nt * 4:(nt + 1) * 4, kk, :], start=(kk == 0),
                      stop=(kk == 7), inc=("T", 1) if kk == 7 else None)
            cx.wait("vector", "T")
            cx.op("vector", "tensor_copy", kraw[:, mr, nt * 512:(nt + 1) * 512],
                  bank[:, 0:512], inc=("V", 1))
    t_kraw = cx.now("T")
    if upto == "proj":
        return

    cx.wait("gpsimd", "T", t_kraw)
    cx.op("gpsimd", "dma_start", out=wqrope_sb,
          in_=k.wqrope_d.rearrange("(a p) n -> p a n", p=P), inc=("LW", 16))
    cx.op("gpsimd", "dma_start", out=wqdec_sb,
          in_=k.wqdec_d.rearrange("(a p) n -> p a n", p=P), inc=("LW", 16))
    lw_b = cx.now("LW")
    for wdsrc, wdsb in ((k.wvdec_d, wvdec_sb), (k.wkdec_d, wkdec_sb)):
        cx.wait("sync", "V")
        for kk in range(2):
            stg = ep_x if kk == 0 else ep_y1
            cx.op("sync", "dma_start", out=stg[:], in_=wdsrc[kk * P:(kk + 1) * P, :],
                  inc=("LX", 16))
        cx.wait("vector", "LX")
        for kk in range(2):
            stg = ep_x if kk == 0 else ep_y1
            cx.op("vector", "tensor_scalar", wdsb[:, kk, :], stg[:],
                  gkv_col[:, kk:kk + 1], None, ALU.mult, inc=("V", 1))
    v_wdec = cx.now("V")

    def rope_pass(raw, hat, tmp_a, tmp_b):
        for second in (False, True):
            cd = k.cos2_d if second else k.cos1_d
            sd = k.sin2_d if second else k.sin1_d
            cx.wait("sync", "V")
            cx.wait("sync", "T", tp_xn)
            cx.op("sync", "dma_start", out=tabs[0][:], in_=cd, inc=("LX", 16))
            cx.op("sync", "dma_start", out=tabs[1][:], in_=sd, inc=("LX", 16))
            cx.wait("vector", "LX")
            for pt in range(2):
                o = (2 if second else 0) + pt
                ob = (0 if second else 2) + pt
                cx.op("vector", "tensor_tensor", tmp_a[:], raw[:, o, :], tabs[0][:],
                      ALU.mult)
                cx.op("vector", "tensor_tensor", tmp_b[:], raw[:, ob, :], tabs[1][:],
                      ALU.mult)
                cx.op("vector", "tensor_tensor", hat[:, o, :], tmp_a[:], tmp_b[:],
                      ALU.add if second else ALU.subtract, inc=("V", 1))

    rope_pass(kraw, khat, rtmp_a, rtmp_b)
    v_khat = cx.now("V")

    qraw = kraw
    cx.wait("tensor", "V", v_khat)
    cx.wait("tensor", "LW", lw_b)
    for mr in range(4):
        for nt in range(4):
            bank = pb[(mr * 4 + nt) % 2]
            cx.wait("tensor", "V")
            for kk in range(4):
                cx.op("tensor", "matmul", bank[:, 0:512],
                      wqrope_sb[:, kk, mr * P:(mr + 1) * P],
                      qc_sb[:, kk, nt * 512:(nt + 1) * 512], start=(kk == 0),
                      stop=(kk == 3), inc=("T", 1) if kk == 3 else None)
            cx.wait("vector", "T")
            cx.op("vector", "tensor_copy", qraw[:, mr, nt * 512:(nt + 1) * 512],
                  bank[:, 0:512], inc=("V", 1))
    rope_pass(qraw, qhat, rtmp_a, rtmp_b)
    v_qhat = cx.now("V")
    if upto == "rope":
        return
    if dbgon:
        cx.wait("gpsimd", "V", v_qhat)
        cx.op("gpsimd", "dma_start", out=k.dbg["khat"], in_=view(O_F, P, [4 * T], BF16)[:],
              inc=("DBG", 16))
        cx.op("gpsimd", "dma_start", out=k.dbg["qhat"], in_=view(O_G, P, [4 * T], BF16)[:],
              inc=("DBG", 16))
        for e_ in ("vector", "scalar", "tensor", "sync"):
            cx.wait(e_, "DBG")

    cx.wait("vector", "V", v_qhat)
    for m in range(TT_N):
        cx.op("vector", "memset",
              vbuf[:, m, :].rearrange("p (h c) -> p h c", c=65)[:, :, 64:65], 1.0,
              inc=("V", 1) if m == TT_N - 1 else None)
    v_ones = cx.now("V")
    cx.wait("tensor", "V", v_ones)
    for m in range(TT_N):
        for nv in range(2):
            bank = pb[(m * 2 + nv) % 2]
            cx.wait("tensor", "V")
            for kk in range(2):
                cx.op("tensor", "matmul", bank[:, 0:512],
                      kvn_fm[:, m, kk, :],
                      wvdec_sb[:, kk, nv * 512:(nv + 1) * 512], start=(kk == 0),
                      stop=(kk == 1), inc=("T", 1) if kk == 1 else None)
            cx.wait("vector", "T")
            cx.op("vector", "tensor_copy",
                  vbuf[:, m, nv * 8 * 65:(nv * 8 + 8) * 65].rearrange(
                      "p (h c) -> p h c", c=65)[:, :, 0:64],
                  bank[:, 0:512].rearrange("p (h c) -> p h c", c=64), inc=("V", 1))
    v_vdone = cx.now("V")
    if upto == "vdec":
        return
    if dbgon:
        cx.wait("gpsimd", "V", v_vdone)
        cx.op("gpsimd", "dma_start", out=k.dbg["vbuf"],
              in_=view(O_H, P, [TT_N * NH * 65], BF16)[:], inc=("DBG", 16))
        for e_ in ("vector", "scalar", "tensor", "sync"):
            cx.wait(e_, "DBG")

    cx.wait("gpsimd", "T", tp_xn)
    cx.op("gpsimd", "dma_start", out=wo_sb, in_=k.wo_d.rearrange("(a p) n -> p a n", p=P),
          inc=("LW", 16))
    lw_o = cx.now("LW")

    for h in range(NH):
        kbx = kqb[(h % 2) * 2]
        qbx = kqb[(h % 2) * 2 + 1]
        cx.wait("vector", "T")
        cx.wait("vector", "S")
        for which, (wsb, nk, bx) in enumerate(
            [(wkdec_sb, 2, kbx), (wqdec_sb, 4, qbx)]
        ):
            for nt in range(4):
                bank = pb[nt % 2]
                cx.wait("tensor", "V")
                for kk in range(nk):
                    if which == 0:
                        mov = kvn_fm[:, nt * 4:(nt + 1) * 4, kk, :]
                        o_ap = bank[0:64, 0:512].rearrange("p (a b) -> p a b", a=4)
                    else:
                        mov = qc_sb[:, kk, nt * 512:(nt + 1) * 512]
                        o_ap = bank[0:64, 0:512]
                    cx.op("tensor", "matmul", o_ap,
                          wsb[:, kk, h * 64:(h + 1) * 64],
                          mov, start=(kk == 0),
                          stop=(kk == nk - 1), inc=("T", 1) if kk == nk - 1 else None)
                cx.wait("vector", "T")
                cx.op("vector", "tensor_copy", bx[0:64, nt * 512:(nt + 1) * 512],
                      bank[0:64, 0:512], inc=("V", 1))
        cx.op("vector", "memset", kbx[96:97, :], 1.0, inc=("V", 1))
        v_dec = cx.now("V")
        src_pt, src_off = h // 8, (h * 16) % P
        cx.wait("sync", "V", v_dec)
        cx.wait("sync", "T")
        for hatsrc, bx in ((khat, kbx), (qhat, qbx)):
            cx.op("sync", "dma_start", out=bx[64:80, :],
                  in_=hatsrc[src_off:src_off + 16, src_pt, :], inc=("AS", 16))
            cx.op("sync", "dma_start", out=bx[80:96, :],
                  in_=hatsrc[src_off:src_off + 16, src_pt + 2, :], inc=("AS", 16))
        as_rope = cx.now("AS")

        cx.wait("vector", "AS", as_rope)
        cx.op("vector", "tensor_tensor", ksq[0:96, :], kbx[0:96, :], kbx[0:96, :],
              ALU.mult, inc=("V", 1))
        vknorm = cx.now("V")
        cx.wait("tensor", "V", vknorm)
        for ch in range(4):
            if ch > 0:
                cx.wait("tensor", "V")
            cx.op("tensor", "matmul", pb[6][0:1, 0:512], ones96,
                  ksq[0:96, ch * 512:(ch + 1) * 512], start=True, stop=True,
                  inc=("T", 1))
            cx.wait("vector", "T")
            cx.op("vector", "reduce_max", sm(4 + ch)[0:1, :], pb[6][0:1, 0:512],
                  axis=AX.X, inc=("V", 1))
        cx.wait("vector", "T")
        cx.op("vector", "tensor_tensor", ksq[0:96, :], qbx[0:96, :], qbx[0:96, :],
              ALU.mult, inc=("V", 1))
        vqnorm = cx.now("V")
        cx.wait("tensor", "V", vqnorm)
        for ch in range(4):
            if ch > 0:
                cx.wait("tensor", "V")
            cx.op("tensor", "matmul", pb[6][0:1, 0:512], ones96,
                  ksq[0:96, ch * 512:(ch + 1) * 512], start=True, stop=True,
                  inc=("T", 1))
            cx.wait("vector", "T")
            cx.op("vector", "reduce_max", sm(8 + ch)[0:1, :], pb[6][0:1, 0:512],
                  axis=AX.X, inc=("V", 1))
        cx.op("vector", "tensor_tensor", sm(4)[0:1, :], sm(4)[0:1, :], sm(5)[0:1, :], ALU.max)
        cx.op("vector", "tensor_tensor", sm(6)[0:1, :], sm(6)[0:1, :], sm(7)[0:1, :], ALU.max)
        cx.op("vector", "tensor_tensor", sm(4)[0:1, :], sm(4)[0:1, :], sm(6)[0:1, :], ALU.max)
        cx.op("vector", "tensor_tensor", sm(8)[0:1, :], sm(8)[0:1, :], sm(9)[0:1, :], ALU.max)
        cx.op("vector", "tensor_tensor", sm(10)[0:1, :], sm(10)[0:1, :], sm(11)[0:1, :], ALU.max)
        cx.op("vector", "tensor_tensor", sm(8)[0:1, :], sm(8)[0:1, :], sm(10)[0:1, :], ALU.max)
        cx.op("vector", "tensor_scalar", sm(4)[0:1, :], sm(4)[0:1, :], 0.5 / AMGM_A,
              None, ALU.mult)
        cx.op("vector", "tensor_scalar", sm(8)[0:1, :], sm(8)[0:1, :], 0.5 * AMGM_A,
              None, ALU.mult)
        cx.op("vector", "tensor_tensor", sm(4)[0:1, :], sm(4)[0:1, :], sm(8)[0:1, :], ALU.add)
        cx.op("vector", "tensor_scalar", sm(4)[0:1, :], sm(4)[0:1, :], 1.0, -1.0,
              ALU.add, ALU.mult)
        cx.op("vector", "tensor_copy", mrow[0:1, :],
              sm(4)[0:1, :].broadcast_to([1, T]), inc=("V", 1))
        v_mrow = cx.now("V")
        cx.wait("sync", "V", v_mrow)
        cx.op("sync", "dma_start", out=qbx[96:97, :], in_=mrow[0:1, :], inc=("AS", 16))
        as_m = cx.now("AS")
        if dbgon and h == 0:
            cx.wait("gpsimd", "AS", as_m)
            cx.op("gpsimd", "dma_start", out=k.dbg["kb0"], in_=kbx[0:97, :], inc=("DBG", 16))
            cx.op("gpsimd", "dma_start", out=k.dbg["qb0"], in_=qbx[0:97, :], inc=("DBG", 16))
            for e_ in ("vector", "scalar", "tensor", "sync"):
                cx.wait(e_, "DBG")

        cx.wait("tensor", "AS", as_m)
        for i in range(4):
            cx.wait("tensor", "V")
            jmax = 4 * i + 4
            for j in range(jmax):
                d = j - 4 * i
                c0 = 128 * d if d >= 0 else 0
                sbank = pb[2 + (j % 2)]
                cx.wait("tensor", "S")
                cx.op("tensor", "matmul", sbank[:, c0:512], kbx[0:97, j * P:(j + 1) * P],
                      qbx[0:97, i * 512 + c0:(i + 1) * 512], start=True, stop=True,
                      inc=("T", 1))
                cx.wait("scalar", "T")
                cx.op("scalar", "activation", p_sb[j % 4][:, c0:512], sbank[:, c0:512],
                      AF.Exp, inc=("S", 1))
                if d >= 0:
                    cx.wait("vector", "S")
                    cx.op("vector", "tensor_tensor", p_sb[j % 4][:, c0:512],
                          p_sb[j % 4][:, c0:512], mask_sb[:, 0:512 - c0], ALU.mult,
                          inc=("V", 1))
                    cx.wait("tensor", "V")
                else:
                    cx.wait("tensor", "S")
                cx.op("tensor", "matmul", pb[4][0:65, c0:512],
                      vbuf[:, j, h * 65:(h + 1) * 65], p_sb[j % 4][:, c0:512],
                      start=(j == 0), stop=(j == jmax - 1), inc=("T", 1))
            cx.wait("vector", "T")
            cx.op("vector", "reciprocal", R_f32[0:1, :], pb[4][64:65, 0:512])
            cx.op("vector", "tensor_copy", R_bf[0:1, 0:512], R_f32[0:1, :],
                  inc=("V", 1))
            cx.wait("tensor", "V")
            cx.op("tensor", "matmul", pb[5][0:64, 0:512], ones64[0:1, 0:64],
                  R_bf[0:1, 0:512], start=True, stop=True, inc=("T", 1))
            cx.op("vector", "tensor_copy", pvs[:, :], pb[4][0:64, 0:512])
            cx.wait("vector", "T")
            cx.op("vector", "tensor_tensor",
                  attn_sb[(h % 2) * 64:(h % 2) * 64 + 64, h // 2, i * 512:(i + 1) * 512],
                  pvs[:, :], pb[5][0:64, 0:512], ALU.mult, inc=("V", 1))
    v_heads = cx.now("V")
    if upto == "heads":
        return
    if dbgon:
        cx.wait("gpsimd", "V", v_heads)
        cx.op("gpsimd", "dma_start", out=k.dbg["attn"], in_=view(O_B, P, [8 * T], BF16)[:],
              inc=("DBG", 16))
        for e_ in ("vector", "scalar", "tensor", "sync"):
            cx.wait(e_, "DBG")

    cx.wait("tensor", "LW", lw_o)
    for m in range(TT_N):
        cx.wait("sync", "V")
        cx.op("sync", "dma_start", out=ep_x[:], in_=k.x_d[x0 + m * P:x0 + (m + 1) * P, :],
              inc=("LX", 16))
        lx_m = cx.now("LX")
        for nf in range(2):
            bank = pb[nf]
            cx.wait("tensor", "V")
            for kk in range(8):
                cx.op("tensor", "matmul", bank[:, 0:512], attn_sb[:, kk, m * P:(m + 1) * P],
                      wo_sb[:, kk, nf * 512:(nf + 1) * 512], start=(kk == 0),
                      stop=(kk == 7), inc=("T", 1) if kk == 7 else None)
        cx.wait("vector", "T")
        cx.wait("vector", "LX", lx_m)
        cx.wait("vector", "SY")
        for nf in range(2):
            cx.op("vector", "tensor_tensor", ep_y1[:, nf * 512:(nf + 1) * 512],
                  pb[nf][:, 0:512], ep_x[:, nf * 512:(nf + 1) * 512], ALU.add,
                  inc=("V", 1) if nf == 1 else None)
        v_y1 = cx.now("V")
        cx.wait("sync", "V", v_y1)
        cx.op("sync", "dma_start", out=k.y1_d[m * P:(m + 1) * P, :], in_=ep_y1[:],
              inc=("SY", 16))
        k.ln(ep_y1[:], ep_xn2[:], C, xsq)
        k.quant(ep_xn2[:], xq_tm[:, m, :], C, O_F)
    if dbgon:
        cx.wait("gpsimd", "V")
        cx.op("gpsimd", "dma_start", out=k.dbg["xq"], in_=view(O_H, P, [TT_N * C], BF16)[:],
              inc=("DBG", 16))
        for e_ in ("vector", "scalar", "tensor", "sync"):
            cx.wait(e_, "DBG")


def emit_ffn(k, b):
    cx, view, pb = k.cx, k.view, k.pb
    dbgon = k.debug and b == 0
    xq_tm = view(O_H, P, [TT_N, C], BF16)
    wdT = view(F_WDT, P, [22, KB], BF16)
    xq_fm = view(F_XQ, P, [TT_N, 8, P], BF16)
    wgn = [view(F_STR + i * 8 * KB, P, [8, 512], BF16) for i in range(2)]
    wun = [view(F_STR + 16 * KB + i * 8 * KB, P, [8, 512], BF16) for i in range(2)]
    hq = [view(F_HQ + m * 5632, P, [IF], BF16) for m in range(8)]
    sG = view(F_TR, P, [512], F32)
    hbuf = view(F_TR + 2 * KB, P, [512], F32)
    hqf = [view(F_TR + 4 * KB + i * 5632, P, [22, P], BF16) for i in range(2)]
    Q_SCR = F_TR + 16 * KB
    y1t = view(F_TR + 26 * KB, P, [C], F32)
    outt = view(F_TR + 30 * KB, P, [C], F32)

    k.transpose_blocks(view(O_H, P, [TT_N * C], BF16)[:], view(F_XQ, P, [TT_N * C], BF16)[:],
                       TT_N * 8, (pb[2].bitcast(BF16), pb[3].bitcast(BF16)))
    tp_xq = cx.now("V")
    cx.op("gpsimd", "dma_start", out=wdT[:], in_=k.wdq_d.rearrange("(a p) n -> p a n", p=P),
          inc=("LW", 16))
    lw_d = cx.now("LW")

    for th in range(2):
        m0 = th * 8
        for n in range(6):
            nb = n % 2
            ncols = FFN_CHUNK[n]
            noff = FFN_OFF[n]
            cx.wait("sync", "T")
            cx.op("sync", "dma_start", out=wgn[nb][:, :, 0:ncols],
                  in_=k.wgq_d[:, noff:noff + ncols].rearrange("(a p) n -> p a n", p=P),
                  inc=("LC", 16))
            cx.op("sync", "dma_start", out=wun[nb][:, :, 0:ncols],
                  in_=k.wuq_d[:, noff:noff + ncols].rearrange("(a p) n -> p a n", p=P),
                  inc=("LC", 16))
            lc = cx.now("LC")
            cx.wait("tensor", "LC", lc)
            cx.wait("tensor", "V", tp_xq)
            for mi in range(8):
                m = m0 + mi
                gbank, ubank = pb[mi % 2], pb[2 + mi % 2]
                cx.wait("tensor", "S")
                cx.wait("tensor", "V")
                for kk in range(8):
                    cx.op("tensor", "matmul", gbank[:, 0:ncols],
                          xq_fm[:, m, kk, :], wgn[nb][:, kk, 0:ncols],
                          start=(kk == 0), stop=(kk == 7),
                          inc=("T", 1) if kk == 7 else None)
                for kk in range(8):
                    cx.op("tensor", "matmul", ubank[:, 0:ncols],
                          xq_fm[:, m, kk, :], wun[nb][:, kk, 0:ncols],
                          start=(kk == 0), stop=(kk == 7),
                          inc=("T", 1) if kk == 7 else None)
                cx.wait("scalar", "T")
                cx.wait("scalar", "V")
                cx.op("scalar", "activation", sG[:, 0:ncols], gbank[:, 0:ncols], AF.Silu,
                      inc=("S", 1))
                cx.wait("vector", "S")
                cx.op("vector", "tensor_tensor", hbuf[:, 0:ncols], sG[:, 0:ncols],
                      ubank[:, 0:ncols], ALU.mult, inc=("V", 1))
                k.quant(hbuf[:, 0:ncols], hq[mi][:, noff:noff + ncols], ncols, Q_SCR)
        v_quant = cx.now("V")
        for mi in range(8):
            m = m0 + mi
            mb = mi % 2
            cx.wait("tensor", "V", v_quant)
            k.transpose_blocks(hq[mi][:], view(F_TR + 4 * KB + mb * 5632, P, [IF], BF16)[:],
                               22, (pb[6].bitcast(BF16), pb[7].bitcast(BF16)))
            tp_h = cx.now("V")
            cx.op("sync", "dma_start", out=y1t[:], in_=k.y1_d[m * P:(m + 1) * P, :],
                  inc=("LX", 16))
            lx = cx.now("LX")
            cx.wait("tensor", "V", tp_h)
            cx.wait("tensor", "LW", lw_d)
            for nf in range(2):
                bank = pb[4 + nf]
                cx.wait("tensor", "V")
                for kk in range(22):
                    cx.op("tensor", "matmul", bank[:, 0:512], hqf[mb][:, kk, :],
                          wdT[:, kk, nf * 512:(nf + 1) * 512], start=(kk == 0),
                          stop=(kk == 21), inc=("T", 1) if kk == 21 else None)
            cx.wait("vector", "T")
            cx.wait("vector", "LX", lx)
            cx.wait("vector", "ST")
            for nf in range(2):
                cx.op("vector", "tensor_tensor", outt[:, nf * 512:(nf + 1) * 512],
                      pb[4 + nf][:, 0:512], y1t[:, nf * 512:(nf + 1) * 512], ALU.add,
                      inc=("V", 1) if nf == 1 else None)
            vout = cx.now("V")
            cx.wait("gpsimd", "V", vout)
            cx.op("gpsimd", "dma_start", out=k.out_d[b * T + m * P:b * T + (m + 1) * P, :],
                  in_=outt[:], inc=("ST", 16))
            cx.wait("sync", "V", vout)
        if dbgon and th == 0:
            cx.wait("gpsimd", "V", v_quant)
            cx.op("gpsimd", "dma_start", out=k.dbg["hq"], in_=hq[0][:], inc=("DBG", 16))
            for e_ in ("vector", "scalar", "tensor", "sync"):
                cx.wait(e_, "DBG")


def build_program(debug=False, upto="full"):
    k = K(debug=debug)
    cx = k.cx
    emit_prep(k)
    cx.barrier()
    if upto != "prep":
        for b in range(NB):
            emit_attn(k, b, upto=upto if upto.startswith("a_") is False else upto[2:])
            cx.barrier()
            if upto.startswith("a_") or upto == "attn0":
                break
            emit_ffn(k, b)
            cx.barrier()
            if upto == "ffn0":
                break
    cx.wait("gpsimd", "ST")
    cx.wait("gpsimd", "DBG")
    cx.wait("gpsimd", "STP")
    cx.wait("gpsimd", "SY")

    with k.nc.Block() as block:
        @block.sync
        def _(e):
            for f in cx.streams["sync"]:
                f(e)

        @block.scalar
        def _(e):
            for f in cx.streams["scalar"]:
                f(e)

        @block.vector
        def _(e):
            for f in cx.streams["vector"]:
                f(e)

        @block.tensor
        def _(e):
            for f in cx.streams["tensor"]:
                f(e)

        @block.gpsimd
        def _(e):
            for f in cx.streams["gpsimd"]:
                f(e)

    return k


_PROG_CACHE = {}


def _get_prog(debug=False):
    key = bool(debug)
    if key not in _PROG_CACHE:
        _PROG_CACHE[key] = build_program(debug=key)
    return _PROG_CACHE[key]


def _host_inputs(inputs):
    f32 = np.float32
    perm = np.zeros(QR, np.int64)
    for h in range(NH):
        for d_ in range(16):
            perm[h * 16 + d_] = h * 32 + d_
            perm[256 + h * 16 + d_] = h * 32 + 16 + d_
    com = {
        "wkv": np.ascontiguousarray(inputs["L0_attn_W_kv"], f32),
        "gkv": np.ascontiguousarray(np.asarray(inputs["L0_attn_g_kv"]).reshape(2, P).T, f32),
        "wkdec": np.ascontiguousarray(inputs["L0_attn_W_kdec"], f32),
        "wvdec": np.ascontiguousarray(inputs["L0_attn_W_vdec"], f32),
        "wq": np.ascontiguousarray(inputs["L0_attn_W_q"], f32),
        "wqdec": np.ascontiguousarray(inputs["L0_attn_W_qdec"], f32),
        "wkrope": np.ascontiguousarray(np.asarray(inputs["L0_attn_W_krope"])[:, perm], f32),
        "wqrope": np.ascontiguousarray(np.asarray(inputs["L0_attn_W_qrope"])[:, perm], f32),
        "wo": np.ascontiguousarray(inputs["L0_attn_W_o"], f32),
        "wg": np.ascontiguousarray(inputs["L0_ffn_w_gate"], f32),
        "wu": np.ascontiguousarray(inputs["L0_ffn_w_up"], f32),
        "wd": np.ascontiguousarray(inputs["L0_ffn_w_down"], f32),
    }
    inv_freq = 1.0 / 10000.0 ** (np.arange(0, RD, 2, dtype=f32) / RD)
    tpos = np.arange(T, dtype=f32)
    freqs = np.outer(tpos, inv_freq)
    emb = np.concatenate([freqs, freqs], -1)
    cos = np.cos(emb).astype(f32)
    sin = np.sin(emb).astype(f32)
    com["cos1"] = np.ascontiguousarray(np.tile(cos[:, :16].T, (8, 1)))
    com["sin1"] = np.ascontiguousarray(np.tile(sin[:, :16].T, (8, 1)))
    com["cos2"] = np.ascontiguousarray(np.tile(cos[:, 16:].T, (8, 1)))
    com["sin2"] = np.ascontiguousarray(np.tile(sin[:, 16:].T, (8, 1)))
    r = np.arange(P)[:, None]
    c = np.arange(512)[None, :]
    com["maskc"] = (r <= c).astype(ml_dtypes.bfloat16)
    com["ident"] = np.eye(P, dtype=ml_dtypes.bfloat16)
    return com


def kernel(**inputs):
    com = _host_inputs(inputs)
    x = np.ascontiguousarray(np.asarray(inputs["L0_attn_x"]), np.float32)
    k = _get_prog(debug=False)
    in_maps = []
    for core in range(8):
        m = dict(com)
        m["x"] = np.ascontiguousarray(x[core * 2:(core + 1) * 2].reshape(NB * T, C))
        in_maps.append(m)
    res = run_bass_kernel_spmd(k.nc, in_maps, core_ids=list(range(8)))
    out = np.empty((16, T, C), np.float32)
    for core in range(8):
        out[core * 2:(core + 1) * 2] = res.results[core]["out"].reshape(NB, T, C)
    return out


# version 04
# speedup vs baseline: 1.2261x; 1.2261x over previous
import math
import numpy as np
import ml_dtypes
import concourse.bass as bass
import concourse.mybir as mybir
from concourse.bass_utils import run_bass_kernel_spmd

F32 = mybir.dt.float32
BF16 = mybir.dt.bfloat16
AF = mybir.ActivationFunctionType
ALU = mybir.AluOpType
AX = mybir.AxisListType

P = 128
T, C, NH, HD, RD, KVR, QR, IF = 2048, 1024, 16, 64, 32, 256, 512, 2816
TT_N = T // P
NB = 2
KB = 1024
RSC = 1.0 / math.sqrt(96.0)
AMGM_A = 12.0
FFN_CHUNK = [512] * 5 + [256]
FFN_OFF = [0, 512, 1024, 1536, 2048, 2560]


def _ap_span(ap):
    try:
        name = ap.tensor.name
        esz = {F32: 4, BF16: 2, mybir.dt.uint8: 1}.get(ap.dtype, 4)
        dims = list(ap.ap)
        reach = 1
        for st_, ct_ in dims[1:]:
            reach += (ct_ - 1) * abs(st_)
        lo = ap.offset * esz
        return (name, lo, lo + reach * esz)
    except Exception:
        return None


def _overlap(a, b):
    return a and b and a[0] == b[0] and a[1] < b[2] and b[1] < a[2]


class Ctx:
    ENGS = ("sync", "scalar", "vector", "tensor", "gpsimd")
    AUTO = {"vector": "V", "scalar": "S"}
    _DMAISH = ("dma_start", "dma_start_transpose", "nop")

    def __init__(self, nc):
        self.nc = nc
        self.streams = {e: [] for e in self.ENGS}
        self.sems = {}
        self.cnt = {}
        self._guards = []
        self.recent = {e: [] for e in self.ENGS}

    def sem(self, name):
        if name not in self.sems:
            g = self.nc.semaphore(name)
            h = g.__enter__()
            self._guards.append(g)
            self.sems[name] = h
            self.cnt[name] = 0
        return self.sems[name]

    def op(self, eng, method, *a, inc=None, **k):
        auto = self.AUTO.get(eng) if method not in self._DMAISH else None
        if auto is not None:
            aps = [x for x in a if isinstance(x, bass.AP)]
            aps += [x for x in k.values() if isinstance(x, bass.AP)]
            spans = [_ap_span(x) for x in aps]
            wspans = spans[0:1]
            rspans = spans[1:]
            need = 0
            for seq_after, ws, rs in self.recent[eng]:
                hit = any(_overlap(w, sp) for w in ws for sp in spans) or \
                      any(_overlap(w, r) for w in wspans for r in rs)
                if hit:
                    need = max(need, seq_after)
            if need > 0:
                self.wait(eng, auto, need)
            if inc is None:
                inc = (auto, 1)
            elif inc[0] != auto:
                raise AssertionError(f"op on {eng} with inc {inc}")
            self.sem(auto)
            seq_after = self.cnt[auto] + inc[1]
            self.recent[eng].append((seq_after, wspans, rspans))
            if len(self.recent[eng]) > 8:
                self.recent[eng].pop(0)
        semh, n = None, 0
        if inc is not None:
            name, n = inc
            semh = self.sem(name)
            self.cnt[name] += n

        def run(e, method=method, a=a, k=k, semh=semh, n=n):
            inst = getattr(e, method)(*a, **k)
            if semh is not None:
                inst.then_inc(semh, n)

        self.streams[eng].append(run)

    def wait(self, eng, name, val=None):
        semh = self.sem(name)
        if val is None:
            val = self.cnt[name]
        if val <= 0:
            return

        def run(e, semh=semh, val=val):
            e.wait_ge(semh, val)

        self.streams[eng].append(run)

    def now(self, name):
        self.sem(name)
        return self.cnt[name]

    def barrier(self):
        engs = self.ENGS
        marks = {}
        for e in engs:
            self.op(e, "nop", inc=("bar_" + e, 1))
            marks[e] = self.now("bar_" + e)
        for e in engs:
            for o in engs:
                if o != e:
                    self.wait(e, "bar_" + o, marks[o])


class K:

    def __init__(self, debug=False):
        self.debug = debug
        nc = bass.Bass("TRN2", target_bir_lowering=False, detect_race_conditions=False)
        self.nc = nc
        cx = Ctx(nc)
        self.cx = cx

        dri = lambda n, s, dt=F32: nc.dram_tensor(n, s, dt, kind="ExternalInput").ap()
        self.x_d = dri("x", [NB * T, C])
        self.wkv_d = dri("wkv", [C, KVR])
        self.gkv_d = dri("gkv", [P, 2])
        self.wkdec_d = dri("wkdec", [KVR, C])
        self.wvdec_d = dri("wvdec", [KVR, C])
        self.wq_d = dri("wq", [C, QR])
        self.wqdec_d = dri("wqdec", [QR, C])
        self.wkrope_d = dri("wkrope", [C, QR])
        self.wqrope_d = dri("wqrope", [QR, QR])
        self.wo_d = dri("wo", [C, C])
        self.wg_d = dri("wg", [IF, C])
        self.wu_d = dri("wu", [IF, C])
        self.wd_d = dri("wd", [C, IF])
        self.cos1_d = dri("cos1", [P, T])
        self.sin1_d = dri("sin1", [P, T])
        self.cos2_d = dri("cos2", [P, T])
        self.sin2_d = dri("sin2", [P, T])
        self.mask_d = dri("maskc", [P, 512], BF16)
        self.ident_d = dri("ident", [P, P], BF16)

        self.out_d = nc.dram_tensor("out", [NB * T, C], F32, kind="ExternalOutput").ap()
        self.y1_d = nc.dram_tensor("y1s", [T, C], F32, kind="Internal").ap()
        self.wgq_d = nc.dram_tensor("wgq", [C, IF], BF16, kind="Internal").ap()
        self.wuq_d = nc.dram_tensor("wuq", [C, IF], BF16, kind="Internal").ap()
        self.wdq_d = nc.dram_tensor("wdq", [IF, C], BF16, kind="Internal").ap()

        self.dbg = {}
        if debug:
            def dbg_t(n, s, dt=F32):
                self.dbg[n] = nc.dram_tensor("d_" + n, s, dt, kind="ExternalOutput").ap()
            dbg_t("xn_tm", [P, TT_N * C], BF16)
            dbg_t("kvn_tm", [P, TT_N * KVR], BF16)
            dbg_t("kvc0", [P, KVR], F32)
            dbg_t("sm12", [P, 4], F32)
            dbg_t("qc", [P, 4 * T], BF16)
            dbg_t("khat", [P, 4 * T], BF16)
            dbg_t("qhat", [P, 4 * T], BF16)
            dbg_t("kb0", [97, T], BF16)
            dbg_t("qb0", [97, T], BF16)
            dbg_t("vbuf", [P, TT_N * NH * 65], BF16)
            dbg_t("attn", [P, 8 * T], BF16)
            dbg_t("xq", [P, TT_N * C], BF16)
            dbg_t("hq", [P, IF], BF16)
            dbg_t("wgqd", [C, IF], BF16)

        AR = 202 * KB
        g = nc.sbuf_tensor("arena", [P, AR], mybir.dt.uint8)
        self.arena = g.__enter__().ap()
        cx._guards.append(g)

        self._qst = {}
        self.pb = []
        for i in range(8):
            pg = nc.psum_tensor(f"pb{i}", [P, 512], F32)
            self.pb.append(pg.__enter__().ap())
            cx._guards.append(pg)

    def view(self, off, rows, shape, dt):
        esz = {F32: 4, BF16: 2}[dt]
        n = 1
        for s in shape:
            n *= s
        ap = self.arena[0:rows, off:off + n * esz].bitcast(dt)
        if len(shape) > 1:
            dims = " ".join(f"d{i}" for i in range(len(shape)))
            kw = {f"d{i}": shape[i] for i in range(1, len(shape))}
            ap = ap.rearrange(f"p ({dims}) -> p {dims}", **kw)
        return ap

    def sm(self, i):
        return self.arena[0:P, 1280 + 4 * i: 1280 + 4 * (i + 1)].bitcast(F32)

    @property
    def ident(self):
        return self.arena[0:P, 2560:2816].bitcast(BF16)

    def transpose_blocks(self, src_flat, dst_flat, nblk, banks, tsem="T", vsem="V"):
        cx = self.cx
        v0 = cx.now(vsem)
        for g in range(nblk):
            pbank = banks[g % 2]
            cx.wait("tensor", vsem, v0 + max(0, g - 1))
            cx.op("tensor", "transpose", pbank[:, 0:P],
                  src_flat[:, g * P:(g + 1) * P], self.ident, inc=(tsem, 1))
            cx.wait("vector", tsem)
            cx.op("vector", "tensor_copy", dst_flat[:, g * P:(g + 1) * P],
                  pbank[:, 0:P], inc=(vsem, 1))

    def quant(self, src, out, FD, scr, sw=None, vw=None):
        cx = self.cx
        NBLK = FD // 16
        v = self.view(scr, P, [FD], F32)
        t = self.view(scr + 4 * FD, P, [FD], F32)
        s1 = self.view(scr + 8 * FD, P, [FD], F32)
        s2 = self.view(scr + 12 * FD, P, [FD], F32)
        am = self.view(scr + 16 * FD, P, [NBLK], F32)
        sc = self.view(scr + 16 * FD + 4 * NBLK, P, [NBLK], F32)
        rs = self.view(scr + 16 * FD + 8 * NBLK, P, [NBLK], F32)
        t3 = t.rearrange("p (b c) -> p b c", c=16)
        v3 = v.rearrange("p (b c) -> p b c", c=16)
        rs_b = rs.unsqueeze(2).broadcast_to([P, NBLK, 16])
        sc_b = sc.unsqueeze(2).broadcast_to([P, NBLK, 16])
        s13 = s1.rearrange("p (b c) -> p b c", c=16)

        st = self._qst.setdefault(scr, {"vfree": 0})
        cx.wait("scalar", "V", st["vfree"])
        if sw:
            for sem, val in sw:
                cx.wait("scalar", sem, val)
        cx.op("scalar", "activation", v[:], src, AF.Abs)
        s_abs = cx.now("S")
        if vw:
            for sem, val in vw:
                cx.wait("vector", sem, val)
        cx.wait("vector", "S", s_abs)
        cx.op("vector", "reduce_max", am[:], v3, axis=AX.X)
        cx.op("vector", "tensor_scalar", sc[:], am[:], 1.0 / 6.0, 1e-12, ALU.mult, ALU.max)
        cx.op("vector", "reciprocal", rs[:], sc[:])
        cx.op("vector", "tensor_tensor", t3, v3, rs_b, ALU.mult)
        st["vfree"] = cx.now("V")
        cx.op("vector", "tensor_single_scalar", s1[:], t[:], 0.25, ALU.is_gt)
        cx.op("vector", "scalar_tensor_tensor", s1[:], t[:], 0.75, s1[:], ALU.is_gt, ALU.add)
        cx.op("vector", "scalar_tensor_tensor", s1[:], t[:], 1.25, s1[:], ALU.is_gt, ALU.add)
        cx.op("vector", "scalar_tensor_tensor", s1[:], t[:], 1.75, s1[:], ALU.is_gt, ALU.add)
        cx.op("vector", "tensor_single_scalar", s2[:], t[:], 2.5, ALU.is_gt)
        cx.op("vector", "scalar_tensor_tensor", s2[:], t[:], 3.5, s2[:], ALU.is_gt, ALU.add)
        cx.op("vector", "scalar_tensor_tensor", s2[:], t[:], 5.0, s2[:], ALU.is_gt, ALU.add)
        cx.op("vector", "scalar_tensor_tensor", s2[:], t[:], 5.0, s2[:], ALU.is_gt, ALU.add)
        cx.op("vector", "scalar_tensor_tensor", s1[:], s2[:], 2.0, s1[:], ALU.mult, ALU.add)
        cx.op("vector", "tensor_tensor", s13, s13, sc_b, ALU.mult)
        cx.op("vector", "tensor_scalar", s2[:], src, 0.0, -0.5, ALU.is_gt, ALU.add)
        cx.op("vector", "tensor_tensor", out, s1[:], s2[:], ALU.mult)

    def ln(self, xsrc, xout, FD, xsq, vw=None):
        cx = self.cx
        sm = self.sm
        if vw:
            for sem, val in vw:
                cx.wait("vector", sem, val)
        cx.op("vector", "tensor_tensor", xsq[:], xsrc, xsrc, ALU.mult)
        cx.op("vector", "reduce_sum", sm(0), xsrc, axis=AX.X)
        cx.op("vector", "reduce_sum", sm(1), xsq[:], axis=AX.X)
        cx.op("vector", "tensor_scalar", sm(0), sm(0), 1.0 / FD, None, ALU.mult)
        cx.op("vector", "tensor_tensor", sm(2), sm(0), sm(0), ALU.mult)
        cx.op("vector", "tensor_scalar", sm(1), sm(1), 1.0 / FD, None, ALU.mult)
        cx.op("vector", "tensor_tensor", sm(1), sm(1), sm(2), ALU.subtract)
        cx.op("vector", "tensor_scalar", sm(1), sm(1), 1e-5, None, ALU.add)
        v_var = cx.now("V")
        cx.wait("scalar", "V", v_var)
        cx.op("scalar", "activation", sm(1), sm(1), AF.Ln)
        cx.op("scalar", "activation", sm(2), sm(1), AF.Exp, scale=-0.5)
        s_r = cx.now("S")
        cx.wait("vector", "S", s_r)
        cx.op("vector", "tensor_scalar", xsq[:], xsrc, sm(0), None, ALU.subtract)
        cx.op("vector", "tensor_scalar", xout, xsq[:], sm(2), None, ALU.mult)


O_MASK = 0
O_ONES = 1024
O_GKV = 1216
O_RB = 1536
O_W = 4 * KB
O_A = 24 * KB
O_B = 56 * KB
O_H = 88 * KB
O_F = 122 * KB
O_G = 138 * KB
O_J = 154 * KB
O_K = 170 * KB
O_L = 178 * KB
O_M = 186 * KB
F_WDT = 4 * KB
F_XQ = 48 * KB
F_TR = 80 * KB
F_STR = 126 * KB
F_HQ = 152 * KB
PR_STG = 60 * KB
PR_Q = 72 * KB
PR_TP = 78 * KB
PR_SCR = 84 * KB


def emit_prep(k):
    cx, view = k.cx, k.view
    cx.op("sync", "dma_start", out=view(O_MASK, P, [512], BF16), in_=k.mask_d,
          inc=("LX", 16))
    cx.op("sync", "dma_start", out=k.arena[0:P, O_GKV:O_GKV + 8].bitcast(F32),
          in_=k.gkv_d, inc=("LX", 16))
    ones64 = k.arena[0:1, O_ONES:O_ONES + 128].bitcast(BF16)
    ones96 = k.arena[0:96, 1184:1186].bitcast(BF16)
    cx.op("sync", "dma_start", out=k.ident, in_=k.ident_d, inc=("LX", 16))
    lx0 = cx.now("LX")
    cx.wait("vector", "LX", lx0)
    cx.op("vector", "memset", ones64, 1.0)
    cx.op("vector", "memset", ones96, 1.0)

    PR_STG2 = [24 * KB, 36 * KB]
    PR_Q2 = [48 * KB, 54 * KB]
    PR_TP2 = [60 * KB, 66 * KB]
    vq_done = [0, 0]
    tq_done = [0, 0]
    st_done = [0, 0]
    gi = 0
    for wsrc, wdst, NT, FD in [(k.wg_d, k.wgq_d, 22, C), (k.wu_d, k.wuq_d, 22, C),
                               (k.wd_d, k.wdq_d, 8, IF)]:
        nstr = FD // P
        for it in range(NT):
            sl = gi % 2
            stg = view(PR_STG2[sl], P, [FD], F32)
            qo = view(PR_Q2[sl], P, [FD], BF16)
            tp = view(PR_TP2[sl], P, [nstr * P], BF16)[:]
            cx.wait("sync", "V", vq_done[sl])
            cx.op("sync", "dma_start", out=stg[:], in_=wsrc[it * P:(it + 1) * P, :],
                  inc=("LX", 16))
            lx = cx.now("LX")
            if FD <= 1024:
                k.quant(stg[:], qo[:], FD, PR_SCR, sw=[("LX", lx)],
                        vw=[("V", tq_done[sl])])
            else:
                for ci, (c0, c1) in enumerate(((0, 1024), (1024, 2048), (2048, 2816))):
                    k.quant(stg[:, c0:c1], qo[:, c0:c1], c1 - c0, PR_SCR,
                            sw=[("LX", lx)],
                            vw=[("V", tq_done[sl])] if ci == 0 else None)
            vq_done[sl] = cx.now("V")
            cx.wait("tensor", "V", vq_done[sl])
            cx.wait("tensor", "STP", st_done[sl])
            k.transpose_blocks(qo[:], tp, nstr,
                               (k.pb[2].bitcast(BF16), k.pb[3].bitcast(BF16)))
            tq_done[sl] = cx.now("V")
            cx.wait("sync", "V", tq_done[sl])
            od = wdst.rearrange("(s f) q -> f s q", f=P)[:, :, it * P:(it + 1) * P]
            cx.op("sync", "dma_start", out=od,
                  in_=tp.rearrange("f (s p) -> f s p", s=nstr), inc=("STP", 16))
            st_done[sl] = cx.now("STP")
            gi += 1
    cx.wait("sync", "STP")
    if k.debug:
        cx.wait("gpsimd", "STP")
        cx.op("gpsimd", "dma_start", out=k.dbg["wgqd"], in_=k.wgq_d, inc=("DBG", 16))
        for e_ in ("vector", "scalar", "tensor", "sync"):
            cx.wait(e_, "DBG")


def emit_attn(k, b, upto="full"):
    cx, view, sm, pb = k.cx, k.view, k.sm, k.pb
    dbgon = k.debug and b == 0
    x0 = b * T

    wkv_sb = view(O_W, P, [8, KVR], BF16)
    wq_sb = view(O_W + 4 * KB, P, [8, QR], BF16)
    wkrope_sb = view(O_W + 12 * KB, P, [8, QR], BF16)
    wvdec_sb = view(O_W, P, [2, C], BF16)
    wqrope_sb = view(O_W + 12 * KB, P, [4, QR], BF16)
    wkdec_sb = view(O_W + 16 * KB, P, [2, C], BF16)
    wqdec_sb = view(O_W + 4 * KB, P, [4, C], BF16)
    wo_sb = view(O_A + 16 * KB, P, [8, C], BF16)

    xn_tm = view(O_A, P, [TT_N, C], BF16)
    tabs = [view(O_A + i * 8 * KB, P, [T], F32) for i in range(2)]
    ksq = view(O_A, P, [T], BF16)
    p_sb = [view(O_A + 4 * KB + i * KB, P, [512], BF16) for i in range(4)]
    mrow = view(O_A + 8 * KB, 1, [T], BF16)
    R_f32 = view(O_A + 12 * KB, 1, [512], F32)
    pvs = view(O_A + 14 * KB, 64, [512], BF16)
    xn_fm = view(O_B, P, [TT_N, 8, P], BF16)
    attn_sb = view(O_B, P, [8, T], BF16)
    rtmp_a = view(O_B, P, [T], F32)
    rtmp_b = view(O_B + 8 * KB, P, [T], F32)
    rtmp_c = view(O_B + 16 * KB, P, [T], F32)
    rtmp_d = view(O_B + 24 * KB, P, [T], F32)
    kraw = view(O_H, P, [4, T], F32)
    vbuf = view(O_H, P, [TT_N, NH * 65], BF16)
    xq_tm = view(O_H, P, [TT_N, C], BF16)
    khat = view(O_F, P, [4, T], BF16)
    qhat = view(O_G, P, [4, T], BF16)
    kqb = [view(O_J + i * 4 * KB, P, [T], BF16) for i in range(4)]
    ep_x = view(O_J, P, [C], F32)
    ep_y1 = view(O_J + 4 * KB, P, [C], F32)
    ep_xn2 = view(O_J + 8 * KB, P, [C], F32)
    xstage = [view(O_K, P, [C], F32), view(O_K + 4 * KB, P, [C], F32)]
    xsq = view(O_J + 12 * KB, P, [C], F32)
    kvn_tm = view(O_K, P, [TT_N, KVR], BF16)
    kvn_fm = view(O_L, P, [TT_N, 2, P], BF16)
    qc_sb = view(O_M, P, [4, T], BF16)
    mask_sb = view(O_MASK, P, [512], BF16)
    ones64 = k.arena[0:1, O_ONES:O_ONES + 128].bitcast(BF16)
    ones96 = k.arena[0:96, 1184:1186].bitcast(BF16)
    R_bf = k.arena[0:1, O_RB:O_RB + 1024].bitcast(BF16)
    gkv_col = k.arena[0:P, O_GKV:O_GKV + 8].bitcast(F32)

    for wd_, sb_ in ((k.wkv_d, wkv_sb), (k.wq_d, wq_sb), (k.wkrope_d, wkrope_sb)):
        cx.op("gpsimd", "dma_start", out=sb_, in_=wd_.rearrange("(a p) n -> p a n", p=P),
              inc=("LW", 16))
    lw_a = cx.now("LW")

    tb23 = (pb[2].bitcast(BF16), pb[3].bitcast(BF16))
    ln_done = [0, 0]
    for m in range(TT_N):
        sl = m % 2
        cx.wait("sync", "V", ln_done[sl])
        cx.op("sync", "dma_start", out=xstage[sl][:],
              in_=k.x_d[x0 + m * P: x0 + (m + 1) * P, :], inc=("LX", 16))
        lx = cx.now("LX")
        k.ln(xstage[sl][:], xn_tm[:, m, :], C, xsq, vw=[("LX", lx)])
        ln_done[sl] = cx.now("V")
        cx.wait("tensor", "V", ln_done[sl])
        k.transpose_blocks(xn_tm[:, m, :],
                           xn_fm[:, m, :, :].rearrange("p a c -> p (a c)"), 8, tb23)
    v_xnf = cx.now("V")
    if upto == "ln":
        return
    if dbgon:
        cx.wait("gpsimd", "V", v_xnf)
        cx.op("gpsimd", "dma_start", out=k.dbg["xn_tm"],
              in_=view(O_A, P, [TT_N * C], BF16)[:], inc=("DBG", 16))
        for e_ in ("vector", "scalar", "tensor", "sync"):
            cx.wait(e_, "DBG")

    cx.wait("tensor", "LW", lw_a)
    cx.wait("tensor", "V", v_xnf)
    b01 = [0, 0]
    for m in range(TT_N):
        sl = m % 2
        bank = pb[sl]
        cx.wait("tensor", "V", b01[sl])
        for kk in range(8):
            cx.op("tensor", "matmul", bank[:, 0:KVR], xn_fm[:, m, kk, :],
                  wkv_sb[:, kk, :], start=(kk == 0), stop=(kk == 7),
                  inc=("T", 1) if kk == 7 else None)
        tmm = cx.now("T")
        cx.wait("vector", "T", tmm)
        cx.op("vector", "tensor_copy", xsq[:, 0:KVR], bank[:, 0:KVR])
        cx.op("vector", "tensor_tensor", xsq[:, 0:KVR], xsq[:, 0:KVR],
              xsq[:, 0:KVR], ALU.mult)
        cx.op("vector", "reduce_sum", sm(0), xsq[:, 0:KVR], axis=AX.X)
        cx.op("vector", "tensor_scalar", sm(1), sm(0), 1.0 / KVR, 1e-6, ALU.mult,
              ALU.add)
        v_var = cx.now("V")
        cx.wait("scalar", "V", v_var)
        cx.op("scalar", "activation", sm(1), sm(1), AF.Ln)
        cx.op("scalar", "activation", sm(2), sm(1), AF.Exp, scale=-0.5)
        s_rstd = cx.now("S")
        cx.wait("vector", "S", s_rstd)
        cx.op("vector", "tensor_scalar", kvn_tm[:, m, :], bank[:, 0:KVR], sm(2), None,
              ALU.mult)
        b01[sl] = cx.now("V")
    v_kv = cx.now("V")
    cx.wait("tensor", "V", v_kv)
    k.transpose_blocks(view(O_K, P, [TT_N * KVR], BF16)[:],
                       view(O_L, P, [TT_N * KVR], BF16)[:], TT_N * 2, tb23)
    v_kvf = cx.now("V")
    if dbgon:
        cx.wait("gpsimd", "V", v_kv)
        cx.op("gpsimd", "dma_start", out=k.dbg["kvn_tm"],
              in_=view(O_K, P, [TT_N * KVR], BF16)[:], inc=("DBG", 16))
        for e_ in ("vector", "scalar", "tensor", "sync"):
            cx.wait(e_, "DBG")

    def proj16(wsb, nk, src_qc, dst, scale, fold4):
        for mq in range(4):
            for nt in range(4):
                i = mq * 4 + nt
                bank = pb[i % 2]
                cx.wait("tensor", "V", b01[i % 2])
                for kk in range(nk):
                    if fold4:
                        o_ap = bank[:, 0:512].rearrange("p (a c) -> p a c", a=4)
                        mov = xn_fm[:, nt * 4:(nt + 1) * 4, kk, :]
                    else:
                        o_ap = bank[:, 0:512]
                        mov = src_qc[:, kk, nt * 512:(nt + 1) * 512]
                    cx.op("tensor", "matmul", o_ap, wsb[:, kk, mq * P:(mq + 1) * P],
                          mov, start=(kk == 0), stop=(kk == nk - 1),
                          inc=("T", 1) if kk == nk - 1 else None)
                tmm = cx.now("T")
                cx.wait("vector", "T", tmm)
                if scale is None:
                    cx.op("vector", "tensor_copy", dst[:, mq, nt * 512:(nt + 1) * 512],
                          bank[:, 0:512])
                else:
                    cx.op("vector", "tensor_scalar", dst[:, mq, nt * 512:(nt + 1) * 512],
                          bank[:, 0:512], scale, None, ALU.mult)
                b01[i % 2] = cx.now("V")

    proj16(wq_sb, 8, None, qc_sb, RSC, True)
    if dbgon:
        cx.wait("gpsimd", "V")
        cx.op("gpsimd", "dma_start", out=k.dbg["qc"], in_=view(O_M, P, [4 * T], BF16)[:],
              inc=("DBG", 16))
        for e_ in ("vector", "scalar", "tensor", "sync"):
            cx.wait(e_, "DBG")
    proj16(wkrope_sb, 8, None, kraw, None, True)
    t_kraw = cx.now("T")
    v_kraw = cx.now("V")
    if upto == "proj":
        return

    cx.wait("gpsimd", "T", t_kraw)
    cx.op("gpsimd", "dma_start", out=wqrope_sb,
          in_=k.wqrope_d.rearrange("(a p) n -> p a n", p=P), inc=("LW", 16))
    cx.op("gpsimd", "dma_start", out=wqdec_sb,
          in_=k.wqdec_d.rearrange("(a p) n -> p a n", p=P), inc=("LW", 16))
    lw_b = cx.now("LW")
    for wdsrc, wdsb in ((k.wvdec_d, wvdec_sb), (k.wkdec_d, wkdec_sb)):
        cx.wait("sync", "V")
        for kk in range(2):
            stg = ep_x if kk == 0 else ep_y1
            cx.op("sync", "dma_start", out=stg[:], in_=wdsrc[kk * P:(kk + 1) * P, :],
                  inc=("LX", 16))
        lx = cx.now("LX")
        cx.wait("vector", "LX", lx)
        for kk in range(2):
            stg = ep_x if kk == 0 else ep_y1
            cx.op("vector", "tensor_scalar", wdsb[:, kk, :], stg[:],
                  gkv_col[:, kk:kk + 1], None, ALU.mult)
    v_wdec = cx.now("V")

    def rope_pass(raw, hat, tmp_a, tmp_b, raw_ready_v, tmp_free_t):
        for second in (False, True):
            cd = k.cos2_d if second else k.cos1_d
            sd = k.sin2_d if second else k.sin1_d
            cx.wait("sync", "V")
            cx.wait("sync", "T", t_kraw)
            cx.op("sync", "dma_start", out=tabs[0][:], in_=cd, inc=("LX", 16))
            cx.op("sync", "dma_start", out=tabs[1][:], in_=sd, inc=("LX", 16))
            lx = cx.now("LX")
            cx.wait("vector", "LX", lx)
            cx.wait("vector", "V", raw_ready_v)
            cx.wait("vector", "T", tmp_free_t)
            for pt in range(2):
                o = (2 if second else 0) + pt
                ob = (0 if second else 2) + pt
                cx.op("vector", "tensor_tensor", tmp_a[:], raw[:, o, :], tabs[0][:],
                      ALU.mult)
                cx.op("vector", "tensor_tensor", tmp_b[:], raw[:, ob, :], tabs[1][:],
                      ALU.mult)
                cx.op("vector", "tensor_tensor", hat[:, o, :], tmp_a[:], tmp_b[:],
                      ALU.add if second else ALU.subtract)

    rope_pass(kraw, khat, rtmp_a, rtmp_b, v_kraw, t_kraw)
    v_khat = cx.now("V")

    qraw = kraw
    cx.wait("tensor", "V", v_khat)
    cx.wait("tensor", "LW", lw_b)
    proj16(wqrope_sb, 4, qc_sb, qraw, None, False)
    v_qraw = cx.now("V")
    rope_pass(qraw, qhat, rtmp_c, rtmp_d, v_qraw, t_kraw)
    v_qhat = cx.now("V")
    if upto == "rope":
        return
    if dbgon:
        cx.wait("gpsimd", "V", v_qhat)
        cx.op("gpsimd", "dma_start", out=k.dbg["khat"], in_=view(O_F, P, [4 * T], BF16)[:],
              inc=("DBG", 16))
        cx.op("gpsimd", "dma_start", out=k.dbg["qhat"], in_=view(O_G, P, [4 * T], BF16)[:],
              inc=("DBG", 16))
        for e_ in ("vector", "scalar", "tensor", "sync"):
            cx.wait(e_, "DBG")

    cx.wait("vector", "V", v_qhat)
    for m in range(TT_N):
        cx.op("vector", "memset",
              vbuf[:, m, :].rearrange("p (h c) -> p h c", c=65)[:, :, 64:65], 1.0)
    v_ones = cx.now("V")
    cx.wait("tensor", "V", v_ones)
    for m in range(TT_N):
        for nv in range(2):
            i = m * 2 + nv
            bank = pb[i % 2]
            cx.wait("tensor", "V", b01[i % 2])
            for kk in range(2):
                cx.op("tensor", "matmul", bank[:, 0:512], kvn_fm[:, m, kk, :],
                      wvdec_sb[:, kk, nv * 512:(nv + 1) * 512], start=(kk == 0),
                      stop=(kk == 1), inc=("T", 1) if kk == 1 else None)
            tmm = cx.now("T")
            cx.wait("vector", "T", tmm)
            cx.op("vector", "tensor_copy",
                  vbuf[:, m, nv * 8 * 65:(nv * 8 + 8) * 65].rearrange(
                      "p (h c) -> p h c", c=65)[:, :, 0:64],
                  bank[:, 0:512].rearrange("p (h c) -> p h c", c=64))
            b01[i % 2] = cx.now("V")
    v_vdone = cx.now("V")
    if upto == "vdec":
        return
    if dbgon:
        cx.wait("gpsimd", "V", v_vdone)
        cx.op("gpsimd", "dma_start", out=k.dbg["vbuf"],
              in_=view(O_H, P, [TT_N * NH * 65], BF16)[:], inc=("DBG", 16))
        for e_ in ("vector", "scalar", "tensor", "sync"):
            cx.wait(e_, "DBG")

    cx.wait("gpsimd", "T", t_kraw)
    cx.op("gpsimd", "dma_start", out=wo_sb, in_=k.wo_d.rearrange("(a p) n -> p a n", p=P),
          inc=("LW", 16))
    lw_o = cx.now("LW")

    sc_banks = [pb[2], pb[3], pb[6]]
    sc_slot3 = [0, 0, 0]
    headA = {}

    def emit_headA(h):
        kbx = kqb[(h % 2) * 2]
        qbx = kqb[(h % 2) * 2 + 1]
        st = {}
        free_t = headA.get(h - 2, {}).get("scoresT", 0)
        src_pt, src_off = h // 8, (h * 16) % P
        cx.wait("sync", "T", free_t)
        cx.wait("sync", "V", v_qhat)
        for half in range(2):
            o1 = src_off
            pt_ = src_pt + (2 if half else 0)
            r0 = 64 + 16 * half
            src2 = k.arena[o1:o1 + 16, O_F + pt_ * 4 * KB:O_F + pt_ * 4 * KB + 2 * T] \
                if False else None
            in3 = k.arena[o1:o1 + 16, O_F + pt_ * 4 * KB: O_G + pt_ * 4 * KB + 4 * KB]
            cx.op("sync", "dma_start", out=kbx[r0:r0 + 16, :],
                  in_=khat[o1:o1 + 16, pt_, :], inc=("AS", 16))
            cx.op("sync", "dma_start", out=qbx[r0:r0 + 16, :],
                  in_=qhat[o1:o1 + 16, pt_, :], inc=("AS", 16))
        as_rope = cx.now("AS")
        for which, (wsb, nk, bx) in enumerate([(wkdec_sb, 2, kbx), (wqdec_sb, 4, qbx)]):
            for nt in range(4):
                bank = pb[nt % 2]
                cx.wait("tensor", "V", b01[nt % 2])
                if which == 0 and nt == 0:
                    cx.wait("tensor", "T", free_t)
                for kk in range(nk):
                    if which == 0:
                        mov = kvn_fm[:, nt * 4:(nt + 1) * 4, kk, :]
                        o_ap = bank[0:64, 0:512].rearrange("p (a c) -> p a c", a=4)
                    else:
                        mov = qc_sb[:, kk, nt * 512:(nt + 1) * 512]
                        o_ap = bank[0:64, 0:512]
                    cx.op("tensor", "matmul", o_ap, wsb[:, kk, h * 64:(h + 1) * 64],
                          mov, start=(kk == 0), stop=(kk == nk - 1),
                          inc=("T", 1) if kk == nk - 1 else None)
                tmm = cx.now("T")
                cx.wait("vector", "T", tmm)
                cx.op("vector", "tensor_copy", bx[0:64, nt * 512:(nt + 1) * 512],
                      bank[0:64, 0:512])
                b01[nt % 2] = cx.now("V")
        cx.op("vector", "memset", kbx[96:97, :], 1.0)
        v_dec = cx.now("V")
        cx.wait("vector", "AS", as_rope)

        def norm_of(bx, outbase):
            cx.op("vector", "tensor_tensor", ksq[0:96, :], bx[0:96, :], bx[0:96, :],
                  ALU.mult)
            for ch in range(4):
                cx.wait("tensor", "V")
                cx.op("tensor", "matmul", pb[7][0:1, 0:512], ones96,
                      ksq[0:96, ch * 512:(ch + 1) * 512], start=True, stop=True,
                      inc=("T", 1))
                tmm = cx.now("T")
                cx.wait("vector", "T", tmm)
                cx.op("vector", "reduce_max", sm(outbase + ch)[0:1, :],
                      pb[7][0:1, 0:512], axis=AX.X)

        norm_of(kbx, 4)
        norm_of(qbx, 8)
        for (a_, b_) in ((4, 5), (6, 7), (4, 6), (8, 9), (10, 11), (8, 10)):
            cx.op("vector", "tensor_tensor", sm(a_)[0:1, :], sm(a_)[0:1, :],
                  sm(b_)[0:1, :], ALU.max)
        cx.op("vector", "tensor_scalar", sm(4)[0:1, :], sm(4)[0:1, :], 0.5 / AMGM_A,
              None, ALU.mult)
        cx.op("vector", "tensor_scalar", sm(8)[0:1, :], sm(8)[0:1, :], 0.5 * AMGM_A,
              None, ALU.mult)
        cx.op("vector", "tensor_tensor", sm(4)[0:1, :], sm(4)[0:1, :], sm(8)[0:1, :],
              ALU.add)
        cx.op("vector", "tensor_scalar", sm(4)[0:1, :], sm(4)[0:1, :], 1.0, -1.0,
              ALU.add, ALU.mult)
        cx.op("vector", "tensor_copy", mrow[0:1, :], sm(4)[0:1, :].broadcast_to([1, T]))
        v_mrow = cx.now("V")
        cx.wait("sync", "V", v_mrow)
        cx.op("sync", "dma_start", out=qbx[96:97, :], in_=mrow[0:1, :], inc=("AS", 16))
        st["asm"] = cx.now("AS")
        st["vdec"] = v_dec
        return st

    def emit_headB(h, st):
        kbx = kqb[(h % 2) * 2]
        qbx = kqb[(h % 2) * 2 + 1]
        cx.wait("tensor", "AS", st["asm"])
        cx.wait("tensor", "V", st["vdec"])
        pv_consumed = 0
        for i in range(4):
            cx.wait("tensor", "V", pv_consumed)
            jmax = 4 * i + 4
            for j in range(jmax):
                d = j - 4 * i
                c0 = 128 * d if d >= 0 else 0
                sl3 = j % 3
                sbank = sc_banks[sl3]
                cx.wait("tensor", "S", sc_slot3[sl3])
                cx.op("tensor", "matmul", sbank[:, c0:512], kbx[0:97, j * P:(j + 1) * P],
                      qbx[0:97, i * 512 + c0:(i + 1) * 512], start=True, stop=True,
                      inc=("T", 1))
                t_sc = cx.now("T")
                cx.wait("scalar", "T", t_sc)
                cx.op("scalar", "activation", p_sb[j % 4][:, c0:512], sbank[:, c0:512],
                      AF.Exp)
                sc_slot3[sl3] = cx.now("S")
                if d >= 0:
                    cx.wait("vector", "S", sc_slot3[sl3])
                    cx.op("vector", "tensor_tensor", p_sb[j % 4][:, c0:512],
                          p_sb[j % 4][:, c0:512], mask_sb[:, 0:512 - c0], ALU.mult)
                    cx.wait("tensor", "V", cx.now("V"))
                else:
                    cx.wait("tensor", "S", sc_slot3[sl3])
                cx.op("tensor", "matmul", pb[4][0:65, c0:512],
                      vbuf[:, j, h * 65:(h + 1) * 65], p_sb[j % 4][:, c0:512],
                      start=(j == 0), stop=(j == jmax - 1), inc=("T", 1))
            t_pv = cx.now("T")
            cx.wait("vector", "T", t_pv)
            cx.op("vector", "reciprocal", R_f32[0:1, :], pb[4][64:65, 0:512])
            cx.op("vector", "tensor_copy", R_bf[0:1, 0:512], R_f32[0:1, :])
            v_r = cx.now("V")
            cx.op("vector", "tensor_copy", pvs[:, :], pb[4][0:64, 0:512])
            cx.wait("tensor", "V", v_r)
            cx.op("tensor", "matmul", pb[5][0:64, 0:512], ones64[0:1, 0:64],
                  R_bf[0:1, 0:512], start=True, stop=True, inc=("T", 1))
            t_rb = cx.now("T")
            cx.wait("vector", "T", t_rb)
            cx.op("vector", "tensor_tensor",
                  attn_sb[(h % 2) * 64:(h % 2) * 64 + 64, h // 2, i * 512:(i + 1) * 512],
                  pvs[:, :], pb[5][0:64, 0:512], ALU.mult)
            pv_consumed = cx.now("V")
        return cx.now("T")

    headA[0] = emit_headA(0)
    for h in range(NH):
        if h + 1 < NH:
            headA[h + 1] = emit_headA(h + 1)
        headA[h]["scoresT"] = emit_headB(h, headA[h])
        if dbgon and h == 0:
            cx.wait("gpsimd", "T", headA[0]["scoresT"])
            cx.op("gpsimd", "dma_start", out=k.dbg["kb0"], in_=kqb[0][0:97, :],
                  inc=("DBG", 16))
            cx.op("gpsimd", "dma_start", out=k.dbg["qb0"], in_=kqb[1][0:97, :],
                  inc=("DBG", 16))
            for e_ in ("vector", "scalar", "tensor", "sync"):
                cx.wait(e_, "DBG")
    v_heads = cx.now("V")
    if upto == "heads":
        return
    if dbgon:
        cx.wait("gpsimd", "V", v_heads)
        cx.op("gpsimd", "dma_start", out=k.dbg["attn"], in_=view(O_B, P, [8 * T], BF16)[:],
              inc=("DBG", 16))
        for e_ in ("vector", "scalar", "tensor", "sync"):
            cx.wait(e_, "DBG")

    cx.wait("tensor", "LW", lw_o)
    t_h14 = headA[NH - 2]["scoresT"]
    t_h15 = headA[NH - 1]["scoresT"]
    cx.wait("sync", "T", t_h14)
    cx.wait("vector", "T", t_h15)
    ep_free = 0
    sy_done = 0
    for m in range(TT_N):
        cx.wait("sync", "V", ep_free)
        cx.op("sync", "dma_start", out=ep_x[:], in_=k.x_d[x0 + m * P:x0 + (m + 1) * P, :],
              inc=("LX", 16))
        lx_m = cx.now("LX")
        for nf in range(2):
            bank = pb[nf]
            cx.wait("tensor", "V", b01[nf])
            for kk in range(8):
                cx.op("tensor", "matmul", bank[:, 0:512], attn_sb[:, kk, m * P:(m + 1) * P],
                      wo_sb[:, kk, nf * 512:(nf + 1) * 512], start=(kk == 0),
                      stop=(kk == 7), inc=("T", 1) if kk == 7 else None)
        t_wo = cx.now("T")
        cx.wait("vector", "T", t_wo)
        cx.wait("vector", "LX", lx_m)
        cx.wait("vector", "SY", sy_done)
        for nf in range(2):
            cx.op("vector", "tensor_tensor", ep_y1[:, nf * 512:(nf + 1) * 512],
                  pb[nf][:, 0:512], ep_x[:, nf * 512:(nf + 1) * 512], ALU.add)
        v_y1 = cx.now("V")
        b01[0] = b01[1] = v_y1
        cx.wait("sync", "V", v_y1)
        cx.op("sync", "dma_start", out=k.y1_d[m * P:(m + 1) * P, :], in_=ep_y1[:],
              inc=("SY", 16))
        sy_done = cx.now("SY")
        k.ln(ep_y1[:], ep_xn2[:], C, xsq)
        vln2 = cx.now("V")
        k.quant(ep_xn2[:], xq_tm[:, m, :], C, O_F, sw=[("V", vln2)])
        ep_free = cx.now("V")
    if dbgon:
        cx.wait("gpsimd", "V")
        cx.op("gpsimd", "dma_start", out=k.dbg["xq"], in_=view(O_H, P, [TT_N * C], BF16)[:],
              inc=("DBG", 16))
        for e_ in ("vector", "scalar", "tensor", "sync"):
            cx.wait(e_, "DBG")


def emit_ffn(k, b):
    cx, view, pb = k.cx, k.view, k.pb
    dbgon = k.debug and b == 0
    xq_tm = view(O_H, P, [TT_N, C], BF16)
    wdT = view(F_WDT, P, [22, KB], BF16)
    xq_fm = view(F_XQ, P, [TT_N, 8, P], BF16)
    wgn = [view(F_STR + i * 8 * KB, P, [8, 512], BF16) for i in range(2)]
    wun = [view(F_STR + 16 * KB + i * 8 * KB, P, [8, 512], BF16) for i in range(2)]
    hq = [view(F_HQ + m * 5632, P, [IF], BF16) for m in range(8)]
    sG = [view(F_TR + i * 2 * KB, P, [512], F32) for i in range(2)]
    hbuf = [view(F_TR + 4 * KB + i * 2 * KB, P, [512], F32) for i in range(2)]
    hqf = [view(F_TR + 8 * KB + i * 5632, P, [22, P], BF16) for i in range(2)]
    Q_SCR = F_TR + 20 * KB
    y1t = [view(F_TR + 30 * KB + i * 4 * KB, P, [C], F32) for i in range(2)]
    outt = [view(F_TR + 38 * KB + i * 4 * KB, P, [C], F32) for i in range(2)]

    tb23 = (pb[2].bitcast(BF16), pb[3].bitcast(BF16))
    k.transpose_blocks(view(O_H, P, [TT_N * C], BF16)[:],
                       view(F_XQ, P, [TT_N * C], BF16)[:], TT_N * 8, tb23)
    tp_xq = cx.now("V")
    cx.op("gpsimd", "dma_start", out=wdT[:], in_=k.wdq_d.rearrange("(a p) n -> p a n", p=P),
          inc=("LW", 16))
    lw_d = cx.now("LW")

    for th in range(2):
        m0 = th * 8
        silu_done = [0, 0]
        hmul_done = [0, 0]
        hbuf_guard = [0, 0]
        quant_last = {}
        t_chunk = [0, 0]
        for n in range(6):
            nb = 0
            ncols = FFN_CHUNK[n]
            noff = FFN_OFF[n]
            cx.wait("sync", "T", t_chunk[nb])
            cx.op("sync", "dma_start", out=wgn[nb][:, :, 0:ncols],
                  in_=k.wgq_d[:, noff:noff + ncols].rearrange("(a p) c -> p a c", p=P),
                  inc=("LC", 16))
            cx.op("sync", "dma_start", out=wun[nb][:, :, 0:ncols],
                  in_=k.wuq_d[:, noff:noff + ncols].rearrange("(a p) c -> p a c", p=P),
                  inc=("LC", 16))
            lc = cx.now("LC")
            cx.wait("tensor", "LC", lc)
            cx.wait("tensor", "V", tp_xq)
            for mi in range(8):
                m = m0 + mi
                gbank, ubank = pb[mi % 2], pb[2 + mi % 2]
                cx.wait("tensor", "S", silu_done[mi % 2])
                cx.wait("tensor", "V", hmul_done[mi % 2])
                for kk in range(8):
                    cx.op("tensor", "matmul", gbank[:, 0:ncols],
                          xq_fm[:, m, kk, :], wgn[nb][:, kk, 0:ncols],
                          start=(kk == 0), stop=(kk == 7),
                          inc=("T", 1) if kk == 7 else None)
                for kk in range(8):
                    cx.op("tensor", "matmul", ubank[:, 0:ncols],
                          xq_fm[:, m, kk, :], wun[nb][:, kk, 0:ncols],
                          start=(kk == 0), stop=(kk == 7),
                          inc=("T", 1) if kk == 7 else None)
                t_mm = cx.now("T")
                cx.wait("scalar", "T", t_mm)
                cx.wait("scalar", "V", hmul_done[mi % 2])
                cx.op("scalar", "activation", sG[mi % 2][:, 0:ncols], gbank[:, 0:ncols],
                      AF.Silu)
                s_silu = cx.now("S")
                silu_done[mi % 2] = s_silu
                cx.wait("vector", "S", s_silu)
                cx.wait("vector", "V", hbuf_guard[mi % 2])
                cx.op("vector", "tensor_tensor", hbuf[mi % 2][:, 0:ncols],
                      sG[mi % 2][:, 0:ncols], ubank[:, 0:ncols], ALU.mult)
                hmul_done[mi % 2] = cx.now("V")
                k.quant(hbuf[mi % 2][:, 0:ncols], hq[mi][:, noff:noff + ncols], ncols,
                        Q_SCR, sw=[("V", hmul_done[mi % 2])])
                quant_last[mi] = hbuf_guard[mi % 2] = cx.now("V")
            t_chunk[nb] = cx.now("T")
        dn_guard = 0
        st_guard = [0, 0]
        for mi in range(8):
            m = m0 + mi
            mb = mi % 2
            cx.wait("tensor", "V", quant_last[mi])
            k.transpose_blocks(hq[mi][:], hqf[mb][:, :, :].rearrange("p a c -> p (a c)"),
                               22, (pb[6].bitcast(BF16), pb[7].bitcast(BF16)))
            tp_h = cx.now("V")
            cx.wait("sync", "V", max(dn_guard, tp_xq))
            cx.op("sync", "dma_start", out=y1t[mb][:], in_=k.y1_d[m * P:(m + 1) * P, :],
                  inc=("LX", 16))
            lx = cx.now("LX")
            cx.wait("tensor", "V", tp_h)
            cx.wait("tensor", "LW", lw_d)
            cx.wait("tensor", "V", dn_guard)
            for nf in range(2):
                bank = pb[4 + nf]
                for kk in range(22):
                    cx.op("tensor", "matmul", bank[:, 0:512], hqf[mb][:, kk, :],
                          wdT[:, kk, nf * 512:(nf + 1) * 512], start=(kk == 0),
                          stop=(kk == 21), inc=("T", 1) if kk == 21 else None)
            t_dn = cx.now("T")
            cx.wait("vector", "T", t_dn)
            cx.wait("vector", "LX", lx)
            cx.wait("vector", "ST", st_guard[mb])
            for nf in range(2):
                cx.op("vector", "tensor_tensor", outt[mb][:, nf * 512:(nf + 1) * 512],
                      pb[4 + nf][:, 0:512], y1t[mb][:, nf * 512:(nf + 1) * 512], ALU.add)
            vout = cx.now("V")
            dn_guard = vout
            cx.wait("gpsimd", "V", vout)
            cx.op("gpsimd", "dma_start", out=k.out_d[b * T + m * P:b * T + (m + 1) * P, :],
                  in_=outt[mb][:], inc=("ST", 16))
            st_guard[mb] = cx.now("ST")
        if dbgon and th == 0:
            cx.wait("gpsimd", "V")
            cx.op("gpsimd", "dma_start", out=k.dbg["hq"], in_=hq[0][:], inc=("DBG", 16))
            for e_ in ("vector", "scalar", "tensor", "sync"):
                cx.wait(e_, "DBG")


def build_program(debug=False, upto="full"):
    k = K(debug=debug)
    cx = k.cx
    emit_prep(k)
    cx.barrier()
    if upto != "prep":
        for b in range(NB):
            emit_attn(k, b, upto=upto if upto.startswith("a_") is False else upto[2:])
            cx.barrier()
            if upto.startswith("a_") or upto == "attn0":
                break
            emit_ffn(k, b)
            cx.barrier()
            if upto == "ffn0":
                break
    cx.wait("gpsimd", "ST")
    cx.wait("gpsimd", "DBG")
    cx.wait("gpsimd", "STP")
    cx.wait("gpsimd", "SY")

    with k.nc.Block() as block:
        @block.sync
        def _(e):
            for f in cx.streams["sync"]:
                f(e)

        @block.scalar
        def _(e):
            for f in cx.streams["scalar"]:
                f(e)

        @block.vector
        def _(e):
            for f in cx.streams["vector"]:
                f(e)

        @block.tensor
        def _(e):
            for f in cx.streams["tensor"]:
                f(e)

        @block.gpsimd
        def _(e):
            for f in cx.streams["gpsimd"]:
                f(e)

    return k


_PROG_CACHE = {}


def _get_prog(debug=False):
    key = bool(debug)
    if key not in _PROG_CACHE:
        _PROG_CACHE[key] = build_program(debug=key)
    return _PROG_CACHE[key]


def _host_inputs(inputs):
    f32 = np.float32
    perm = np.zeros(QR, np.int64)
    for h in range(NH):
        for d_ in range(16):
            perm[h * 16 + d_] = h * 32 + d_
            perm[256 + h * 16 + d_] = h * 32 + 16 + d_
    com = {
        "wkv": np.ascontiguousarray(inputs["L0_attn_W_kv"], f32),
        "gkv": np.ascontiguousarray(np.asarray(inputs["L0_attn_g_kv"]).reshape(2, P).T, f32),
        "wkdec": np.ascontiguousarray(inputs["L0_attn_W_kdec"], f32),
        "wvdec": np.ascontiguousarray(inputs["L0_attn_W_vdec"], f32),
        "wq": np.ascontiguousarray(inputs["L0_attn_W_q"], f32),
        "wqdec": np.ascontiguousarray(inputs["L0_attn_W_qdec"], f32),
        "wkrope": np.ascontiguousarray(np.asarray(inputs["L0_attn_W_krope"])[:, perm], f32),
        "wqrope": np.ascontiguousarray(np.asarray(inputs["L0_attn_W_qrope"])[:, perm], f32),
        "wo": np.ascontiguousarray(inputs["L0_attn_W_o"], f32),
        "wg": np.ascontiguousarray(inputs["L0_ffn_w_gate"], f32),
        "wu": np.ascontiguousarray(inputs["L0_ffn_w_up"], f32),
        "wd": np.ascontiguousarray(inputs["L0_ffn_w_down"], f32),
    }
    inv_freq = 1.0 / 10000.0 ** (np.arange(0, RD, 2, dtype=f32) / RD)
    tpos = np.arange(T, dtype=f32)
    freqs = np.outer(tpos, inv_freq)
    emb = np.concatenate([freqs, freqs], -1)
    cos = np.cos(emb).astype(f32)
    sin = np.sin(emb).astype(f32)
    com["cos1"] = np.ascontiguousarray(np.tile(cos[:, :16].T, (8, 1)))
    com["sin1"] = np.ascontiguousarray(np.tile(sin[:, :16].T, (8, 1)))
    com["cos2"] = np.ascontiguousarray(np.tile(cos[:, 16:].T, (8, 1)))
    com["sin2"] = np.ascontiguousarray(np.tile(sin[:, 16:].T, (8, 1)))
    r = np.arange(P)[:, None]
    c = np.arange(512)[None, :]
    com["maskc"] = (r <= c).astype(ml_dtypes.bfloat16)
    com["ident"] = np.eye(P, dtype=ml_dtypes.bfloat16)
    return com


def kernel(**inputs):
    com = _host_inputs(inputs)
    x = np.ascontiguousarray(np.asarray(inputs["L0_attn_x"]), np.float32)
    k = _get_prog(debug=False)
    in_maps = []
    for core in range(8):
        m = dict(com)
        m["x"] = np.ascontiguousarray(x[core * 2:(core + 1) * 2].reshape(NB * T, C))
        in_maps.append(m)
    res = run_bass_kernel_spmd(k.nc, in_maps, core_ids=list(range(8)))
    out = np.empty((16, T, C), np.float32)
    for core in range(8):
        out[core * 2:(core + 1) * 2] = res.results[core]["out"].reshape(NB, T, C)
    return out


# version 05
# speedup vs baseline: 1.4241x; 1.1616x over previous
import math
import numpy as np
import ml_dtypes
import concourse.bass as bass
import concourse.mybir as mybir
from concourse.bass_utils import run_bass_kernel_spmd

F32 = mybir.dt.float32
BF16 = mybir.dt.bfloat16
AF = mybir.ActivationFunctionType
ALU = mybir.AluOpType
AX = mybir.AxisListType

P = 128
T, C, NH, HD, RD, KVR, QR, IF = 2048, 1024, 16, 64, 32, 256, 512, 2816
TT_N = T // P
NB = 2
KB = 1024
RSC = 1.0 / math.sqrt(96.0)
AMGM_A = 12.0
FFN_CHUNK = [512] * 5 + [256]
FFN_OFF = [0, 512, 1024, 1536, 2048, 2560]


def _ap_span(ap):
    try:
        name = ap.tensor.name
        esz = {F32: 4, BF16: 2, mybir.dt.uint8: 1}.get(ap.dtype, 4)
        dims = list(ap.ap)
        reach = 1
        for st_, ct_ in dims[1:]:
            reach += (ct_ - 1) * abs(st_)
        lo = ap.offset * esz
        return (name, lo, lo + reach * esz)
    except Exception:
        return None


def _overlap(a, b):
    return a and b and a[0] == b[0] and a[1] < b[2] and b[1] < a[2]


class Ctx:
    ENGS = ("sync", "scalar", "vector", "tensor", "gpsimd")
    AUTO = {"vector": "V", "scalar": "S"}
    _DMAISH = ("dma_start", "dma_start_transpose", "nop")

    def __init__(self, nc):
        self.nc = nc
        self.streams = {e: [] for e in self.ENGS}
        self.sems = {}
        self.cnt = {}
        self._guards = []
        self.recent = {e: [] for e in self.ENGS}

    def sem(self, name):
        if name not in self.sems:
            g = self.nc.semaphore(name)
            h = g.__enter__()
            self._guards.append(g)
            self.sems[name] = h
            self.cnt[name] = 0
        return self.sems[name]

    def op(self, eng, method, *a, inc=None, **k):
        auto = self.AUTO.get(eng) if method not in self._DMAISH else None
        if auto is not None:
            aps = [x for x in a if isinstance(x, bass.AP)]
            aps += [x for x in k.values() if isinstance(x, bass.AP)]
            spans = [_ap_span(x) for x in aps]
            wspans = spans[0:1]
            rspans = spans[1:]
            need = 0
            for seq_after, ws, rs in self.recent[eng]:
                hit = any(_overlap(w, sp) for w in ws for sp in spans) or \
                      any(_overlap(w, r) for w in wspans for r in rs)
                if hit:
                    need = max(need, seq_after)
            if need > 0:
                self.wait(eng, auto, need)
            if inc is None:
                inc = (auto, 1)
            elif inc[0] != auto:
                raise AssertionError(f"op on {eng} with inc {inc}")
            self.sem(auto)
            seq_after = self.cnt[auto] + inc[1]
            self.recent[eng].append((seq_after, wspans, rspans))
            if len(self.recent[eng]) > 8:
                self.recent[eng].pop(0)
        semh, n = None, 0
        if inc is not None:
            name, n = inc
            semh = self.sem(name)
            self.cnt[name] += n

        def run(e, method=method, a=a, k=k, semh=semh, n=n):
            inst = getattr(e, method)(*a, **k)
            if semh is not None:
                inst.then_inc(semh, n)

        self.streams[eng].append(run)

    def wait(self, eng, name, val=None):
        semh = self.sem(name)
        if val is None:
            val = self.cnt[name]
        if val <= 0:
            return

        def run(e, semh=semh, val=val):
            e.wait_ge(semh, val)

        self.streams[eng].append(run)

    def now(self, name):
        self.sem(name)
        return self.cnt[name]

    def barrier(self):
        engs = self.ENGS
        marks = {}
        for e in engs:
            self.op(e, "nop", inc=("bar_" + e, 1))
            marks[e] = self.now("bar_" + e)
        for e in engs:
            for o in engs:
                if o != e:
                    self.wait(e, "bar_" + o, marks[o])


class K:

    def __init__(self, debug=False):
        self.debug = debug
        nc = bass.Bass("TRN2", target_bir_lowering=False, detect_race_conditions=False)
        self.nc = nc
        cx = Ctx(nc)
        self.cx = cx

        dri = lambda n, s, dt=F32: nc.dram_tensor(n, s, dt, kind="ExternalInput").ap()
        self.x_d = dri("x", [NB * T, C])
        self.wkv_d = dri("wkv", [C, KVR])
        self.gkv_d = dri("gkv", [P, 2])
        self.wkdec_d = dri("wkdec", [KVR, C])
        self.wvdec_d = dri("wvdec", [KVR, C])
        self.wq_d = dri("wq", [C, QR])
        self.wqdec_d = dri("wqdec", [QR, C])
        self.wkrope_d = dri("wkrope", [C, QR])
        self.wqrope_d = dri("wqrope", [QR, QR])
        self.wo_d = dri("wo", [C, C])
        self.wg_d = dri("wg", [IF, C])
        self.wu_d = dri("wu", [IF, C])
        self.wd_d = dri("wd", [C, IF])
        self.cos1_d = dri("cos1", [P, T])
        self.sin1_d = dri("sin1", [P, T])
        self.cos2_d = dri("cos2", [P, T])
        self.sin2_d = dri("sin2", [P, T])
        self.mask_d = dri("maskc", [P, 512], BF16)
        self.ident_d = dri("ident", [P, P], BF16)

        self.out_d = nc.dram_tensor("out", [NB * T, C], F32, kind="ExternalOutput").ap()
        self.y1_d = nc.dram_tensor("y1s", [T, C], F32, kind="Internal").ap()
        self.wgq_d = nc.dram_tensor("wgq", [C, IF], BF16, kind="Internal").ap()
        self.wuq_d = nc.dram_tensor("wuq", [C, IF], BF16, kind="Internal").ap()
        self.wdq_d = nc.dram_tensor("wdq", [IF, C], BF16, kind="Internal").ap()

        self.dbg = {}
        if debug:
            def dbg_t(n, s, dt=F32):
                self.dbg[n] = nc.dram_tensor("d_" + n, s, dt, kind="ExternalOutput").ap()
            dbg_t("xn_tm", [P, TT_N * C], BF16)
            dbg_t("kvn_tm", [P, TT_N * KVR], BF16)
            dbg_t("kvc0", [P, KVR], F32)
            dbg_t("sm12", [P, 4], F32)
            dbg_t("qc", [P, 4 * T], BF16)
            dbg_t("khat", [P, 4 * T], BF16)
            dbg_t("qhat", [P, 4 * T], BF16)
            dbg_t("kb0", [97, T], BF16)
            dbg_t("qb0", [97, T], BF16)
            dbg_t("vbuf", [P, TT_N * NH * 65], BF16)
            dbg_t("attn", [P, 8 * T], BF16)
            dbg_t("xq", [P, TT_N * C], BF16)
            dbg_t("hq", [P, IF], BF16)
            dbg_t("wgqd", [C, IF], BF16)

        AR = 202 * KB
        g = nc.sbuf_tensor("arena", [P, AR], mybir.dt.uint8)
        self.arena = g.__enter__().ap()
        cx._guards.append(g)

        self._qst = {}
        self.pb = []
        for i in range(8):
            pg = nc.psum_tensor(f"pb{i}", [P, 512], F32)
            self.pb.append(pg.__enter__().ap())
            cx._guards.append(pg)

    def view(self, off, rows, shape, dt):
        esz = {F32: 4, BF16: 2}[dt]
        n = 1
        for s in shape:
            n *= s
        ap = self.arena[0:rows, off:off + n * esz].bitcast(dt)
        if len(shape) > 1:
            dims = " ".join(f"d{i}" for i in range(len(shape)))
            kw = {f"d{i}": shape[i] for i in range(1, len(shape))}
            ap = ap.rearrange(f"p ({dims}) -> p {dims}", **kw)
        return ap

    def sm(self, i):
        return self.arena[0:P, 1280 + 4 * i: 1280 + 4 * (i + 1)].bitcast(F32)

    @property
    def ident(self):
        return self.arena[0:P, 2560:2816].bitcast(BF16)

    def transpose_blocks(self, src_flat, dst_flat, nblk, banks, tsem="T", vsem="V"):
        cx = self.cx
        v0 = cx.now(vsem)
        for g in range(nblk):
            pbank = banks[g % 2]
            cx.wait("tensor", vsem, v0 + max(0, g - 1))
            cx.op("tensor", "transpose", pbank[:, 0:P],
                  src_flat[:, g * P:(g + 1) * P], self.ident, inc=(tsem, 1))
            cx.wait("vector", tsem)
            cx.op("vector", "tensor_copy", dst_flat[:, g * P:(g + 1) * P],
                  pbank[:, 0:P], inc=(vsem, 1))

    def quant(self, src, out, FD, scr, sw=None, vw=None):
        cx = self.cx
        NBLK = FD // 16
        v = self.view(scr, P, [FD], F32)
        t = self.view(scr + 4 * FD, P, [FD], F32)
        s1 = self.view(scr + 8 * FD, P, [FD], F32)
        s2 = self.view(scr + 12 * FD, P, [FD], F32)
        am = self.view(scr + 16 * FD, P, [NBLK], F32)
        sc = self.view(scr + 16 * FD + 4 * NBLK, P, [NBLK], F32)
        rs = self.view(scr + 16 * FD + 8 * NBLK, P, [NBLK], F32)
        t3 = t.rearrange("p (b c) -> p b c", c=16)
        v3 = v.rearrange("p (b c) -> p b c", c=16)
        rs_b = rs.unsqueeze(2).broadcast_to([P, NBLK, 16])
        sc_b = sc.unsqueeze(2).broadcast_to([P, NBLK, 16])
        s13 = s1.rearrange("p (b c) -> p b c", c=16)

        st = self._qst.setdefault(scr, {"vfree": 0})
        cx.wait("scalar", "V", st["vfree"])
        if sw:
            for sem, val in sw:
                cx.wait("scalar", sem, val)
        cx.op("scalar", "activation", v[:], src, AF.Abs)
        s_abs = cx.now("S")
        if vw:
            for sem, val in vw:
                cx.wait("vector", sem, val)
        cx.wait("vector", "S", s_abs)
        cx.op("vector", "reduce_max", am[:], v3, axis=AX.X)
        cx.op("vector", "tensor_scalar", sc[:], am[:], 1.0 / 6.0, 1e-12, ALU.mult, ALU.max)
        cx.op("vector", "reciprocal", rs[:], sc[:])
        cx.op("vector", "tensor_tensor", t3, v3, rs_b, ALU.mult)
        st["vfree"] = cx.now("V")
        cx.op("vector", "tensor_single_scalar", s1[:], t[:], 0.25, ALU.is_gt)
        cx.op("vector", "scalar_tensor_tensor", s1[:], t[:], 0.75, s1[:], ALU.is_gt, ALU.add)
        cx.op("vector", "scalar_tensor_tensor", s1[:], t[:], 1.25, s1[:], ALU.is_gt, ALU.add)
        cx.op("vector", "scalar_tensor_tensor", s1[:], t[:], 1.75, s1[:], ALU.is_gt, ALU.add)
        cx.op("vector", "tensor_single_scalar", s2[:], t[:], 2.5, ALU.is_gt)
        cx.op("vector", "scalar_tensor_tensor", s2[:], t[:], 3.5, s2[:], ALU.is_gt, ALU.add)
        cx.op("vector", "scalar_tensor_tensor", s2[:], t[:], 5.0, s2[:], ALU.is_gt, ALU.add)
        cx.op("vector", "scalar_tensor_tensor", s2[:], t[:], 5.0, s2[:], ALU.is_gt, ALU.add)
        cx.op("vector", "scalar_tensor_tensor", s1[:], s2[:], 2.0, s1[:], ALU.mult, ALU.add)
        cx.op("vector", "tensor_tensor", s13, s13, sc_b, ALU.mult)
        cx.op("vector", "tensor_scalar", s2[:], src, 0.0, -0.5, ALU.is_gt, ALU.add)
        cx.op("vector", "tensor_tensor", out, s1[:], s2[:], ALU.mult)

    def ln(self, xsrc, xout, FD, xsq, vw=None):
        cx = self.cx
        sm = self.sm
        if vw:
            for sem, val in vw:
                cx.wait("vector", sem, val)
        cx.op("vector", "tensor_tensor", xsq[:], xsrc, xsrc, ALU.mult)
        cx.op("vector", "reduce_sum", sm(0), xsrc, axis=AX.X)
        cx.op("vector", "reduce_sum", sm(1), xsq[:], axis=AX.X)
        cx.op("vector", "tensor_scalar", sm(0), sm(0), 1.0 / FD, None, ALU.mult)
        cx.op("vector", "tensor_tensor", sm(2), sm(0), sm(0), ALU.mult)
        cx.op("vector", "tensor_scalar", sm(1), sm(1), 1.0 / FD, None, ALU.mult)
        cx.op("vector", "tensor_tensor", sm(1), sm(1), sm(2), ALU.subtract)
        cx.op("vector", "tensor_scalar", sm(1), sm(1), 1e-5, None, ALU.add)
        v_var = cx.now("V")
        cx.wait("scalar", "V", v_var)
        cx.op("scalar", "activation", sm(1), sm(1), AF.Ln)
        cx.op("scalar", "activation", sm(2), sm(1), AF.Exp, scale=-0.5)
        s_r = cx.now("S")
        cx.wait("vector", "S", s_r)
        cx.op("vector", "tensor_scalar", xsq[:], xsrc, sm(0), None, ALU.subtract)
        cx.op("vector", "tensor_scalar", xout, xsq[:], sm(2), None, ALU.mult)


O_MASK = 0
O_ONES = 1024
O_GKV = 1216
O_RB = 1536
O_W = 4 * KB
O_A = 24 * KB
O_B = 56 * KB
O_H = 88 * KB
O_F = 122 * KB
O_G = 138 * KB
O_J = 154 * KB
O_K = 170 * KB
O_L = 178 * KB
O_M = 186 * KB
F_WDT = 4 * KB
F_XQ = 48 * KB
F_TR = 80 * KB
F_STR = 126 * KB
F_HQ = 152 * KB
PR_STG = 60 * KB
PR_Q = 72 * KB
PR_TP = 78 * KB
PR_SCR = 84 * KB


def emit_prep(k):
    cx, view = k.cx, k.view
    cx.op("sync", "dma_start", out=view(O_MASK, P, [512], BF16), in_=k.mask_d,
          inc=("LX", 16))
    cx.op("sync", "dma_start", out=k.arena[0:P, O_GKV:O_GKV + 8].bitcast(F32),
          in_=k.gkv_d, inc=("LX", 16))
    ones64 = k.arena[0:1, O_ONES:O_ONES + 128].bitcast(BF16)
    ones96 = k.arena[0:96, 1184:1186].bitcast(BF16)
    cx.op("sync", "dma_start", out=k.ident, in_=k.ident_d, inc=("LX", 16))
    lx0 = cx.now("LX")
    cx.wait("vector", "LX", lx0)
    cx.op("vector", "memset", ones64, 1.0)
    cx.op("vector", "memset", ones96, 1.0)

    PR_STG2 = [24 * KB, 36 * KB]
    PR_Q2 = [48 * KB, 54 * KB]
    PR_TP2 = [60 * KB, 66 * KB]
    vq_done = [0, 0]
    tq_done = [0, 0]
    st_done = [0, 0]
    gi = 0
    for wsrc, wdst, NT, FD in [(k.wg_d, k.wgq_d, 22, C), (k.wu_d, k.wuq_d, 22, C),
                               (k.wd_d, k.wdq_d, 8, IF)]:
        nstr = FD // P
        for it in range(NT):
            sl = gi % 2
            stg = view(PR_STG2[sl], P, [FD], F32)
            qo = view(PR_Q2[sl], P, [FD], BF16)
            tp = view(PR_TP2[sl], P, [nstr * P], BF16)[:]
            cx.wait("sync", "V", vq_done[sl])
            cx.op("sync", "dma_start", out=stg[:], in_=wsrc[it * P:(it + 1) * P, :],
                  inc=("LX", 16))
            lx = cx.now("LX")
            if FD <= 1024:
                k.quant(stg[:], qo[:], FD, PR_SCR, sw=[("LX", lx)],
                        vw=[("V", tq_done[sl])])
            else:
                for ci, (c0, c1) in enumerate(((0, 1024), (1024, 2048), (2048, 2816))):
                    k.quant(stg[:, c0:c1], qo[:, c0:c1], c1 - c0, PR_SCR,
                            sw=[("LX", lx)],
                            vw=[("V", tq_done[sl])] if ci == 0 else None)
            vq_done[sl] = cx.now("V")
            cx.wait("tensor", "V", vq_done[sl])
            cx.wait("tensor", "STP", st_done[sl])
            k.transpose_blocks(qo[:], tp, nstr,
                               (k.pb[2].bitcast(BF16), k.pb[3].bitcast(BF16)))
            tq_done[sl] = cx.now("V")
            cx.wait("sync", "V", tq_done[sl])
            od = wdst.rearrange("(s f) q -> f s q", f=P)[:, :, it * P:(it + 1) * P]
            cx.op("sync", "dma_start", out=od,
                  in_=tp.rearrange("f (s p) -> f s p", s=nstr), inc=("STP", 16))
            st_done[sl] = cx.now("STP")
            gi += 1
    cx.wait("sync", "STP")
    if k.debug:
        cx.wait("gpsimd", "STP")
        cx.op("gpsimd", "dma_start", out=k.dbg["wgqd"], in_=k.wgq_d, inc=("DBG", 16))
        for e_ in ("vector", "scalar", "tensor", "sync"):
            cx.wait(e_, "DBG")


def emit_attn(k, b, upto="full"):
    cx, view, sm, pb = k.cx, k.view, k.sm, k.pb
    dbgon = k.debug and b == 0
    x0 = b * T

    wkv_sb = view(O_W, P, [8, KVR], BF16)
    wq_sb = view(O_W + 4 * KB, P, [8, QR], BF16)
    wkrope_sb = view(O_W + 12 * KB, P, [8, QR], BF16)
    wvdec_sb = view(O_W, P, [2, C], BF16)
    wqrope_sb = view(O_W + 12 * KB, P, [4, QR], BF16)
    wkdec_sb = view(O_W + 16 * KB, P, [2, C], BF16)
    wqdec_sb = view(O_W + 4 * KB, P, [4, C], BF16)
    wo_sb = view(O_A + 16 * KB, P, [8, C], BF16)

    xn_tm = view(O_A, P, [TT_N, C], BF16)
    tabs = [view(O_A + i * 8 * KB, P, [T], F32) for i in range(2)]
    ksq = view(O_A, P, [T], BF16)
    p_sb = [view(O_A + 4 * KB + i * KB, P, [512], BF16) for i in range(4)]
    mrow = view(O_A + 8 * KB, 1, [T], BF16)
    R_f32 = view(O_A + 12 * KB, 1, [512], F32)
    pvs = view(O_A + 14 * KB, 64, [512], BF16)
    xn_fm = view(O_B, P, [TT_N, 8, P], BF16)
    attn_sb = view(O_B, P, [8, T], BF16)
    rtmp_a = view(O_B, P, [T], F32)
    rtmp_b = view(O_B + 8 * KB, P, [T], F32)
    rtmp_c = view(O_B + 16 * KB, P, [T], F32)
    rtmp_d = view(O_B + 24 * KB, P, [T], F32)
    kraw = view(O_H, P, [4, T], F32)
    vbuf = view(O_H, P, [TT_N, NH * 65], BF16)
    xq_tm = view(O_H, P, [TT_N, C], BF16)
    khat = view(O_F, P, [4, T], BF16)
    qhat = view(O_G, P, [4, T], BF16)
    kqb = [view(O_J + i * 4 * KB, P, [T], BF16) for i in range(4)]
    ep_x = view(O_J, P, [C], F32)
    ep_y1 = view(O_J + 4 * KB, P, [C], F32)
    ep_xn2 = view(O_J + 8 * KB, P, [C], F32)
    xstage = [view(O_K, P, [C], F32), view(O_K + 4 * KB, P, [C], F32)]
    xsq = view(O_J + 12 * KB, P, [C], F32)
    kvn_tm = view(O_K, P, [TT_N, KVR], BF16)
    kvn_fm = view(O_L, P, [TT_N, 2, P], BF16)
    qc_sb = view(O_M, P, [4, T], BF16)
    mask_sb = view(O_MASK, P, [512], BF16)
    ones64 = k.arena[0:1, O_ONES:O_ONES + 128].bitcast(BF16)
    ones96 = k.arena[0:96, 1184:1186].bitcast(BF16)
    R_bf = k.arena[0:1, O_RB:O_RB + 1024].bitcast(BF16)
    gkv_col = k.arena[0:P, O_GKV:O_GKV + 8].bitcast(F32)

    for wd_, sb_ in ((k.wkv_d, wkv_sb), (k.wq_d, wq_sb), (k.wkrope_d, wkrope_sb)):
        cx.op("gpsimd", "dma_start", out=sb_, in_=wd_.rearrange("(a p) n -> p a n", p=P),
              inc=("LW", 16))
    lw_a = cx.now("LW")

    tb23 = (pb[2].bitcast(BF16), pb[3].bitcast(BF16))
    ln_done = [0, 0]
    for m in range(TT_N):
        sl = m % 2
        cx.wait("sync", "V", ln_done[sl])
        cx.op("sync", "dma_start", out=xstage[sl][:],
              in_=k.x_d[x0 + m * P: x0 + (m + 1) * P, :], inc=("LX", 16))
        lx = cx.now("LX")
        k.ln(xstage[sl][:], xn_tm[:, m, :], C, xsq, vw=[("LX", lx)])
        ln_done[sl] = cx.now("V")
        cx.wait("tensor", "V", ln_done[sl])
        k.transpose_blocks(xn_tm[:, m, :],
                           xn_fm[:, m, :, :].rearrange("p a c -> p (a c)"), 8, tb23)
    v_xnf = cx.now("V")
    if upto == "ln":
        return
    if dbgon:
        cx.wait("gpsimd", "V", v_xnf)
        cx.op("gpsimd", "dma_start", out=k.dbg["xn_tm"],
              in_=view(O_A, P, [TT_N * C], BF16)[:], inc=("DBG", 16))
        for e_ in ("vector", "scalar", "tensor", "sync"):
            cx.wait(e_, "DBG")

    cx.wait("tensor", "LW", lw_a)
    cx.wait("tensor", "V", v_xnf)
    b01 = [0, 0]
    for m in range(TT_N):
        sl = m % 2
        bank = pb[sl]
        cx.wait("tensor", "V", b01[sl])
        for kk in range(8):
            cx.op("tensor", "matmul", bank[:, 0:KVR], xn_fm[:, m, kk, :],
                  wkv_sb[:, kk, :], start=(kk == 0), stop=(kk == 7),
                  inc=("T", 1) if kk == 7 else None)
        tmm = cx.now("T")
        cx.wait("vector", "T", tmm)
        cx.op("vector", "tensor_copy", xsq[:, 0:KVR], bank[:, 0:KVR])
        cx.op("vector", "tensor_tensor", xsq[:, 0:KVR], xsq[:, 0:KVR],
              xsq[:, 0:KVR], ALU.mult)
        cx.op("vector", "reduce_sum", sm(0), xsq[:, 0:KVR], axis=AX.X)
        cx.op("vector", "tensor_scalar", sm(1), sm(0), 1.0 / KVR, 1e-6, ALU.mult,
              ALU.add)
        v_var = cx.now("V")
        cx.wait("scalar", "V", v_var)
        cx.op("scalar", "activation", sm(1), sm(1), AF.Ln)
        cx.op("scalar", "activation", sm(2), sm(1), AF.Exp, scale=-0.5)
        s_rstd = cx.now("S")
        cx.wait("vector", "S", s_rstd)
        cx.op("vector", "tensor_scalar", kvn_tm[:, m, :], bank[:, 0:KVR], sm(2), None,
              ALU.mult)
        b01[sl] = cx.now("V")
    v_kv = cx.now("V")
    cx.wait("tensor", "V", v_kv)
    k.transpose_blocks(view(O_K, P, [TT_N * KVR], BF16)[:],
                       view(O_L, P, [TT_N * KVR], BF16)[:], TT_N * 2, tb23)
    v_kvf = cx.now("V")
    if dbgon:
        cx.wait("gpsimd", "V", v_kv)
        cx.op("gpsimd", "dma_start", out=k.dbg["kvn_tm"],
              in_=view(O_K, P, [TT_N * KVR], BF16)[:], inc=("DBG", 16))
        for e_ in ("vector", "scalar", "tensor", "sync"):
            cx.wait(e_, "DBG")

    def proj16(wsb, nk, src_qc, dst, scale, fold4):
        for mq in range(4):
            for nt in range(4):
                i = mq * 4 + nt
                bank = pb[i % 2]
                cx.wait("tensor", "V", b01[i % 2])
                for kk in range(nk):
                    if fold4:
                        o_ap = bank[:, 0:512].rearrange("p (a c) -> p a c", a=4)
                        mov = xn_fm[:, nt * 4:(nt + 1) * 4, kk, :]
                    else:
                        o_ap = bank[:, 0:512]
                        mov = src_qc[:, kk, nt * 512:(nt + 1) * 512]
                    cx.op("tensor", "matmul", o_ap, wsb[:, kk, mq * P:(mq + 1) * P],
                          mov, start=(kk == 0), stop=(kk == nk - 1),
                          inc=("T", 1) if kk == nk - 1 else None)
                tmm = cx.now("T")
                cx.wait("vector", "T", tmm)
                if scale is None:
                    cx.op("vector", "tensor_copy", dst[:, mq, nt * 512:(nt + 1) * 512],
                          bank[:, 0:512])
                else:
                    cx.op("vector", "tensor_scalar", dst[:, mq, nt * 512:(nt + 1) * 512],
                          bank[:, 0:512], scale, None, ALU.mult)
                b01[i % 2] = cx.now("V")

    proj16(wq_sb, 8, None, qc_sb, RSC, True)
    if dbgon:
        cx.wait("gpsimd", "V")
        cx.op("gpsimd", "dma_start", out=k.dbg["qc"], in_=view(O_M, P, [4 * T], BF16)[:],
              inc=("DBG", 16))
        for e_ in ("vector", "scalar", "tensor", "sync"):
            cx.wait(e_, "DBG")
    proj16(wkrope_sb, 8, None, kraw, None, True)
    t_kraw = cx.now("T")
    v_kraw = cx.now("V")
    if upto == "proj":
        return

    cx.wait("gpsimd", "T", t_kraw)
    cx.op("gpsimd", "dma_start", out=wqrope_sb,
          in_=k.wqrope_d.rearrange("(a p) n -> p a n", p=P), inc=("LW", 16))
    cx.op("gpsimd", "dma_start", out=wqdec_sb,
          in_=k.wqdec_d.rearrange("(a p) n -> p a n", p=P), inc=("LW", 16))
    lw_b = cx.now("LW")
    for wdsrc, wdsb in ((k.wvdec_d, wvdec_sb), (k.wkdec_d, wkdec_sb)):
        cx.wait("sync", "V")
        for kk in range(2):
            stg = ep_x if kk == 0 else ep_y1
            cx.op("sync", "dma_start", out=stg[:], in_=wdsrc[kk * P:(kk + 1) * P, :],
                  inc=("LX", 16))
        lx = cx.now("LX")
        cx.wait("vector", "LX", lx)
        for kk in range(2):
            stg = ep_x if kk == 0 else ep_y1
            cx.op("vector", "tensor_scalar", wdsb[:, kk, :], stg[:],
                  gkv_col[:, kk:kk + 1], None, ALU.mult)
    v_wdec = cx.now("V")

    def rope_pass(raw, hat, tmp_a, tmp_b, raw_ready_v, tmp_free_t):
        for second in (False, True):
            cd = k.cos2_d if second else k.cos1_d
            sd = k.sin2_d if second else k.sin1_d
            cx.wait("sync", "V")
            cx.wait("sync", "T", t_kraw)
            cx.op("sync", "dma_start", out=tabs[0][:], in_=cd, inc=("LX", 16))
            cx.op("sync", "dma_start", out=tabs[1][:], in_=sd, inc=("LX", 16))
            lx = cx.now("LX")
            cx.wait("vector", "LX", lx)
            cx.wait("vector", "V", raw_ready_v)
            cx.wait("vector", "T", tmp_free_t)
            for pt in range(2):
                o = (2 if second else 0) + pt
                ob = (0 if second else 2) + pt
                cx.op("vector", "tensor_tensor", tmp_a[:], raw[:, o, :], tabs[0][:],
                      ALU.mult)
                cx.op("vector", "tensor_tensor", tmp_b[:], raw[:, ob, :], tabs[1][:],
                      ALU.mult)
                cx.op("vector", "tensor_tensor", hat[:, o, :], tmp_a[:], tmp_b[:],
                      ALU.add if second else ALU.subtract)

    rope_pass(kraw, khat, rtmp_a, rtmp_b, v_kraw, t_kraw)
    v_khat = cx.now("V")

    qraw = kraw
    cx.wait("tensor", "V", v_khat)
    cx.wait("tensor", "LW", lw_b)
    proj16(wqrope_sb, 4, qc_sb, qraw, None, False)
    v_qraw = cx.now("V")
    rope_pass(qraw, qhat, rtmp_c, rtmp_d, v_qraw, t_kraw)
    v_qhat = cx.now("V")
    if upto == "rope":
        return
    if dbgon:
        cx.wait("gpsimd", "V", v_qhat)
        cx.op("gpsimd", "dma_start", out=k.dbg["khat"], in_=view(O_F, P, [4 * T], BF16)[:],
              inc=("DBG", 16))
        cx.op("gpsimd", "dma_start", out=k.dbg["qhat"], in_=view(O_G, P, [4 * T], BF16)[:],
              inc=("DBG", 16))
        for e_ in ("vector", "scalar", "tensor", "sync"):
            cx.wait(e_, "DBG")

    cx.wait("vector", "V", v_qhat)
    for m in range(TT_N):
        cx.op("vector", "memset",
              vbuf[:, m, :].rearrange("p (h c) -> p h c", c=65)[:, :, 64:65], 1.0)
    v_ones = cx.now("V")
    cx.wait("tensor", "V", v_ones)
    for m in range(TT_N):
        for nv in range(2):
            i = m * 2 + nv
            bank = pb[i % 2]
            cx.wait("tensor", "V", b01[i % 2])
            for kk in range(2):
                cx.op("tensor", "matmul", bank[:, 0:512], kvn_fm[:, m, kk, :],
                      wvdec_sb[:, kk, nv * 512:(nv + 1) * 512], start=(kk == 0),
                      stop=(kk == 1), inc=("T", 1) if kk == 1 else None)
            tmm = cx.now("T")
            cx.wait("vector", "T", tmm)
            cx.op("vector", "tensor_copy",
                  vbuf[:, m, nv * 8 * 65:(nv * 8 + 8) * 65].rearrange(
                      "p (h c) -> p h c", c=65)[:, :, 0:64],
                  bank[:, 0:512].rearrange("p (h c) -> p h c", c=64))
            b01[i % 2] = cx.now("V")
    v_vdone = cx.now("V")
    if upto == "vdec":
        return
    if dbgon:
        cx.wait("gpsimd", "V", v_vdone)
        cx.op("gpsimd", "dma_start", out=k.dbg["vbuf"],
              in_=view(O_H, P, [TT_N * NH * 65], BF16)[:], inc=("DBG", 16))
        for e_ in ("vector", "scalar", "tensor", "sync"):
            cx.wait(e_, "DBG")

    cx.wait("gpsimd", "T", t_kraw)
    cx.op("gpsimd", "dma_start", out=wo_sb, in_=k.wo_d.rearrange("(a p) n -> p a n", p=P),
          inc=("LW", 16))
    lw_o = cx.now("LW")

    sc_banks = [pb[2], pb[3], pb[6]]
    sc_slot3 = [0, 0, 0]
    headA = {}

    def emit_headA(h):
        kbx = kqb[(h % 2) * 2]
        qbx = kqb[(h % 2) * 2 + 1]
        st = {}
        free_t = headA.get(h - 2, {}).get("scoresT", 0)
        src_pt, src_off = h // 8, (h * 16) % P
        cx.wait("sync", "T", free_t)
        cx.wait("sync", "V", v_qhat)
        for half in range(2):
            o1 = src_off
            pt_ = src_pt + (2 if half else 0)
            r0 = 64 + 16 * half
            src2 = k.arena[o1:o1 + 16, O_F + pt_ * 4 * KB:O_F + pt_ * 4 * KB + 2 * T] \
                if False else None
            in3 = k.arena[o1:o1 + 16, O_F + pt_ * 4 * KB: O_G + pt_ * 4 * KB + 4 * KB]
            cx.op("sync", "dma_start", out=kbx[r0:r0 + 16, :],
                  in_=khat[o1:o1 + 16, pt_, :], inc=("AS", 16))
            cx.op("sync", "dma_start", out=qbx[r0:r0 + 16, :],
                  in_=qhat[o1:o1 + 16, pt_, :], inc=("AS", 16))
        as_rope = cx.now("AS")
        for which, (wsb, nk, bx) in enumerate([(wkdec_sb, 2, kbx), (wqdec_sb, 4, qbx)]):
            for nt in range(4):
                bank = pb[nt % 2]
                cx.wait("tensor", "V", b01[nt % 2])
                if which == 0 and nt == 0:
                    cx.wait("tensor", "T", free_t)
                for kk in range(nk):
                    if which == 0:
                        mov = kvn_fm[:, nt * 4:(nt + 1) * 4, kk, :]
                        o_ap = bank[0:64, 0:512].rearrange("p (a c) -> p a c", a=4)
                    else:
                        mov = qc_sb[:, kk, nt * 512:(nt + 1) * 512]
                        o_ap = bank[0:64, 0:512]
                    cx.op("tensor", "matmul", o_ap, wsb[:, kk, h * 64:(h + 1) * 64],
                          mov, start=(kk == 0), stop=(kk == nk - 1),
                          inc=("T", 1) if kk == nk - 1 else None)
                tmm = cx.now("T")
                cx.wait("vector", "T", tmm)
                cx.op("vector", "tensor_copy", bx[0:64, nt * 512:(nt + 1) * 512],
                      bank[0:64, 0:512])
                b01[nt % 2] = cx.now("V")
        cx.op("vector", "memset", kbx[96:97, :], 1.0)
        v_dec = cx.now("V")
        cx.wait("vector", "AS", as_rope)

        def norm_of(bx, outbase):
            cx.op("vector", "tensor_tensor", ksq[0:96, :], bx[0:96, :], bx[0:96, :],
                  ALU.mult)
            for ch in range(4):
                cx.wait("tensor", "V")
                cx.op("tensor", "matmul", pb[7][0:1, 0:512], ones96,
                      ksq[0:96, ch * 512:(ch + 1) * 512], start=True, stop=True,
                      inc=("T", 1))
                tmm = cx.now("T")
                cx.wait("vector", "T", tmm)
                cx.op("vector", "reduce_max", sm(outbase + ch)[0:1, :],
                      pb[7][0:1, 0:512], axis=AX.X)

        norm_of(kbx, 4)
        norm_of(qbx, 8)
        for (a_, b_) in ((4, 5), (6, 7), (4, 6), (8, 9), (10, 11), (8, 10)):
            cx.op("vector", "tensor_tensor", sm(a_)[0:1, :], sm(a_)[0:1, :],
                  sm(b_)[0:1, :], ALU.max)
        cx.op("vector", "tensor_scalar", sm(4)[0:1, :], sm(4)[0:1, :], 0.5 / AMGM_A,
              None, ALU.mult)
        cx.op("vector", "tensor_scalar", sm(8)[0:1, :], sm(8)[0:1, :], 0.5 * AMGM_A,
              None, ALU.mult)
        cx.op("vector", "tensor_tensor", sm(4)[0:1, :], sm(4)[0:1, :], sm(8)[0:1, :],
              ALU.add)
        cx.op("vector", "tensor_scalar", sm(4)[0:1, :], sm(4)[0:1, :], 1.0, -1.0,
              ALU.add, ALU.mult)
        cx.op("vector", "tensor_copy", mrow[0:1, :], sm(4)[0:1, :].broadcast_to([1, T]))
        v_mrow = cx.now("V")
        cx.wait("sync", "V", v_mrow)
        cx.op("sync", "dma_start", out=qbx[96:97, :], in_=mrow[0:1, :], inc=("AS", 16))
        st["asm"] = cx.now("AS")
        st["vdec"] = v_dec
        return st

    def emit_headB(h, st):
        kbx = kqb[(h % 2) * 2]
        qbx = kqb[(h % 2) * 2 + 1]
        cx.wait("tensor", "AS", st["asm"])
        cx.wait("tensor", "V", st["vdec"])
        pv_consumed = 0
        for i in range(4):
            cx.wait("tensor", "V", pv_consumed)
            jmax = 4 * i + 4
            pend = []

            def emit_pv(ent):
                j, c0, rsem, rval = ent
                cx.wait("tensor", rsem, rval)
                cx.op("tensor", "matmul", pb[4][0:65, c0:512],
                      vbuf[:, j, h * 65:(h + 1) * 65], p_sb[j % 4][:, c0:512],
                      start=(j == 0), stop=(j == jmax - 1), inc=("T", 1))

            for j in range(jmax):
                d = j - 4 * i
                c0 = 128 * d if d >= 0 else 0
                sl3 = j % 3
                sbank = sc_banks[sl3]
                cx.wait("tensor", "S", sc_slot3[sl3])
                cx.op("tensor", "matmul", sbank[:, c0:512], kbx[0:97, j * P:(j + 1) * P],
                      qbx[0:97, i * 512 + c0:(i + 1) * 512], start=True, stop=True,
                      inc=("T", 1))
                t_sc = cx.now("T")
                cx.wait("scalar", "T", t_sc)
                cx.op("scalar", "activation", p_sb[j % 4][:, c0:512], sbank[:, c0:512],
                      AF.Exp)
                sc_slot3[sl3] = cx.now("S")
                if d >= 0:
                    cx.wait("vector", "S", sc_slot3[sl3])
                    cx.op("vector", "tensor_tensor", p_sb[j % 4][:, c0:512],
                          p_sb[j % 4][:, c0:512], mask_sb[:, 0:512 - c0], ALU.mult)
                    pend.append((j, c0, "V", cx.now("V")))
                else:
                    pend.append((j, c0, "S", sc_slot3[sl3]))
                if len(pend) > 2:
                    emit_pv(pend.pop(0))
            for ent in pend:
                emit_pv(ent)
            t_pv = cx.now("T")
            cx.wait("vector", "T", t_pv)
            cx.op("vector", "reciprocal", R_f32[0:1, :], pb[4][64:65, 0:512])
            cx.op("vector", "tensor_copy", R_bf[0:1, 0:512], R_f32[0:1, :])
            v_r = cx.now("V")
            cx.op("vector", "tensor_copy", pvs[:, :], pb[4][0:64, 0:512])
            cx.wait("tensor", "V", v_r)
            cx.op("tensor", "matmul", pb[5][0:64, 0:512], ones64[0:1, 0:64],
                  R_bf[0:1, 0:512], start=True, stop=True, inc=("T", 1))
            t_rb = cx.now("T")
            cx.wait("vector", "T", t_rb)
            cx.op("vector", "tensor_tensor",
                  attn_sb[(h % 2) * 64:(h % 2) * 64 + 64, h // 2, i * 512:(i + 1) * 512],
                  pvs[:, :], pb[5][0:64, 0:512], ALU.mult)
            pv_consumed = cx.now("V")
        return cx.now("T")

    headA[0] = emit_headA(0)
    for h in range(NH):
        if h + 1 < NH:
            headA[h + 1] = emit_headA(h + 1)
        headA[h]["scoresT"] = emit_headB(h, headA[h])
        if dbgon and h == 0:
            cx.wait("gpsimd", "T", headA[0]["scoresT"])
            cx.op("gpsimd", "dma_start", out=k.dbg["kb0"], in_=kqb[0][0:97, :],
                  inc=("DBG", 16))
            cx.op("gpsimd", "dma_start", out=k.dbg["qb0"], in_=kqb[1][0:97, :],
                  inc=("DBG", 16))
            for e_ in ("vector", "scalar", "tensor", "sync"):
                cx.wait(e_, "DBG")
    v_heads = cx.now("V")
    if upto == "heads":
        return
    if dbgon:
        cx.wait("gpsimd", "V", v_heads)
        cx.op("gpsimd", "dma_start", out=k.dbg["attn"], in_=view(O_B, P, [8 * T], BF16)[:],
              inc=("DBG", 16))
        for e_ in ("vector", "scalar", "tensor", "sync"):
            cx.wait(e_, "DBG")

    cx.wait("tensor", "LW", lw_o)
    t_h14 = headA[NH - 2]["scoresT"]
    t_h15 = headA[NH - 1]["scoresT"]
    cx.wait("sync", "T", t_h14)
    cx.wait("vector", "T", t_h15)
    ep_free = 0
    sy_done = 0
    for m in range(TT_N):
        cx.wait("sync", "V", ep_free)
        cx.op("sync", "dma_start", out=ep_x[:], in_=k.x_d[x0 + m * P:x0 + (m + 1) * P, :],
              inc=("LX", 16))
        lx_m = cx.now("LX")
        for nf in range(2):
            bank = pb[nf]
            cx.wait("tensor", "V", b01[nf])
            for kk in range(8):
                cx.op("tensor", "matmul", bank[:, 0:512], attn_sb[:, kk, m * P:(m + 1) * P],
                      wo_sb[:, kk, nf * 512:(nf + 1) * 512], start=(kk == 0),
                      stop=(kk == 7), inc=("T", 1) if kk == 7 else None)
        t_wo = cx.now("T")
        cx.wait("vector", "T", t_wo)
        cx.wait("vector", "LX", lx_m)
        cx.wait("vector", "SY", sy_done)
        for nf in range(2):
            cx.op("vector", "tensor_tensor", ep_y1[:, nf * 512:(nf + 1) * 512],
                  pb[nf][:, 0:512], ep_x[:, nf * 512:(nf + 1) * 512], ALU.add)
        v_y1 = cx.now("V")
        b01[0] = b01[1] = v_y1
        cx.wait("sync", "V", v_y1)
        cx.op("sync", "dma_start", out=k.y1_d[m * P:(m + 1) * P, :], in_=ep_y1[:],
              inc=("SY", 16))
        sy_done = cx.now("SY")
        k.ln(ep_y1[:], ep_xn2[:], C, xsq)
        vln2 = cx.now("V")
        k.quant(ep_xn2[:], xq_tm[:, m, :], C, O_F, sw=[("V", vln2)])
        ep_free = cx.now("V")
    if dbgon:
        cx.wait("gpsimd", "V")
        cx.op("gpsimd", "dma_start", out=k.dbg["xq"], in_=view(O_H, P, [TT_N * C], BF16)[:],
              inc=("DBG", 16))
        for e_ in ("vector", "scalar", "tensor", "sync"):
            cx.wait(e_, "DBG")


def emit_ffn(k, b):
    cx, view, pb = k.cx, k.view, k.pb
    dbgon = k.debug and b == 0
    xq_tm = view(O_H, P, [TT_N, C], BF16)
    wdT = view(F_WDT, P, [22, KB], BF16)
    xq_fm = view(F_XQ, P, [TT_N, 8, P], BF16)
    wgn = [view(F_STR + i * 8 * KB, P, [8, 512], BF16) for i in range(2)]
    wun = [view(F_STR + 16 * KB + i * 8 * KB, P, [8, 512], BF16) for i in range(2)]
    hq = [view(F_HQ + m * 5632, P, [IF], BF16) for m in range(8)]
    sG = [view(F_TR + i * 2 * KB, P, [512], F32) for i in range(2)]
    hbuf = [view(F_TR + 4 * KB + i * 2 * KB, P, [512], F32) for i in range(2)]
    hqf = [view(F_TR + 8 * KB + i * 5632, P, [22, P], BF16) for i in range(2)]
    Q_SCR = F_TR + 20 * KB
    y1t = [view(F_TR + 30 * KB + i * 4 * KB, P, [C], F32) for i in range(2)]
    outt = [view(F_TR + 38 * KB + i * 4 * KB, P, [C], F32) for i in range(2)]

    tb23 = (pb[2].bitcast(BF16), pb[3].bitcast(BF16))
    k.transpose_blocks(view(O_H, P, [TT_N * C], BF16)[:],
                       view(F_XQ, P, [TT_N * C], BF16)[:], TT_N * 8, tb23)
    tp_xq = cx.now("V")
    cx.op("gpsimd", "dma_start", out=wdT[:], in_=k.wdq_d.rearrange("(a p) n -> p a n", p=P),
          inc=("LW", 16))
    lw_d = cx.now("LW")

    for th in range(2):
        m0 = th * 8
        silu_done = [0, 0]
        hmul_done = [0, 0]
        hbuf_guard = [0, 0]
        quant_last = {}
        t_chunk = [0, 0]
        for n in range(6):
            nb = 0
            ncols = FFN_CHUNK[n]
            noff = FFN_OFF[n]
            cx.wait("sync", "T", t_chunk[nb])
            cx.op("sync", "dma_start", out=wgn[nb][:, :, 0:ncols],
                  in_=k.wgq_d[:, noff:noff + ncols].rearrange("(a p) c -> p a c", p=P),
                  inc=("LC", 16))
            cx.op("sync", "dma_start", out=wun[nb][:, :, 0:ncols],
                  in_=k.wuq_d[:, noff:noff + ncols].rearrange("(a p) c -> p a c", p=P),
                  inc=("LC", 16))
            lc = cx.now("LC")
            cx.wait("tensor", "LC", lc)
            cx.wait("tensor", "V", tp_xq)
            for mi in range(8):
                m = m0 + mi
                gbank, ubank = pb[mi % 2], pb[2 + mi % 2]
                cx.wait("tensor", "S", silu_done[mi % 2])
                cx.wait("tensor", "V", hmul_done[mi % 2])
                for kk in range(8):
                    cx.op("tensor", "matmul", gbank[:, 0:ncols],
                          xq_fm[:, m, kk, :], wgn[nb][:, kk, 0:ncols],
                          start=(kk == 0), stop=(kk == 7),
                          inc=("T", 1) if kk == 7 else None)
                for kk in range(8):
                    cx.op("tensor", "matmul", ubank[:, 0:ncols],
                          xq_fm[:, m, kk, :], wun[nb][:, kk, 0:ncols],
                          start=(kk == 0), stop=(kk == 7),
                          inc=("T", 1) if kk == 7 else None)
                t_mm = cx.now("T")
                cx.wait("scalar", "T", t_mm)
                cx.wait("scalar", "V", hmul_done[mi % 2])
                cx.op("scalar", "activation", sG[mi % 2][:, 0:ncols], gbank[:, 0:ncols],
                      AF.Silu)
                s_silu = cx.now("S")
                silu_done[mi % 2] = s_silu
                cx.wait("vector", "S", s_silu)
                cx.wait("vector", "V", hbuf_guard[mi % 2])
                cx.op("vector", "tensor_tensor", hbuf[mi % 2][:, 0:ncols],
                      sG[mi % 2][:, 0:ncols], ubank[:, 0:ncols], ALU.mult)
                hmul_done[mi % 2] = cx.now("V")
                k.quant(hbuf[mi % 2][:, 0:ncols], hq[mi][:, noff:noff + ncols], ncols,
                        Q_SCR, sw=[("V", hmul_done[mi % 2])])
                quant_last[mi] = hbuf_guard[mi % 2] = cx.now("V")
            t_chunk[nb] = cx.now("T")
        dn_guard = 0
        st_guard = [0, 0]
        for mi in range(8):
            m = m0 + mi
            mb = mi % 2
            cx.wait("tensor", "V", quant_last[mi])
            k.transpose_blocks(hq[mi][:], hqf[mb][:, :, :].rearrange("p a c -> p (a c)"),
                               22, (pb[6].bitcast(BF16), pb[7].bitcast(BF16)))
            tp_h = cx.now("V")
            cx.wait("sync", "V", max(dn_guard, tp_xq))
            cx.op("sync", "dma_start", out=y1t[mb][:], in_=k.y1_d[m * P:(m + 1) * P, :],
                  inc=("LX", 16))
            lx = cx.now("LX")
            cx.wait("tensor", "V", tp_h)
            cx.wait("tensor", "LW", lw_d)
            cx.wait("tensor", "V", dn_guard)
            for nf in range(2):
                bank = pb[4 + nf]
                for kk in range(22):
                    cx.op("tensor", "matmul", bank[:, 0:512], hqf[mb][:, kk, :],
                          wdT[:, kk, nf * 512:(nf + 1) * 512], start=(kk == 0),
                          stop=(kk == 21), inc=("T", 1) if kk == 21 else None)
            t_dn = cx.now("T")
            cx.wait("vector", "T", t_dn)
            cx.wait("vector", "LX", lx)
            cx.wait("vector", "ST", st_guard[mb])
            for nf in range(2):
                cx.op("vector", "tensor_tensor", outt[mb][:, nf * 512:(nf + 1) * 512],
                      pb[4 + nf][:, 0:512], y1t[mb][:, nf * 512:(nf + 1) * 512], ALU.add)
            vout = cx.now("V")
            dn_guard = vout
            cx.wait("gpsimd", "V", vout)
            cx.op("gpsimd", "dma_start", out=k.out_d[b * T + m * P:b * T + (m + 1) * P, :],
                  in_=outt[mb][:], inc=("ST", 16))
            st_guard[mb] = cx.now("ST")
        if dbgon and th == 0:
            cx.wait("gpsimd", "V")
            cx.op("gpsimd", "dma_start", out=k.dbg["hq"], in_=hq[0][:], inc=("DBG", 16))
            for e_ in ("vector", "scalar", "tensor", "sync"):
                cx.wait(e_, "DBG")


def build_program(debug=False, upto="full"):
    k = K(debug=debug)
    cx = k.cx
    emit_prep(k)
    cx.barrier()
    if upto != "prep":
        for b in range(NB):
            emit_attn(k, b, upto=upto if upto.startswith("a_") is False else upto[2:])
            cx.barrier()
            if upto.startswith("a_") or upto == "attn0":
                break
            emit_ffn(k, b)
            cx.barrier()
            if upto == "ffn0":
                break
    cx.wait("gpsimd", "ST")
    cx.wait("gpsimd", "DBG")
    cx.wait("gpsimd", "STP")
    cx.wait("gpsimd", "SY")

    with k.nc.Block() as block:
        @block.sync
        def _(e):
            for f in cx.streams["sync"]:
                f(e)

        @block.scalar
        def _(e):
            for f in cx.streams["scalar"]:
                f(e)

        @block.vector
        def _(e):
            for f in cx.streams["vector"]:
                f(e)

        @block.tensor
        def _(e):
            for f in cx.streams["tensor"]:
                f(e)

        @block.gpsimd
        def _(e):
            for f in cx.streams["gpsimd"]:
                f(e)

    return k


_PROG_CACHE = {}


def _get_prog(debug=False):
    key = bool(debug)
    if key not in _PROG_CACHE:
        _PROG_CACHE[key] = build_program(debug=key)
    return _PROG_CACHE[key]


def _host_inputs(inputs):
    f32 = np.float32
    perm = np.zeros(QR, np.int64)
    for h in range(NH):
        for d_ in range(16):
            perm[h * 16 + d_] = h * 32 + d_
            perm[256 + h * 16 + d_] = h * 32 + 16 + d_
    com = {
        "wkv": np.ascontiguousarray(inputs["L0_attn_W_kv"], f32),
        "gkv": np.ascontiguousarray(np.asarray(inputs["L0_attn_g_kv"]).reshape(2, P).T, f32),
        "wkdec": np.ascontiguousarray(inputs["L0_attn_W_kdec"], f32),
        "wvdec": np.ascontiguousarray(inputs["L0_attn_W_vdec"], f32),
        "wq": np.ascontiguousarray(inputs["L0_attn_W_q"], f32),
        "wqdec": np.ascontiguousarray(inputs["L0_attn_W_qdec"], f32),
        "wkrope": np.ascontiguousarray(np.asarray(inputs["L0_attn_W_krope"])[:, perm], f32),
        "wqrope": np.ascontiguousarray(np.asarray(inputs["L0_attn_W_qrope"])[:, perm], f32),
        "wo": np.ascontiguousarray(inputs["L0_attn_W_o"], f32),
        "wg": np.ascontiguousarray(inputs["L0_ffn_w_gate"], f32),
        "wu": np.ascontiguousarray(inputs["L0_ffn_w_up"], f32),
        "wd": np.ascontiguousarray(inputs["L0_ffn_w_down"], f32),
    }
    inv_freq = 1.0 / 10000.0 ** (np.arange(0, RD, 2, dtype=f32) / RD)
    tpos = np.arange(T, dtype=f32)
    freqs = np.outer(tpos, inv_freq)
    emb = np.concatenate([freqs, freqs], -1)
    cos = np.cos(emb).astype(f32)
    sin = np.sin(emb).astype(f32)
    com["cos1"] = np.ascontiguousarray(np.tile(cos[:, :16].T, (8, 1)))
    com["sin1"] = np.ascontiguousarray(np.tile(sin[:, :16].T, (8, 1)))
    com["cos2"] = np.ascontiguousarray(np.tile(cos[:, 16:].T, (8, 1)))
    com["sin2"] = np.ascontiguousarray(np.tile(sin[:, 16:].T, (8, 1)))
    r = np.arange(P)[:, None]
    c = np.arange(512)[None, :]
    com["maskc"] = (r <= c).astype(ml_dtypes.bfloat16)
    com["ident"] = np.eye(P, dtype=ml_dtypes.bfloat16)
    return com


def kernel(**inputs):
    com = _host_inputs(inputs)
    x = np.ascontiguousarray(np.asarray(inputs["L0_attn_x"]), np.float32)
    k = _get_prog(debug=False)
    in_maps = []
    for core in range(8):
        m = dict(com)
        m["x"] = np.ascontiguousarray(x[core * 2:(core + 1) * 2].reshape(NB * T, C))
        in_maps.append(m)
    res = run_bass_kernel_spmd(k.nc, in_maps, core_ids=list(range(8)))
    out = np.empty((16, T, C), np.float32)
    for core in range(8):
        out[core * 2:(core + 1) * 2] = res.results[core]["out"].reshape(NB, T, C)
    return out


# version 06
# speedup vs baseline: 1.4417x; 1.0123x over previous
import math
import numpy as np
import ml_dtypes
import concourse.bass as bass
import concourse.mybir as mybir
from concourse.bass_utils import run_bass_kernel_spmd

F32 = mybir.dt.float32
BF16 = mybir.dt.bfloat16
AF = mybir.ActivationFunctionType
ALU = mybir.AluOpType
AX = mybir.AxisListType

P = 128
T, C, NH, HD, RD, KVR, QR, IF = 2048, 1024, 16, 64, 32, 256, 512, 2816
TT_N = T // P
NB = 2
KB = 1024
RSC = 1.0 / math.sqrt(96.0)
AMGM_A = 12.0
FFN_CHUNK = [512] * 5 + [256]
FFN_OFF = [0, 512, 1024, 1536, 2048, 2560]


def _ap_span(ap):
    try:
        name = ap.tensor.name
        esz = {F32: 4, BF16: 2, mybir.dt.uint8: 1}.get(ap.dtype, 4)
        dims = list(ap.ap)
        reach = 1
        for st_, ct_ in dims[1:]:
            reach += (ct_ - 1) * abs(st_)
        lo = ap.offset * esz
        return (name, lo, lo + reach * esz)
    except Exception:
        return None


def _overlap(a, b):
    return a and b and a[0] == b[0] and a[1] < b[2] and b[1] < a[2]


class Ctx:
    ENGS = ("sync", "scalar", "vector", "tensor", "gpsimd")
    AUTO = {"vector": "V", "scalar": "S"}
    _DMAISH = ("dma_start", "dma_start_transpose", "nop")

    def __init__(self, nc):
        self.nc = nc
        self.streams = {e: [] for e in self.ENGS}
        self.sems = {}
        self.cnt = {}
        self._guards = []
        self.recent = {e: [] for e in self.ENGS}

    def sem(self, name):
        if name not in self.sems:
            g = self.nc.semaphore(name)
            h = g.__enter__()
            self._guards.append(g)
            self.sems[name] = h
            self.cnt[name] = 0
        return self.sems[name]

    def op(self, eng, method, *a, inc=None, **k):
        auto = self.AUTO.get(eng) if method not in self._DMAISH else None
        if auto is not None:
            aps = [x for x in a if isinstance(x, bass.AP)]
            aps += [x for x in k.values() if isinstance(x, bass.AP)]
            spans = [_ap_span(x) for x in aps]
            wspans = spans[0:1]
            rspans = spans[1:]
            need = 0
            for seq_after, ws, rs in self.recent[eng]:
                hit = any(_overlap(w, sp) for w in ws for sp in spans) or \
                      any(_overlap(w, r) for w in wspans for r in rs)
                if hit:
                    need = max(need, seq_after)
            if need > 0:
                self.wait(eng, auto, need)
            if inc is None:
                inc = (auto, 1)
            elif inc[0] != auto:
                raise AssertionError(f"op on {eng} with inc {inc}")
            self.sem(auto)
            seq_after = self.cnt[auto] + inc[1]
            self.recent[eng].append((seq_after, wspans, rspans))
            if len(self.recent[eng]) > 8:
                self.recent[eng].pop(0)
        semh, n = None, 0
        if inc is not None:
            name, n = inc
            semh = self.sem(name)
            self.cnt[name] += n

        def run(e, method=method, a=a, k=k, semh=semh, n=n):
            inst = getattr(e, method)(*a, **k)
            if semh is not None:
                inst.then_inc(semh, n)

        self.streams[eng].append(run)

    def wait(self, eng, name, val=None):
        semh = self.sem(name)
        if val is None:
            val = self.cnt[name]
        if val <= 0:
            return

        def run(e, semh=semh, val=val):
            e.wait_ge(semh, val)

        self.streams[eng].append(run)

    def now(self, name):
        self.sem(name)
        return self.cnt[name]

    def barrier(self):
        engs = self.ENGS
        marks = {}
        for e in engs:
            self.op(e, "nop", inc=("bar_" + e, 1))
            marks[e] = self.now("bar_" + e)
        for e in engs:
            for o in engs:
                if o != e:
                    self.wait(e, "bar_" + o, marks[o])


class K:

    def __init__(self, debug=False):
        self.debug = debug
        nc = bass.Bass("TRN2", target_bir_lowering=False, detect_race_conditions=False)
        self.nc = nc
        cx = Ctx(nc)
        self.cx = cx

        dri = lambda n, s, dt=F32: nc.dram_tensor(n, s, dt, kind="ExternalInput").ap()
        self.x_d = dri("x", [NB * T, C])
        self.wkv_d = dri("wkv", [C, KVR])
        self.gkv_d = dri("gkv", [P, 2])
        self.wkdec_d = dri("wkdec", [KVR, C])
        self.wvdec_d = dri("wvdec", [KVR, C])
        self.wq_d = dri("wq", [C, QR])
        self.wqdec_d = dri("wqdec", [QR, C])
        self.wkrope_d = dri("wkrope", [C, QR])
        self.wqrope_d = dri("wqrope", [QR, QR])
        self.wo_d = dri("wo", [C, C])
        self.wg_d = dri("wg", [IF, C])
        self.wu_d = dri("wu", [IF, C])
        self.wd_d = dri("wd", [C, IF])
        self.cos1_d = dri("cos1", [P, T])
        self.sin1_d = dri("sin1", [P, T])
        self.cos2_d = dri("cos2", [P, T])
        self.sin2_d = dri("sin2", [P, T])
        self.mask_d = dri("maskc", [P, 512], BF16)
        self.ident_d = dri("ident", [P, P], BF16)

        self.out_d = nc.dram_tensor("out", [NB * T, C], F32, kind="ExternalOutput").ap()
        self.y1_d = nc.dram_tensor("y1s", [T, C], F32, kind="Internal").ap()
        self.wgq_d = nc.dram_tensor("wgq", [C, IF], BF16, kind="Internal").ap()
        self.wuq_d = nc.dram_tensor("wuq", [C, IF], BF16, kind="Internal").ap()
        self.wdq_d = nc.dram_tensor("wdq", [IF, C], BF16, kind="Internal").ap()

        self.dbg = {}
        if debug:
            def dbg_t(n, s, dt=F32):
                self.dbg[n] = nc.dram_tensor("d_" + n, s, dt, kind="ExternalOutput").ap()
            dbg_t("xn_tm", [P, TT_N * C], BF16)
            dbg_t("kvn_tm", [P, TT_N * KVR], BF16)
            dbg_t("kvc0", [P, KVR], F32)
            dbg_t("sm12", [P, 4], F32)
            dbg_t("qc", [P, 4 * T], BF16)
            dbg_t("khat", [P, 4 * T], BF16)
            dbg_t("qhat", [P, 4 * T], BF16)
            dbg_t("kb0", [97, T], BF16)
            dbg_t("qb0", [97, T], BF16)
            dbg_t("vbuf", [P, TT_N * NH * 65], BF16)
            dbg_t("attn", [P, 8 * T], BF16)
            dbg_t("xq", [P, TT_N * C], BF16)
            dbg_t("hq", [P, IF], BF16)
            dbg_t("wgqd", [C, IF], BF16)

        AR = 202 * KB
        g = nc.sbuf_tensor("arena", [P, AR], mybir.dt.uint8)
        self.arena = g.__enter__().ap()
        cx._guards.append(g)

        self._qst = {}
        self.pb = []
        for i in range(8):
            pg = nc.psum_tensor(f"pb{i}", [P, 512], F32)
            self.pb.append(pg.__enter__().ap())
            cx._guards.append(pg)

    def view(self, off, rows, shape, dt):
        esz = {F32: 4, BF16: 2}[dt]
        n = 1
        for s in shape:
            n *= s
        ap = self.arena[0:rows, off:off + n * esz].bitcast(dt)
        if len(shape) > 1:
            dims = " ".join(f"d{i}" for i in range(len(shape)))
            kw = {f"d{i}": shape[i] for i in range(1, len(shape))}
            ap = ap.rearrange(f"p ({dims}) -> p {dims}", **kw)
        return ap

    def sm(self, i):
        return self.arena[0:P, 1280 + 4 * i: 1280 + 4 * (i + 1)].bitcast(F32)

    @property
    def ident(self):
        return self.arena[0:P, 2560:2816].bitcast(BF16)

    def transpose_blocks(self, src_flat, dst_flat, nblk, banks, tsem="T", vsem="V"):
        cx = self.cx
        v0 = cx.now(vsem)
        for g in range(nblk):
            pbank = banks[g % 2]
            cx.wait("tensor", vsem, v0 + max(0, g - 1))
            cx.op("tensor", "transpose", pbank[:, 0:P],
                  src_flat[:, g * P:(g + 1) * P], self.ident, inc=(tsem, 1))
            cx.wait("vector", tsem)
            cx.op("vector", "tensor_copy", dst_flat[:, g * P:(g + 1) * P],
                  pbank[:, 0:P], inc=(vsem, 1))

    def quant(self, src, out, FD, scr, sw=None, vw=None):
        cx = self.cx
        NBLK = FD // 16
        v = self.view(scr, P, [FD], F32)
        t = self.view(scr + 4 * FD, P, [FD], F32)
        s1 = self.view(scr + 8 * FD, P, [FD], F32)
        s2 = self.view(scr + 12 * FD, P, [FD], F32)
        am = self.view(scr + 16 * FD, P, [NBLK], F32)
        sc = self.view(scr + 16 * FD + 4 * NBLK, P, [NBLK], F32)
        rs = self.view(scr + 16 * FD + 8 * NBLK, P, [NBLK], F32)
        t3 = t.rearrange("p (b c) -> p b c", c=16)
        v3 = v.rearrange("p (b c) -> p b c", c=16)
        rs_b = rs.unsqueeze(2).broadcast_to([P, NBLK, 16])
        sc_b = sc.unsqueeze(2).broadcast_to([P, NBLK, 16])
        s13 = s1.rearrange("p (b c) -> p b c", c=16)

        st = self._qst.setdefault(scr, {"vfree": 0})
        cx.wait("scalar", "V", st["vfree"])
        if sw:
            for sem, val in sw:
                cx.wait("scalar", sem, val)
        cx.op("scalar", "activation", v[:], src, AF.Abs)
        s_abs = cx.now("S")
        if vw:
            for sem, val in vw:
                cx.wait("vector", sem, val)
        cx.wait("vector", "S", s_abs)
        cx.op("vector", "reduce_max", am[:], v3, axis=AX.X)
        cx.op("vector", "tensor_scalar", sc[:], am[:], 1.0 / 6.0, 1e-12, ALU.mult, ALU.max)
        cx.op("vector", "reciprocal", rs[:], sc[:])
        cx.op("vector", "tensor_tensor", t3, v3, rs_b, ALU.mult)
        st["vfree"] = cx.now("V")
        cx.op("vector", "tensor_single_scalar", s1[:], t[:], 0.25, ALU.is_gt)
        cx.op("vector", "scalar_tensor_tensor", s1[:], t[:], 0.75, s1[:], ALU.is_gt, ALU.add)
        cx.op("vector", "scalar_tensor_tensor", s1[:], t[:], 1.25, s1[:], ALU.is_gt, ALU.add)
        cx.op("vector", "scalar_tensor_tensor", s1[:], t[:], 1.75, s1[:], ALU.is_gt, ALU.add)
        cx.op("vector", "tensor_single_scalar", s2[:], t[:], 2.5, ALU.is_gt)
        cx.op("vector", "scalar_tensor_tensor", s2[:], t[:], 3.5, s2[:], ALU.is_gt, ALU.add)
        cx.op("vector", "scalar_tensor_tensor", s2[:], t[:], 5.0, s2[:], ALU.is_gt, ALU.add)
        cx.op("vector", "scalar_tensor_tensor", s2[:], t[:], 5.0, s2[:], ALU.is_gt, ALU.add)
        cx.op("vector", "scalar_tensor_tensor", s1[:], s2[:], 2.0, s1[:], ALU.mult, ALU.add)
        cx.op("vector", "tensor_tensor", s13, s13, sc_b, ALU.mult)
        cx.op("vector", "tensor_scalar", s2[:], src, 0.0, -0.5, ALU.is_gt, ALU.add)
        cx.op("vector", "tensor_tensor", out, s1[:], s2[:], ALU.mult)

    def ln(self, xsrc, xout, FD, xsq, vw=None):
        cx = self.cx
        sm = self.sm
        if vw:
            for sem, val in vw:
                cx.wait("vector", sem, val)
        cx.op("vector", "tensor_tensor", xsq[:], xsrc, xsrc, ALU.mult)
        cx.op("vector", "reduce_sum", sm(0), xsrc, axis=AX.X)
        cx.op("vector", "reduce_sum", sm(1), xsq[:], axis=AX.X)
        cx.op("vector", "tensor_scalar", sm(0), sm(0), 1.0 / FD, None, ALU.mult)
        cx.op("vector", "tensor_tensor", sm(2), sm(0), sm(0), ALU.mult)
        cx.op("vector", "tensor_scalar", sm(1), sm(1), 1.0 / FD, None, ALU.mult)
        cx.op("vector", "tensor_tensor", sm(1), sm(1), sm(2), ALU.subtract)
        cx.op("vector", "tensor_scalar", sm(1), sm(1), 1e-5, None, ALU.add)
        v_var = cx.now("V")
        cx.wait("scalar", "V", v_var)
        cx.op("scalar", "activation", sm(1), sm(1), AF.Ln)
        cx.op("scalar", "activation", sm(2), sm(1), AF.Exp, scale=-0.5)
        s_r = cx.now("S")
        cx.wait("vector", "S", s_r)
        cx.op("vector", "tensor_scalar", xsq[:], xsrc, sm(0), None, ALU.subtract)
        cx.op("vector", "tensor_scalar", xout, xsq[:], sm(2), None, ALU.mult)


O_MASK = 0
O_ONES = 1024
O_GKV = 1216
O_RB = 1536
O_W = 4 * KB
O_A = 24 * KB
O_B = 56 * KB
O_H = 88 * KB
O_F = 122 * KB
O_G = 138 * KB
O_J = 154 * KB
O_K = 170 * KB
O_L = 178 * KB
O_M = 186 * KB
F_WDT = 4 * KB
F_XQ = 48 * KB
F_TR = 80 * KB
F_STR = 126 * KB
F_HQ = 152 * KB
PR_STG = 60 * KB
PR_Q = 72 * KB
PR_TP = 78 * KB
PR_SCR = 84 * KB


def emit_prep(k):
    cx, view = k.cx, k.view
    cx.op("sync", "dma_start", out=view(O_MASK, P, [512], BF16), in_=k.mask_d,
          inc=("LX", 16))
    cx.op("sync", "dma_start", out=k.arena[0:P, O_GKV:O_GKV + 8].bitcast(F32),
          in_=k.gkv_d, inc=("LX", 16))
    ones64 = k.arena[0:1, O_ONES:O_ONES + 128].bitcast(BF16)
    ones96 = k.arena[0:96, 1184:1186].bitcast(BF16)
    cx.op("sync", "dma_start", out=k.ident, in_=k.ident_d, inc=("LX", 16))
    lx0 = cx.now("LX")
    cx.wait("vector", "LX", lx0)
    cx.op("vector", "memset", ones64, 1.0)
    cx.op("vector", "memset", ones96, 1.0)

    PR_STG2 = [24 * KB, 36 * KB]
    PR_Q2 = [48 * KB, 54 * KB]
    PR_TP2 = [60 * KB, 66 * KB]
    vq_done = [0, 0]
    tq_done = [0, 0]
    st_done = [0, 0]
    gi = 0
    for wsrc, wdst, NT, FD in [(k.wg_d, k.wgq_d, 22, C), (k.wu_d, k.wuq_d, 22, C),
                               (k.wd_d, k.wdq_d, 8, IF)]:
        nstr = FD // P
        for it in range(NT):
            sl = gi % 2
            stg = view(PR_STG2[sl], P, [FD], F32)
            qo = view(PR_Q2[sl], P, [FD], BF16)
            tp = view(PR_TP2[sl], P, [nstr * P], BF16)[:]
            cx.wait("sync", "V", vq_done[sl])
            cx.op("sync", "dma_start", out=stg[:], in_=wsrc[it * P:(it + 1) * P, :],
                  inc=("LX", 16))
            lx = cx.now("LX")
            if FD <= 1024:
                k.quant(stg[:], qo[:], FD, PR_SCR, sw=[("LX", lx)],
                        vw=[("V", tq_done[sl])])
            else:
                for ci, (c0, c1) in enumerate(((0, 1024), (1024, 2048), (2048, 2816))):
                    k.quant(stg[:, c0:c1], qo[:, c0:c1], c1 - c0, PR_SCR,
                            sw=[("LX", lx)],
                            vw=[("V", tq_done[sl])] if ci == 0 else None)
            vq_done[sl] = cx.now("V")
            cx.wait("tensor", "V", vq_done[sl])
            cx.wait("tensor", "STP", st_done[sl])
            k.transpose_blocks(qo[:], tp, nstr,
                               (k.pb[2].bitcast(BF16), k.pb[3].bitcast(BF16)))
            tq_done[sl] = cx.now("V")
            cx.wait("sync", "V", tq_done[sl])
            od = wdst.rearrange("(s f) q -> f s q", f=P)[:, :, it * P:(it + 1) * P]
            cx.op("sync", "dma_start", out=od,
                  in_=tp.rearrange("f (s p) -> f s p", s=nstr), inc=("STP", 16))
            st_done[sl] = cx.now("STP")
            gi += 1
    cx.wait("sync", "STP")
    if k.debug:
        cx.wait("gpsimd", "STP")
        cx.op("gpsimd", "dma_start", out=k.dbg["wgqd"], in_=k.wgq_d, inc=("DBG", 16))
        for e_ in ("vector", "scalar", "tensor", "sync"):
            cx.wait(e_, "DBG")


def emit_attn(k, b, upto="full"):
    cx, view, sm, pb = k.cx, k.view, k.sm, k.pb
    dbgon = k.debug and b == 0
    x0 = b * T

    wkv_sb = view(O_W, P, [8, KVR], BF16)
    wq_sb = view(O_W + 4 * KB, P, [8, QR], BF16)
    wkrope_sb = view(O_W + 12 * KB, P, [8, QR], BF16)
    wvdec_sb = view(O_W, P, [2, C], BF16)
    wqrope_sb = view(O_W + 12 * KB, P, [4, QR], BF16)
    wkdec_sb = view(O_W + 16 * KB, P, [2, C], BF16)
    wqdec_sb = view(O_W + 4 * KB, P, [4, C], BF16)
    wo_sb = view(O_A + 16 * KB, P, [8, C], BF16)

    xn_tm = view(O_A, P, [TT_N, C], BF16)
    tabs = [view(O_A + i * 8 * KB, P, [T], F32) for i in range(2)]
    ksq = view(O_A, P, [T], BF16)
    p_sb = [view(O_A + 4 * KB + i * KB, P, [512], BF16) for i in range(4)]
    mrow = view(O_A + 8 * KB, 1, [T], BF16)
    R_f32 = view(O_A + 12 * KB, 1, [512], F32)
    pvs = view(O_A + 14 * KB, 64, [512], BF16)
    xn_fm = view(O_B, P, [TT_N, 8, P], BF16)
    attn_sb = view(O_B, P, [8, T], BF16)
    rtmp_a = view(O_B, P, [T], F32)
    rtmp_b = view(O_B + 8 * KB, P, [T], F32)
    rtmp_c = view(O_B + 16 * KB, P, [T], F32)
    rtmp_d = view(O_B + 24 * KB, P, [T], F32)
    kraw = view(O_H, P, [4, T], F32)
    vbuf = view(O_H, P, [TT_N, NH * 65], BF16)
    xq_tm = view(O_H, P, [TT_N, C], BF16)
    khat = view(O_F, P, [4, T], BF16)
    qhat = view(O_G, P, [4, T], BF16)
    kqb = [view(O_J + i * 4 * KB, P, [T], BF16) for i in range(4)]
    ep_x = view(O_J, P, [C], F32)
    ep_y1 = view(O_J + 4 * KB, P, [C], F32)
    ep_xn2 = view(O_J + 8 * KB, P, [C], F32)
    xstage = [view(O_K, P, [C], F32), view(O_K + 4 * KB, P, [C], F32)]
    xsq = view(O_J + 12 * KB, P, [C], F32)
    kvn_tm = view(O_K, P, [TT_N, KVR], BF16)
    kvn_fm = view(O_L, P, [TT_N, 2, P], BF16)
    qc_sb = view(O_M, P, [4, T], BF16)
    mask_sb = view(O_MASK, P, [512], BF16)
    ones64 = k.arena[0:1, O_ONES:O_ONES + 128].bitcast(BF16)
    ones96 = k.arena[0:96, 1184:1186].bitcast(BF16)
    R_bf = k.arena[0:1, O_RB:O_RB + 1024].bitcast(BF16)
    gkv_col = k.arena[0:P, O_GKV:O_GKV + 8].bitcast(F32)

    for wd_, sb_ in ((k.wkv_d, wkv_sb), (k.wq_d, wq_sb), (k.wkrope_d, wkrope_sb)):
        cx.op("gpsimd", "dma_start", out=sb_, in_=wd_.rearrange("(a p) n -> p a n", p=P),
              inc=("LW", 16))
    lw_a = cx.now("LW")

    tb23 = (pb[2].bitcast(BF16), pb[3].bitcast(BF16))
    ln_done = [0, 0]
    for m in range(TT_N):
        sl = m % 2
        cx.wait("sync", "V", ln_done[sl])
        cx.op("sync", "dma_start", out=xstage[sl][:],
              in_=k.x_d[x0 + m * P: x0 + (m + 1) * P, :], inc=("LX", 16))
        lx = cx.now("LX")
        k.ln(xstage[sl][:], xn_tm[:, m, :], C, xsq, vw=[("LX", lx)])
        ln_done[sl] = cx.now("V")
        cx.wait("tensor", "V", ln_done[sl])
        k.transpose_blocks(xn_tm[:, m, :],
                           xn_fm[:, m, :, :].rearrange("p a c -> p (a c)"), 8, tb23)
    v_xnf = cx.now("V")
    if upto == "ln":
        return
    if dbgon:
        cx.wait("gpsimd", "V", v_xnf)
        cx.op("gpsimd", "dma_start", out=k.dbg["xn_tm"],
              in_=view(O_A, P, [TT_N * C], BF16)[:], inc=("DBG", 16))
        for e_ in ("vector", "scalar", "tensor", "sync"):
            cx.wait(e_, "DBG")

    cx.wait("tensor", "LW", lw_a)
    cx.wait("tensor", "V", v_xnf)
    b01 = [0, 0]
    for m in range(TT_N):
        sl = m % 2
        bank = pb[sl]
        cx.wait("tensor", "V", b01[sl])
        for kk in range(8):
            cx.op("tensor", "matmul", bank[:, 0:KVR], xn_fm[:, m, kk, :],
                  wkv_sb[:, kk, :], start=(kk == 0), stop=(kk == 7),
                  inc=("T", 1) if kk == 7 else None)
        tmm = cx.now("T")
        cx.wait("vector", "T", tmm)
        cx.op("vector", "tensor_copy", xsq[:, 0:KVR], bank[:, 0:KVR])
        cx.op("vector", "tensor_tensor", xsq[:, 0:KVR], xsq[:, 0:KVR],
              xsq[:, 0:KVR], ALU.mult)
        cx.op("vector", "reduce_sum", sm(0), xsq[:, 0:KVR], axis=AX.X)
        cx.op("vector", "tensor_scalar", sm(1), sm(0), 1.0 / KVR, 1e-6, ALU.mult,
              ALU.add)
        v_var = cx.now("V")
        cx.wait("scalar", "V", v_var)
        cx.op("scalar", "activation", sm(1), sm(1), AF.Ln)
        cx.op("scalar", "activation", sm(2), sm(1), AF.Exp, scale=-0.5)
        s_rstd = cx.now("S")
        cx.wait("vector", "S", s_rstd)
        cx.op("vector", "tensor_scalar", kvn_tm[:, m, :], bank[:, 0:KVR], sm(2), None,
              ALU.mult)
        b01[sl] = cx.now("V")
    v_kv = cx.now("V")
    cx.wait("tensor", "V", v_kv)
    k.transpose_blocks(view(O_K, P, [TT_N * KVR], BF16)[:],
                       view(O_L, P, [TT_N * KVR], BF16)[:], TT_N * 2, tb23)
    v_kvf = cx.now("V")
    if dbgon:
        cx.wait("gpsimd", "V", v_kv)
        cx.op("gpsimd", "dma_start", out=k.dbg["kvn_tm"],
              in_=view(O_K, P, [TT_N * KVR], BF16)[:], inc=("DBG", 16))
        for e_ in ("vector", "scalar", "tensor", "sync"):
            cx.wait(e_, "DBG")

    def proj16(wsb, nk, src_qc, dst, scale, fold4):
        for mq in range(4):
            for nt in range(4):
                i = mq * 4 + nt
                bank = pb[i % 2]
                cx.wait("tensor", "V", b01[i % 2])
                for kk in range(nk):
                    if fold4:
                        o_ap = bank[:, 0:512].rearrange("p (a c) -> p a c", a=4)
                        mov = xn_fm[:, nt * 4:(nt + 1) * 4, kk, :]
                    else:
                        o_ap = bank[:, 0:512]
                        mov = src_qc[:, kk, nt * 512:(nt + 1) * 512]
                    cx.op("tensor", "matmul", o_ap, wsb[:, kk, mq * P:(mq + 1) * P],
                          mov, start=(kk == 0), stop=(kk == nk - 1),
                          inc=("T", 1) if kk == nk - 1 else None)
                tmm = cx.now("T")
                cx.wait("vector", "T", tmm)
                if scale is None:
                    cx.op("vector", "tensor_copy", dst[:, mq, nt * 512:(nt + 1) * 512],
                          bank[:, 0:512])
                else:
                    cx.op("vector", "tensor_scalar", dst[:, mq, nt * 512:(nt + 1) * 512],
                          bank[:, 0:512], scale, None, ALU.mult)
                b01[i % 2] = cx.now("V")

    proj16(wq_sb, 8, None, qc_sb, RSC, True)
    if dbgon:
        cx.wait("gpsimd", "V")
        cx.op("gpsimd", "dma_start", out=k.dbg["qc"], in_=view(O_M, P, [4 * T], BF16)[:],
              inc=("DBG", 16))
        for e_ in ("vector", "scalar", "tensor", "sync"):
            cx.wait(e_, "DBG")
    proj16(wkrope_sb, 8, None, kraw, None, True)
    t_kraw = cx.now("T")
    v_kraw = cx.now("V")
    if upto == "proj":
        return

    cx.wait("gpsimd", "T", t_kraw)
    cx.op("gpsimd", "dma_start", out=wqrope_sb,
          in_=k.wqrope_d.rearrange("(a p) n -> p a n", p=P), inc=("LW", 16))
    cx.op("gpsimd", "dma_start", out=wqdec_sb,
          in_=k.wqdec_d.rearrange("(a p) n -> p a n", p=P), inc=("LW", 16))
    lw_b = cx.now("LW")
    for wdsrc, wdsb in ((k.wvdec_d, wvdec_sb), (k.wkdec_d, wkdec_sb)):
        cx.wait("sync", "V")
        for kk in range(2):
            stg = ep_x if kk == 0 else ep_y1
            cx.op("sync", "dma_start", out=stg[:], in_=wdsrc[kk * P:(kk + 1) * P, :],
                  inc=("LX", 16))
        lx = cx.now("LX")
        cx.wait("vector", "LX", lx)
        for kk in range(2):
            stg = ep_x if kk == 0 else ep_y1
            cx.op("vector", "tensor_scalar", wdsb[:, kk, :], stg[:],
                  gkv_col[:, kk:kk + 1], None, ALU.mult)
    v_wdec = cx.now("V")

    def rope_pass(raw, hat, tmp_a, tmp_b, raw_ready_v, tmp_free_t):
        for second in (False, True):
            cd = k.cos2_d if second else k.cos1_d
            sd = k.sin2_d if second else k.sin1_d
            cx.wait("sync", "V")
            cx.wait("sync", "T", t_kraw)
            cx.op("sync", "dma_start", out=tabs[0][:], in_=cd, inc=("LX", 16))
            cx.op("sync", "dma_start", out=tabs[1][:], in_=sd, inc=("LX", 16))
            lx = cx.now("LX")
            cx.wait("vector", "LX", lx)
            cx.wait("vector", "V", raw_ready_v)
            cx.wait("vector", "T", tmp_free_t)
            for pt in range(2):
                o = (2 if second else 0) + pt
                ob = (0 if second else 2) + pt
                cx.op("vector", "tensor_tensor", tmp_a[:], raw[:, o, :], tabs[0][:],
                      ALU.mult)
                cx.op("vector", "tensor_tensor", tmp_b[:], raw[:, ob, :], tabs[1][:],
                      ALU.mult)
                cx.op("vector", "tensor_tensor", hat[:, o, :], tmp_a[:], tmp_b[:],
                      ALU.add if second else ALU.subtract)

    rope_pass(kraw, khat, rtmp_a, rtmp_b, v_kraw, t_kraw)
    v_khat = cx.now("V")

    qraw = kraw
    cx.wait("tensor", "V", v_khat)
    cx.wait("tensor", "LW", lw_b)
    proj16(wqrope_sb, 4, qc_sb, qraw, None, False)
    v_qraw = cx.now("V")
    rope_pass(qraw, qhat, rtmp_c, rtmp_d, v_qraw, t_kraw)
    v_qhat = cx.now("V")
    if upto == "rope":
        return
    if dbgon:
        cx.wait("gpsimd", "V", v_qhat)
        cx.op("gpsimd", "dma_start", out=k.dbg["khat"], in_=view(O_F, P, [4 * T], BF16)[:],
              inc=("DBG", 16))
        cx.op("gpsimd", "dma_start", out=k.dbg["qhat"], in_=view(O_G, P, [4 * T], BF16)[:],
              inc=("DBG", 16))
        for e_ in ("vector", "scalar", "tensor", "sync"):
            cx.wait(e_, "DBG")

    cx.wait("vector", "V", v_qhat)
    for m in range(TT_N):
        cx.op("vector", "memset",
              vbuf[:, m, :].rearrange("p (h c) -> p h c", c=65)[:, :, 64:65], 1.0)
    v_ones = cx.now("V")
    cx.wait("tensor", "V", v_ones)
    for m in range(TT_N):
        for nv in range(2):
            i = m * 2 + nv
            bank = pb[i % 2]
            cx.wait("tensor", "V", b01[i % 2])
            for kk in range(2):
                cx.op("tensor", "matmul", bank[:, 0:512], kvn_fm[:, m, kk, :],
                      wvdec_sb[:, kk, nv * 512:(nv + 1) * 512], start=(kk == 0),
                      stop=(kk == 1), inc=("T", 1) if kk == 1 else None)
            tmm = cx.now("T")
            cx.wait("vector", "T", tmm)
            cx.op("vector", "tensor_copy",
                  vbuf[:, m, nv * 8 * 65:(nv * 8 + 8) * 65].rearrange(
                      "p (h c) -> p h c", c=65)[:, :, 0:64],
                  bank[:, 0:512].rearrange("p (h c) -> p h c", c=64))
            b01[i % 2] = cx.now("V")
    v_vdone = cx.now("V")
    if upto == "vdec":
        return
    if dbgon:
        cx.wait("gpsimd", "V", v_vdone)
        cx.op("gpsimd", "dma_start", out=k.dbg["vbuf"],
              in_=view(O_H, P, [TT_N * NH * 65], BF16)[:], inc=("DBG", 16))
        for e_ in ("vector", "scalar", "tensor", "sync"):
            cx.wait(e_, "DBG")

    cx.wait("gpsimd", "T", t_kraw)
    cx.op("gpsimd", "dma_start", out=wo_sb, in_=k.wo_d.rearrange("(a p) n -> p a n", p=P),
          inc=("LW", 16))
    lw_o = cx.now("LW")

    sc_banks = [pb[2], pb[3], pb[6]]
    sc_slot3 = [0, 0, 0]
    headA = {}

    def emit_headA(h):
        kbx = kqb[(h % 2) * 2]
        qbx = kqb[(h % 2) * 2 + 1]
        st = {}
        free_t = headA.get(h - 2, {}).get("scoresT", 0)
        src_pt, src_off = h // 8, (h * 16) % P
        cx.wait("sync", "T", free_t)
        cx.wait("sync", "V", v_qhat)
        for half in range(2):
            o1 = src_off
            pt_ = src_pt + (2 if half else 0)
            r0 = 64 + 16 * half
            src2 = k.arena[o1:o1 + 16, O_F + pt_ * 4 * KB:O_F + pt_ * 4 * KB + 2 * T] \
                if False else None
            in3 = k.arena[o1:o1 + 16, O_F + pt_ * 4 * KB: O_G + pt_ * 4 * KB + 4 * KB]
            cx.op("sync", "dma_start", out=kbx[r0:r0 + 16, :],
                  in_=khat[o1:o1 + 16, pt_, :], inc=("AS", 16))
            cx.op("sync", "dma_start", out=qbx[r0:r0 + 16, :],
                  in_=qhat[o1:o1 + 16, pt_, :], inc=("AS", 16))
        as_rope = cx.now("AS")
        for which, (wsb, nk, bx) in enumerate([(wkdec_sb, 2, kbx), (wqdec_sb, 4, qbx)]):
            for nt in range(4):
                bank = pb[nt % 2]
                cx.wait("tensor", "V", b01[nt % 2])
                if which == 0 and nt == 0:
                    cx.wait("tensor", "T", free_t)
                for kk in range(nk):
                    if which == 0:
                        mov = kvn_fm[:, nt * 4:(nt + 1) * 4, kk, :]
                        o_ap = bank[0:64, 0:512].rearrange("p (a c) -> p a c", a=4)
                    else:
                        mov = qc_sb[:, kk, nt * 512:(nt + 1) * 512]
                        o_ap = bank[0:64, 0:512]
                    cx.op("tensor", "matmul", o_ap, wsb[:, kk, h * 64:(h + 1) * 64],
                          mov, start=(kk == 0), stop=(kk == nk - 1),
                          inc=("T", 1) if kk == nk - 1 else None)
                tmm = cx.now("T")
                cx.wait("vector", "T", tmm)
                cx.op("vector", "tensor_copy", bx[0:64, nt * 512:(nt + 1) * 512],
                      bank[0:64, 0:512])
                b01[nt % 2] = cx.now("V")
        cx.op("vector", "memset", kbx[96:97, :], 1.0)
        v_dec = cx.now("V")
        cx.wait("vector", "AS", as_rope)

        def norm_of(bx, outbase):
            cx.op("vector", "tensor_tensor", ksq[0:96, :], bx[0:96, :], bx[0:96, :],
                  ALU.mult)
            vsq = cx.now("V")
            cx.wait("tensor", "V", vsq)
            for ch in range(4):
                cx.op("tensor", "matmul", pb[7][32 * ch:32 * ch + 1, 0:512], ones96,
                      ksq[0:96, ch * 512:(ch + 1) * 512], start=True, stop=True,
                      tile_position=(0, 32 * ch), inc=("T", 1))
            tmm = cx.now("T")
            cx.wait("vector", "T", tmm)
            for ch in range(4):
                cx.op("vector", "reduce_max", sm(outbase + ch)[0:1, :],
                      pb[7][32 * ch:32 * ch + 1, 0:512], axis=AX.X)

        norm_of(kbx, 4)
        norm_of(qbx, 8)
        for (a_, b_) in ((4, 5), (6, 7), (4, 6), (8, 9), (10, 11), (8, 10)):
            cx.op("vector", "tensor_tensor", sm(a_)[0:1, :], sm(a_)[0:1, :],
                  sm(b_)[0:1, :], ALU.max)
        cx.op("vector", "tensor_scalar", sm(4)[0:1, :], sm(4)[0:1, :], 0.5 / AMGM_A,
              None, ALU.mult)
        cx.op("vector", "tensor_scalar", sm(8)[0:1, :], sm(8)[0:1, :], 0.5 * AMGM_A,
              None, ALU.mult)
        cx.op("vector", "tensor_tensor", sm(4)[0:1, :], sm(4)[0:1, :], sm(8)[0:1, :],
              ALU.add)
        cx.op("vector", "tensor_scalar", sm(4)[0:1, :], sm(4)[0:1, :], 1.0, -1.0,
              ALU.add, ALU.mult)
        cx.op("vector", "tensor_copy", mrow[0:1, :], sm(4)[0:1, :].broadcast_to([1, T]))
        v_mrow = cx.now("V")
        cx.wait("sync", "V", v_mrow)
        cx.op("sync", "dma_start", out=qbx[96:97, :], in_=mrow[0:1, :], inc=("AS", 16))
        st["asm"] = cx.now("AS")
        st["vdec"] = v_dec
        return st

    def emit_headB(h, st):
        kbx = kqb[(h % 2) * 2]
        qbx = kqb[(h % 2) * 2 + 1]
        cx.wait("tensor", "AS", st["asm"])
        cx.wait("tensor", "V", st["vdec"])

        def emit_epi(i):
            pvb = pb[4 + i % 2]
            cx.wait("vector", "T", pv_done[i])
            cx.op("vector", "reciprocal", R_f32[0:1, :], pvb[64:65, 0:512])
            cx.op("vector", "tensor_copy", R_bf[0:1, 0:512], R_f32[0:1, :])
            v_r = cx.now("V")
            cx.op("vector", "tensor_copy", pvs[:, :], pvb[0:64, 0:512])
            v_pvs = cx.now("V")
            cx.wait("tensor", "V", v_pvs)
            cx.op("tensor", "matmul", pvb[0:64, 0:512], ones64[0:1, 0:64],
                  R_bf[0:1, 0:512], start=True, stop=True, inc=("T", 1))
            t_rb = cx.now("T")
            cx.wait("vector", "T", t_rb)
            cx.op("vector", "tensor_tensor",
                  attn_sb[(h % 2) * 64:(h % 2) * 64 + 64, h // 2, i * 512:(i + 1) * 512],
                  pvs[:, :], pvb[0:64, 0:512], ALU.mult)
            epi_done[i] = cx.now("V")

        pv_done = {}
        epi_done = {-2: 0, -1: 0}
        for i in range(4):
            cx.wait("tensor", "V", epi_done[i - 2])
            jmax = 4 * i + 4
            pend = []

            def emit_pv(ent, i=i, jmax=jmax):
                j, c0, rsem, rval = ent
                cx.wait("tensor", rsem, rval)
                cx.op("tensor", "matmul", pb[4 + i % 2][0:65, c0:512],
                      vbuf[:, j, h * 65:(h + 1) * 65], p_sb[j % 4][:, c0:512],
                      start=(j == 0), stop=(j == jmax - 1), inc=("T", 1))

            for j in range(jmax):
                d = j - 4 * i
                c0 = 128 * d if d >= 0 else 0
                sl3 = j % 3
                sbank = sc_banks[sl3]
                cx.wait("tensor", "S", sc_slot3[sl3])
                cx.op("tensor", "matmul", sbank[:, c0:512], kbx[0:97, j * P:(j + 1) * P],
                      qbx[0:97, i * 512 + c0:(i + 1) * 512], start=True, stop=True,
                      inc=("T", 1))
                t_sc = cx.now("T")
                cx.wait("scalar", "T", t_sc)
                cx.op("scalar", "activation", p_sb[j % 4][:, c0:512], sbank[:, c0:512],
                      AF.Exp)
                sc_slot3[sl3] = cx.now("S")
                if d >= 0:
                    cx.wait("vector", "S", sc_slot3[sl3])
                    cx.op("vector", "tensor_tensor", p_sb[j % 4][:, c0:512],
                          p_sb[j % 4][:, c0:512], mask_sb[:, 0:512 - c0], ALU.mult)
                    pend.append((j, c0, "V", cx.now("V")))
                else:
                    pend.append((j, c0, "S", sc_slot3[sl3]))
                if len(pend) > 2:
                    emit_pv(pend.pop(0))
            for ent in pend:
                emit_pv(ent)
            pv_done[i] = cx.now("T")
            if i >= 1:
                emit_epi(i - 1)
        emit_epi(3)
        cx.wait("tensor", "V", epi_done[3])
        return cx.now("T")

    headA[0] = emit_headA(0)
    for h in range(NH):
        if h + 1 < NH:
            headA[h + 1] = emit_headA(h + 1)
        headA[h]["scoresT"] = emit_headB(h, headA[h])
        if dbgon and h == 0:
            cx.wait("gpsimd", "T", headA[0]["scoresT"])
            cx.op("gpsimd", "dma_start", out=k.dbg["kb0"], in_=kqb[0][0:97, :],
                  inc=("DBG", 16))
            cx.op("gpsimd", "dma_start", out=k.dbg["qb0"], in_=kqb[1][0:97, :],
                  inc=("DBG", 16))
            for e_ in ("vector", "scalar", "tensor", "sync"):
                cx.wait(e_, "DBG")
    v_heads = cx.now("V")
    if upto == "heads":
        return
    if dbgon:
        cx.wait("gpsimd", "V", v_heads)
        cx.op("gpsimd", "dma_start", out=k.dbg["attn"], in_=view(O_B, P, [8 * T], BF16)[:],
              inc=("DBG", 16))
        for e_ in ("vector", "scalar", "tensor", "sync"):
            cx.wait(e_, "DBG")

    cx.wait("tensor", "LW", lw_o)
    t_h14 = headA[NH - 2]["scoresT"]
    t_h15 = headA[NH - 1]["scoresT"]
    cx.wait("sync", "T", t_h14)
    cx.wait("vector", "T", t_h15)
    ep_free = 0
    sy_done = 0
    for m in range(TT_N):
        cx.wait("sync", "V", ep_free)
        cx.op("sync", "dma_start", out=ep_x[:], in_=k.x_d[x0 + m * P:x0 + (m + 1) * P, :],
              inc=("LX", 16))
        lx_m = cx.now("LX")
        for nf in range(2):
            bank = pb[nf]
            cx.wait("tensor", "V", b01[nf])
            for kk in range(8):
                cx.op("tensor", "matmul", bank[:, 0:512], attn_sb[:, kk, m * P:(m + 1) * P],
                      wo_sb[:, kk, nf * 512:(nf + 1) * 512], start=(kk == 0),
                      stop=(kk == 7), inc=("T", 1) if kk == 7 else None)
        t_wo = cx.now("T")
        cx.wait("vector", "T", t_wo)
        cx.wait("vector", "LX", lx_m)
        cx.wait("vector", "SY", sy_done)
        for nf in range(2):
            cx.op("vector", "tensor_tensor", ep_y1[:, nf * 512:(nf + 1) * 512],
                  pb[nf][:, 0:512], ep_x[:, nf * 512:(nf + 1) * 512], ALU.add)
        v_y1 = cx.now("V")
        b01[0] = b01[1] = v_y1
        cx.wait("sync", "V", v_y1)
        cx.op("sync", "dma_start", out=k.y1_d[m * P:(m + 1) * P, :], in_=ep_y1[:],
              inc=("SY", 16))
        sy_done = cx.now("SY")
        k.ln(ep_y1[:], ep_xn2[:], C, xsq)
        vln2 = cx.now("V")
        k.quant(ep_xn2[:], xq_tm[:, m, :], C, O_F, sw=[("V", vln2)])
        ep_free = cx.now("V")
    if dbgon:
        cx.wait("gpsimd", "V")
        cx.op("gpsimd", "dma_start", out=k.dbg["xq"], in_=view(O_H, P, [TT_N * C], BF16)[:],
              inc=("DBG", 16))
        for e_ in ("vector", "scalar", "tensor", "sync"):
            cx.wait(e_, "DBG")


def emit_ffn(k, b):
    cx, view, pb = k.cx, k.view, k.pb
    dbgon = k.debug and b == 0
    xq_tm = view(O_H, P, [TT_N, C], BF16)
    wdT = view(F_WDT, P, [22, KB], BF16)
    xq_fm = view(F_XQ, P, [TT_N, 8, P], BF16)
    wgn = [view(F_STR + i * 8 * KB, P, [8, 512], BF16) for i in range(2)]
    wun = [view(F_STR + 16 * KB + i * 8 * KB, P, [8, 512], BF16) for i in range(2)]
    hq = [view(F_HQ + m * 5632, P, [IF], BF16) for m in range(8)]
    sG = [view(F_TR + i * 2 * KB, P, [512], F32) for i in range(2)]
    hbuf = [view(F_TR + 4 * KB + i * 2 * KB, P, [512], F32) for i in range(2)]
    hqf = [view(F_TR + 8 * KB + i * 5632, P, [22, P], BF16) for i in range(2)]
    Q_SCR = F_TR + 20 * KB
    y1t = [view(F_TR + 30 * KB + i * 4 * KB, P, [C], F32) for i in range(2)]
    outt = [view(F_TR + 38 * KB + i * 4 * KB, P, [C], F32) for i in range(2)]

    tb23 = (pb[2].bitcast(BF16), pb[3].bitcast(BF16))
    k.transpose_blocks(view(O_H, P, [TT_N * C], BF16)[:],
                       view(F_XQ, P, [TT_N * C], BF16)[:], TT_N * 8, tb23)
    tp_xq = cx.now("V")
    cx.op("gpsimd", "dma_start", out=wdT[:], in_=k.wdq_d.rearrange("(a p) n -> p a n", p=P),
          inc=("LW", 16))
    lw_d = cx.now("LW")

    for th in range(2):
        m0 = th * 8
        silu_done = [0, 0]
        hmul_done = [0, 0]
        hbuf_guard = [0, 0]
        quant_last = {}
        t_chunk = [0, 0]
        for n in range(6):
            nb = 0
            ncols = FFN_CHUNK[n]
            noff = FFN_OFF[n]
            cx.wait("sync", "T", t_chunk[nb])
            cx.op("sync", "dma_start", out=wgn[nb][:, :, 0:ncols],
                  in_=k.wgq_d[:, noff:noff + ncols].rearrange("(a p) c -> p a c", p=P),
                  inc=("LC", 16))
            cx.op("sync", "dma_start", out=wun[nb][:, :, 0:ncols],
                  in_=k.wuq_d[:, noff:noff + ncols].rearrange("(a p) c -> p a c", p=P),
                  inc=("LC", 16))
            lc = cx.now("LC")
            cx.wait("tensor", "LC", lc)
            cx.wait("tensor", "V", tp_xq)
            for mi in range(8):
                m = m0 + mi
                gbank, ubank = pb[mi % 2], pb[2 + mi % 2]
                cx.wait("tensor", "S", silu_done[mi % 2])
                cx.wait("tensor", "V", hmul_done[mi % 2])
                for kk in range(8):
                    cx.op("tensor", "matmul", gbank[:, 0:ncols],
                          xq_fm[:, m, kk, :], wgn[nb][:, kk, 0:ncols],
                          start=(kk == 0), stop=(kk == 7),
                          inc=("T", 1) if kk == 7 else None)
                for kk in range(8):
                    cx.op("tensor", "matmul", ubank[:, 0:ncols],
                          xq_fm[:, m, kk, :], wun[nb][:, kk, 0:ncols],
                          start=(kk == 0), stop=(kk == 7),
                          inc=("T", 1) if kk == 7 else None)
                t_mm = cx.now("T")
                cx.wait("scalar", "T", t_mm)
                cx.wait("scalar", "V", hmul_done[mi % 2])
                cx.op("scalar", "activation", sG[mi % 2][:, 0:ncols], gbank[:, 0:ncols],
                      AF.Silu)
                s_silu = cx.now("S")
                silu_done[mi % 2] = s_silu
                cx.wait("vector", "S", s_silu)
                cx.wait("vector", "V", hbuf_guard[mi % 2])
                cx.op("vector", "tensor_tensor", hbuf[mi % 2][:, 0:ncols],
                      sG[mi % 2][:, 0:ncols], ubank[:, 0:ncols], ALU.mult)
                hmul_done[mi % 2] = cx.now("V")
                k.quant(hbuf[mi % 2][:, 0:ncols], hq[mi][:, noff:noff + ncols], ncols,
                        Q_SCR, sw=[("V", hmul_done[mi % 2])])
                quant_last[mi] = hbuf_guard[mi % 2] = cx.now("V")
            t_chunk[nb] = cx.now("T")
        dn_guard = 0
        st_guard = [0, 0]
        for mi in range(8):
            m = m0 + mi
            mb = mi % 2
            cx.wait("tensor", "V", quant_last[mi])
            k.transpose_blocks(hq[mi][:], hqf[mb][:, :, :].rearrange("p a c -> p (a c)"),
                               22, (pb[6].bitcast(BF16), pb[7].bitcast(BF16)))
            tp_h = cx.now("V")
            cx.wait("sync", "V", max(dn_guard, tp_xq))
            cx.op("sync", "dma_start", out=y1t[mb][:], in_=k.y1_d[m * P:(m + 1) * P, :],
                  inc=("LX", 16))
            lx = cx.now("LX")
            cx.wait("tensor", "V", tp_h)
            cx.wait("tensor", "LW", lw_d)
            cx.wait("tensor", "V", dn_guard)
            for nf in range(2):
                bank = pb[4 + nf]
                for kk in range(22):
                    cx.op("tensor", "matmul", bank[:, 0:512], hqf[mb][:, kk, :],
                          wdT[:, kk, nf * 512:(nf + 1) * 512], start=(kk == 0),
                          stop=(kk == 21), inc=("T", 1) if kk == 21 else None)
            t_dn = cx.now("T")
            cx.wait("vector", "T", t_dn)
            cx.wait("vector", "LX", lx)
            cx.wait("vector", "ST", st_guard[mb])
            for nf in range(2):
                cx.op("vector", "tensor_tensor", outt[mb][:, nf * 512:(nf + 1) * 512],
                      pb[4 + nf][:, 0:512], y1t[mb][:, nf * 512:(nf + 1) * 512], ALU.add)
            vout = cx.now("V")
            dn_guard = vout
            cx.wait("gpsimd", "V", vout)
            cx.op("gpsimd", "dma_start", out=k.out_d[b * T + m * P:b * T + (m + 1) * P, :],
                  in_=outt[mb][:], inc=("ST", 16))
            st_guard[mb] = cx.now("ST")
        if dbgon and th == 0:
            cx.wait("gpsimd", "V")
            cx.op("gpsimd", "dma_start", out=k.dbg["hq"], in_=hq[0][:], inc=("DBG", 16))
            for e_ in ("vector", "scalar", "tensor", "sync"):
                cx.wait(e_, "DBG")


def build_program(debug=False, upto="full"):
    k = K(debug=debug)
    cx = k.cx
    emit_prep(k)
    cx.barrier()
    if upto != "prep":
        for b in range(NB):
            emit_attn(k, b, upto=upto if upto.startswith("a_") is False else upto[2:])
            cx.barrier()
            if upto.startswith("a_") or upto == "attn0":
                break
            emit_ffn(k, b)
            cx.barrier()
            if upto == "ffn0":
                break
    cx.wait("gpsimd", "ST")
    cx.wait("gpsimd", "DBG")
    cx.wait("gpsimd", "STP")
    cx.wait("gpsimd", "SY")

    with k.nc.Block() as block:
        @block.sync
        def _(e):
            for f in cx.streams["sync"]:
                f(e)

        @block.scalar
        def _(e):
            for f in cx.streams["scalar"]:
                f(e)

        @block.vector
        def _(e):
            for f in cx.streams["vector"]:
                f(e)

        @block.tensor
        def _(e):
            for f in cx.streams["tensor"]:
                f(e)

        @block.gpsimd
        def _(e):
            for f in cx.streams["gpsimd"]:
                f(e)

    return k


_PROG_CACHE = {}


def _get_prog(debug=False):
    key = bool(debug)
    if key not in _PROG_CACHE:
        _PROG_CACHE[key] = build_program(debug=key)
    return _PROG_CACHE[key]


def _host_inputs(inputs):
    f32 = np.float32
    perm = np.zeros(QR, np.int64)
    for h in range(NH):
        for d_ in range(16):
            perm[h * 16 + d_] = h * 32 + d_
            perm[256 + h * 16 + d_] = h * 32 + 16 + d_
    com = {
        "wkv": np.ascontiguousarray(inputs["L0_attn_W_kv"], f32),
        "gkv": np.ascontiguousarray(np.asarray(inputs["L0_attn_g_kv"]).reshape(2, P).T, f32),
        "wkdec": np.ascontiguousarray(inputs["L0_attn_W_kdec"], f32),
        "wvdec": np.ascontiguousarray(inputs["L0_attn_W_vdec"], f32),
        "wq": np.ascontiguousarray(inputs["L0_attn_W_q"], f32),
        "wqdec": np.ascontiguousarray(inputs["L0_attn_W_qdec"], f32),
        "wkrope": np.ascontiguousarray(np.asarray(inputs["L0_attn_W_krope"])[:, perm], f32),
        "wqrope": np.ascontiguousarray(np.asarray(inputs["L0_attn_W_qrope"])[:, perm], f32),
        "wo": np.ascontiguousarray(inputs["L0_attn_W_o"], f32),
        "wg": np.ascontiguousarray(inputs["L0_ffn_w_gate"], f32),
        "wu": np.ascontiguousarray(inputs["L0_ffn_w_up"], f32),
        "wd": np.ascontiguousarray(inputs["L0_ffn_w_down"], f32),
    }
    inv_freq = 1.0 / 10000.0 ** (np.arange(0, RD, 2, dtype=f32) / RD)
    tpos = np.arange(T, dtype=f32)
    freqs = np.outer(tpos, inv_freq)
    emb = np.concatenate([freqs, freqs], -1)
    cos = np.cos(emb).astype(f32)
    sin = np.sin(emb).astype(f32)
    com["cos1"] = np.ascontiguousarray(np.tile(cos[:, :16].T, (8, 1)))
    com["sin1"] = np.ascontiguousarray(np.tile(sin[:, :16].T, (8, 1)))
    com["cos2"] = np.ascontiguousarray(np.tile(cos[:, 16:].T, (8, 1)))
    com["sin2"] = np.ascontiguousarray(np.tile(sin[:, 16:].T, (8, 1)))
    r = np.arange(P)[:, None]
    c = np.arange(512)[None, :]
    com["maskc"] = (r <= c).astype(ml_dtypes.bfloat16)
    com["ident"] = np.eye(P, dtype=ml_dtypes.bfloat16)
    return com


def kernel(**inputs):
    com = _host_inputs(inputs)
    x = np.ascontiguousarray(np.asarray(inputs["L0_attn_x"]), np.float32)
    k = _get_prog(debug=False)
    in_maps = []
    for core in range(8):
        m = dict(com)
        m["x"] = np.ascontiguousarray(x[core * 2:(core + 1) * 2].reshape(NB * T, C))
        in_maps.append(m)
    res = run_bass_kernel_spmd(k.nc, in_maps, core_ids=list(range(8)))
    out = np.empty((16, T, C), np.float32)
    for core in range(8):
        out[core * 2:(core + 1) * 2] = res.results[core]["out"].reshape(NB, T, C)
    return out


# version 07
# speedup vs baseline: 1.4526x; 1.0076x over previous
import math
import numpy as np
import ml_dtypes
import concourse.bass as bass
import concourse.mybir as mybir
from concourse.bass_utils import run_bass_kernel_spmd

F32 = mybir.dt.float32
BF16 = mybir.dt.bfloat16
AF = mybir.ActivationFunctionType
ALU = mybir.AluOpType
AX = mybir.AxisListType

P = 128
T, C, NH, HD, RD, KVR, QR, IF = 2048, 1024, 16, 64, 32, 256, 512, 2816
TT_N = T // P
NB = 2
KB = 1024
RSC = 1.0 / math.sqrt(96.0)
AMGM_A = 12.0
FFN_CHUNK = [512] * 5 + [256]
FFN_OFF = [0, 512, 1024, 1536, 2048, 2560]


def _ap_span(ap):
    try:
        name = ap.tensor.name
        esz = {F32: 4, BF16: 2, mybir.dt.uint8: 1}.get(ap.dtype, 4)
        dims = list(ap.ap)
        reach = 1
        for st_, ct_ in dims[1:]:
            reach += (ct_ - 1) * abs(st_)
        lo = ap.offset * esz
        return (name, lo, lo + reach * esz)
    except Exception:
        return None


def _overlap(a, b):
    return a and b and a[0] == b[0] and a[1] < b[2] and b[1] < a[2]


class Ctx:
    ENGS = ("sync", "scalar", "vector", "tensor", "gpsimd")
    AUTO = {"vector": "V", "scalar": "S"}
    _DMAISH = ("dma_start", "dma_start_transpose", "nop")

    def __init__(self, nc):
        self.nc = nc
        self.streams = {e: [] for e in self.ENGS}
        self.sems = {}
        self.cnt = {}
        self._guards = []
        self.recent = {e: [] for e in self.ENGS}

    def sem(self, name):
        if name not in self.sems:
            g = self.nc.semaphore(name)
            h = g.__enter__()
            self._guards.append(g)
            self.sems[name] = h
            self.cnt[name] = 0
        return self.sems[name]

    def op(self, eng, method, *a, inc=None, **k):
        auto = self.AUTO.get(eng) if method not in self._DMAISH else None
        if auto is not None:
            aps = [x for x in a if isinstance(x, bass.AP)]
            aps += [x for x in k.values() if isinstance(x, bass.AP)]
            spans = [_ap_span(x) for x in aps]
            wspans = spans[0:1]
            rspans = spans[1:]
            need = 0
            for seq_after, ws, rs in self.recent[eng]:
                hit = any(_overlap(w, sp) for w in ws for sp in spans) or \
                      any(_overlap(w, r) for w in wspans for r in rs)
                if hit:
                    need = max(need, seq_after)
            if need > 0:
                self.wait(eng, auto, need)
            if inc is None:
                inc = (auto, 1)
            elif inc[0] != auto:
                raise AssertionError(f"op on {eng} with inc {inc}")
            self.sem(auto)
            seq_after = self.cnt[auto] + inc[1]
            self.recent[eng].append((seq_after, wspans, rspans))
            if len(self.recent[eng]) > 8:
                self.recent[eng].pop(0)
        iname = None
        if inc is not None:
            iname, n = inc
            self.sem(iname)
            self.cnt[iname] += n
        self.streams[eng].append(("op", method, a, k, iname,
                                  self.cnt[iname] if iname else 0))

    def wait(self, eng, name, val=None):
        self.sem(name)
        if val is None:
            val = self.cnt[name]
        if val <= 0:
            return
        self.streams[eng].append(("wait", name, val))

    def finalize(self):
        pruned = {}
        waited = {n: set() for n in self.sems}
        for eng, st in self.streams.items():
            out = []
            hi = {}
            for ent in st:
                if ent[0] == "wait":
                    _, nm, val = ent
                    if val <= hi.get(nm, 0):
                        continue
                    hi[nm] = val
                    waited[nm].add(val)
                out.append(ent)
            pruned[eng] = out
        steps = {n: (16 if any(ent[0] == "op" and ent[4] == n and ent[5] % 16 == 0
                               and "dma" in ent[1] for st in pruned.values()
                               for ent in st) else 1) for n in self.sems}
        ordered = {n: sorted(waited[n]) for n in self.sems}
        import bisect
        remap = {}
        for n, vals in ordered.items():
            stp = steps[n]
            remap[n] = {v: (i + 1) * stp for i, v in enumerate(vals)}
        self._final = {}
        for eng, st in pruned.items():
            out = []
            for ent in st:
                if ent[0] == "wait":
                    _, nm, val = ent
                    vals = ordered[nm]
                    i = bisect.bisect_left(vals, val)
                    assert i < len(vals) and vals[i] == val, (nm, val)
                    out.append(("wait", nm, (i + 1) * steps[nm]))
                else:
                    _, method, a, k, iname, lv = ent
                    if iname is not None and lv in waited[iname]:
                        out.append(("op", method, a, k, iname, remap[iname][lv]))
                    elif iname is not None and "dma" in method:
                        out.append(("op", method, a, k, iname, -1))
                    else:
                        out.append(("op", method, a, k, None, 0))
            self._final[eng] = out
        dma_sems = set()
        for eng, st in self.streams.items():
            for ent in st:
                if ent[0] == "op" and ent[4] is not None and "dma" in ent[1]:
                    dma_sems.add(ent[4])
        for eng, st in pruned.items():
            out = []
            for ent in st:
                if ent[0] == "wait":
                    _, nm, val = ent
                    if nm in dma_sems:
                        out.append(("wait", nm, val))
                    else:
                        vals = ordered[nm]
                        i = bisect.bisect_left(vals, val)
                        out.append(("wait", nm, (i + 1) * steps[nm]))
                else:
                    _, method, a, k, iname, lv = ent
                    if iname is None:
                        out.append(("op", method, a, k, None, 0))
                    elif iname in dma_sems:
                        out.append(("op", method, a, k, iname,
                                    16 if lv % 16 == 0 else 1, True))
                    elif lv in waited[iname]:
                        out.append(("op", method, a, k, iname, remap[iname][lv]))
                    else:
                        out.append(("op", method, a, k, None, 0))
            self._final[eng] = out

    def emit_engine(self, eng, e):
        track = {}
        for ent in self._final[eng]:
            if ent[0] == "wait":
                _, nm, val = ent
                e.wait_ge(self.sems[nm], val)
            else:
                if len(ent) == 7:
                    _, method, a, k, iname, n, _dma = ent
                    inst = getattr(e, method)(*a, **k)
                    inst.then_inc(self.sems[iname], n)
                else:
                    _, method, a, k, iname, target = ent
                    inst = getattr(e, method)(*a, **k)
                    if iname is not None:
                        cur = track.get(iname, 0)
                        inst.then_inc(self.sems[iname], target - cur)
                        track[iname] = target
                    else:
                        pass

    def now(self, name):
        self.sem(name)
        return self.cnt[name]

    def barrier(self):
        engs = self.ENGS
        marks = {}
        for e in engs:
            self.op(e, "nop", inc=("bar_" + e, 1))
            marks[e] = self.now("bar_" + e)
        for e in engs:
            for o in engs:
                if o != e:
                    self.wait(e, "bar_" + o, marks[o])


class K:

    def __init__(self, debug=False):
        self.debug = debug
        nc = bass.Bass("TRN2", target_bir_lowering=False, detect_race_conditions=False)
        self.nc = nc
        cx = Ctx(nc)
        self.cx = cx

        dri = lambda n, s, dt=F32: nc.dram_tensor(n, s, dt, kind="ExternalInput").ap()
        self.x_d = dri("x", [NB * T, C])
        self.wkv_d = dri("wkv", [C, KVR])
        self.gkv_d = dri("gkv", [P, 2])
        self.wkdec_d = dri("wkdec", [KVR, C])
        self.wvdec_d = dri("wvdec", [KVR, C])
        self.wq_d = dri("wq", [C, QR])
        self.wqdec_d = dri("wqdec", [QR, C])
        self.wkrope_d = dri("wkrope", [C, QR])
        self.wqrope_d = dri("wqrope", [QR, QR])
        self.wo_d = dri("wo", [C, C])
        self.wg_d = dri("wg", [IF, C])
        self.wu_d = dri("wu", [IF, C])
        self.wd_d = dri("wd", [C, IF])
        self.cos1_d = dri("cos1", [P, T])
        self.sin1_d = dri("sin1", [P, T])
        self.cos2_d = dri("cos2", [P, T])
        self.sin2_d = dri("sin2", [P, T])
        self.mask_d = dri("maskc", [P, 512], BF16)
        self.ident_d = dri("ident", [P, P], BF16)

        self.out_d = nc.dram_tensor("out", [NB * T, C], F32, kind="ExternalOutput").ap()
        self.y1_d = nc.dram_tensor("y1s", [T, C], F32, kind="Internal").ap()
        self.wgq_d = nc.dram_tensor("wgq", [C, IF], BF16, kind="Internal").ap()
        self.wuq_d = nc.dram_tensor("wuq", [C, IF], BF16, kind="Internal").ap()
        self.wdq_d = nc.dram_tensor("wdq", [IF, C], BF16, kind="Internal").ap()

        self.dbg = {}
        if debug:
            def dbg_t(n, s, dt=F32):
                self.dbg[n] = nc.dram_tensor("d_" + n, s, dt, kind="ExternalOutput").ap()
            dbg_t("xn_tm", [P, TT_N * C], BF16)
            dbg_t("kvn_tm", [P, TT_N * KVR], BF16)
            dbg_t("kvc0", [P, KVR], F32)
            dbg_t("sm12", [P, 4], F32)
            dbg_t("qc", [P, 4 * T], BF16)
            dbg_t("khat", [P, 4 * T], BF16)
            dbg_t("qhat", [P, 4 * T], BF16)
            dbg_t("kb0", [97, T], BF16)
            dbg_t("qb0", [97, T], BF16)
            dbg_t("vbuf", [P, TT_N * NH * 65], BF16)
            dbg_t("attn", [P, 8 * T], BF16)
            dbg_t("xq", [P, TT_N * C], BF16)
            dbg_t("hq", [P, IF], BF16)
            dbg_t("wgqd", [C, IF], BF16)

        AR = 202 * KB
        g = nc.sbuf_tensor("arena", [P, AR], mybir.dt.uint8)
        self.arena = g.__enter__().ap()
        cx._guards.append(g)

        self._qst = {}
        self.pb = []
        for i in range(8):
            pg = nc.psum_tensor(f"pb{i}", [P, 512], F32)
            self.pb.append(pg.__enter__().ap())
            cx._guards.append(pg)

    def view(self, off, rows, shape, dt):
        esz = {F32: 4, BF16: 2}[dt]
        n = 1
        for s in shape:
            n *= s
        ap = self.arena[0:rows, off:off + n * esz].bitcast(dt)
        if len(shape) > 1:
            dims = " ".join(f"d{i}" for i in range(len(shape)))
            kw = {f"d{i}": shape[i] for i in range(1, len(shape))}
            ap = ap.rearrange(f"p ({dims}) -> p {dims}", **kw)
        return ap

    def sm(self, i):
        return self.arena[0:P, 1280 + 4 * i: 1280 + 4 * (i + 1)].bitcast(F32)

    @property
    def ident(self):
        return self.arena[0:P, 2560:2816].bitcast(BF16)

    def transpose_blocks(self, src_flat, dst_flat, nblk, banks, tsem="T", vsem="V"):
        cx = self.cx
        v0 = cx.now(vsem)
        for g in range(nblk):
            pbank = banks[g % 2]
            cx.wait("tensor", vsem, v0 + max(0, g - 1))
            cx.op("tensor", "transpose", pbank[:, 0:P],
                  src_flat[:, g * P:(g + 1) * P], self.ident, inc=(tsem, 1))
            cx.wait("vector", tsem)
            cx.op("vector", "tensor_copy", dst_flat[:, g * P:(g + 1) * P],
                  pbank[:, 0:P], inc=(vsem, 1))

    def quant(self, src, out, FD, scr, sw=None, vw=None):
        cx = self.cx
        NBLK = FD // 16
        v = self.view(scr, P, [FD], F32)
        t = self.view(scr + 4 * FD, P, [FD], F32)
        s1 = self.view(scr + 8 * FD, P, [FD], F32)
        s2 = self.view(scr + 12 * FD, P, [FD], F32)
        am = self.view(scr + 16 * FD, P, [NBLK], F32)
        sc = self.view(scr + 16 * FD + 4 * NBLK, P, [NBLK], F32)
        rs = self.view(scr + 16 * FD + 8 * NBLK, P, [NBLK], F32)
        t3 = t.rearrange("p (b c) -> p b c", c=16)
        v3 = v.rearrange("p (b c) -> p b c", c=16)
        rs_b = rs.unsqueeze(2).broadcast_to([P, NBLK, 16])
        sc_b = sc.unsqueeze(2).broadcast_to([P, NBLK, 16])
        s13 = s1.rearrange("p (b c) -> p b c", c=16)

        st = self._qst.setdefault(scr, {"vfree": 0})
        cx.wait("scalar", "V", st["vfree"])
        if sw:
            for sem, val in sw:
                cx.wait("scalar", sem, val)
        cx.op("scalar", "activation", v[:], src, AF.Abs)
        s_abs = cx.now("S")
        if vw:
            for sem, val in vw:
                cx.wait("vector", sem, val)
        cx.wait("vector", "S", s_abs)
        cx.op("vector", "reduce_max", am[:], v3, axis=AX.X)
        cx.op("vector", "tensor_scalar", sc[:], am[:], 1.0 / 6.0, 1e-12, ALU.mult, ALU.max)
        cx.op("vector", "reciprocal", rs[:], sc[:])
        cx.op("vector", "tensor_tensor", t3, v3, rs_b, ALU.mult)
        st["vfree"] = cx.now("V")
        cx.op("vector", "tensor_single_scalar", s1[:], t[:], 0.25, ALU.is_gt)
        cx.op("vector", "scalar_tensor_tensor", s1[:], t[:], 0.75, s1[:], ALU.is_gt, ALU.add)
        cx.op("vector", "scalar_tensor_tensor", s1[:], t[:], 1.25, s1[:], ALU.is_gt, ALU.add)
        cx.op("vector", "scalar_tensor_tensor", s1[:], t[:], 1.75, s1[:], ALU.is_gt, ALU.add)
        cx.op("vector", "tensor_single_scalar", s2[:], t[:], 2.5, ALU.is_gt)
        cx.op("vector", "scalar_tensor_tensor", s2[:], t[:], 3.5, s2[:], ALU.is_gt, ALU.add)
        cx.op("vector", "scalar_tensor_tensor", s2[:], t[:], 5.0, s2[:], ALU.is_gt, ALU.add)
        cx.op("vector", "scalar_tensor_tensor", s2[:], t[:], 5.0, s2[:], ALU.is_gt, ALU.add)
        cx.op("vector", "scalar_tensor_tensor", s1[:], s2[:], 2.0, s1[:], ALU.mult, ALU.add)
        cx.op("vector", "tensor_tensor", s13, s13, sc_b, ALU.mult)
        cx.op("vector", "tensor_scalar", s2[:], src, 0.0, -0.5, ALU.is_gt, ALU.add)
        cx.op("vector", "tensor_tensor", out, s1[:], s2[:], ALU.mult)

    def ln(self, xsrc, xout, FD, xsq, vw=None):
        cx = self.cx
        sm = self.sm
        if vw:
            for sem, val in vw:
                cx.wait("vector", sem, val)
        cx.op("vector", "tensor_tensor", xsq[:], xsrc, xsrc, ALU.mult)
        cx.op("vector", "reduce_sum", sm(0), xsrc, axis=AX.X)
        cx.op("vector", "reduce_sum", sm(1), xsq[:], axis=AX.X)
        cx.op("vector", "tensor_scalar", sm(0), sm(0), 1.0 / FD, None, ALU.mult)
        cx.op("vector", "tensor_tensor", sm(2), sm(0), sm(0), ALU.mult)
        cx.op("vector", "tensor_scalar", sm(1), sm(1), 1.0 / FD, None, ALU.mult)
        cx.op("vector", "tensor_tensor", sm(1), sm(1), sm(2), ALU.subtract)
        cx.op("vector", "tensor_scalar", sm(1), sm(1), 1e-5, None, ALU.add)
        v_var = cx.now("V")
        cx.wait("scalar", "V", v_var)
        cx.op("scalar", "activation", sm(1), sm(1), AF.Ln)
        cx.op("scalar", "activation", sm(2), sm(1), AF.Exp, scale=-0.5)
        s_r = cx.now("S")
        cx.wait("vector", "S", s_r)
        cx.op("vector", "tensor_scalar", xsq[:], xsrc, sm(0), None, ALU.subtract)
        cx.op("vector", "tensor_scalar", xout, xsq[:], sm(2), None, ALU.mult)


O_MASK = 0
O_ONES = 1024
O_GKV = 1216
O_RB = 1536
O_W = 4 * KB
O_A = 24 * KB
O_B = 56 * KB
O_H = 88 * KB
O_F = 122 * KB
O_G = 138 * KB
O_J = 154 * KB
O_K = 170 * KB
O_L = 178 * KB
O_M = 186 * KB
F_WDT = 4 * KB
F_XQ = 48 * KB
F_TR = 80 * KB
F_STR = 126 * KB
F_HQ = 152 * KB
PR_STG = 60 * KB
PR_Q = 72 * KB
PR_TP = 78 * KB
PR_SCR = 84 * KB


def emit_prep(k):
    cx, view = k.cx, k.view
    cx.op("sync", "dma_start", out=view(O_MASK, P, [512], BF16), in_=k.mask_d,
          inc=("LX", 16))
    cx.op("sync", "dma_start", out=k.arena[0:P, O_GKV:O_GKV + 8].bitcast(F32),
          in_=k.gkv_d, inc=("LX", 16))
    ones64 = k.arena[0:1, O_ONES:O_ONES + 128].bitcast(BF16)
    ones96 = k.arena[0:96, 1184:1186].bitcast(BF16)
    cx.op("sync", "dma_start", out=k.ident, in_=k.ident_d, inc=("LX", 16))
    lx0 = cx.now("LX")
    cx.wait("vector", "LX", lx0)
    cx.op("vector", "memset", ones64, 1.0)
    cx.op("vector", "memset", ones96, 1.0)

    PR_STG2 = [24 * KB, 36 * KB]
    PR_Q2 = [48 * KB, 54 * KB]
    PR_TP2 = [60 * KB, 66 * KB]
    vq_done = [0, 0]
    tq_done = [0, 0]
    st_done = [0, 0]
    gi = 0
    for wsrc, wdst, NT, FD in [(k.wg_d, k.wgq_d, 22, C), (k.wu_d, k.wuq_d, 22, C),
                               (k.wd_d, k.wdq_d, 8, IF)]:
        nstr = FD // P
        for it in range(NT):
            sl = gi % 2
            stg = view(PR_STG2[sl], P, [FD], F32)
            qo = view(PR_Q2[sl], P, [FD], BF16)
            tp = view(PR_TP2[sl], P, [nstr * P], BF16)[:]
            cx.wait("sync", "V", vq_done[sl])
            cx.op("sync", "dma_start", out=stg[:], in_=wsrc[it * P:(it + 1) * P, :],
                  inc=("LX", 16))
            lx = cx.now("LX")
            if FD <= 1024:
                k.quant(stg[:], qo[:], FD, PR_SCR, sw=[("LX", lx)],
                        vw=[("V", tq_done[sl])])
            else:
                for ci, (c0, c1) in enumerate(((0, 1024), (1024, 2048), (2048, 2816))):
                    k.quant(stg[:, c0:c1], qo[:, c0:c1], c1 - c0, PR_SCR,
                            sw=[("LX", lx)],
                            vw=[("V", tq_done[sl])] if ci == 0 else None)
            vq_done[sl] = cx.now("V")
            cx.wait("tensor", "V", vq_done[sl])
            cx.wait("tensor", "STP", st_done[sl])
            k.transpose_blocks(qo[:], tp, nstr,
                               (k.pb[2].bitcast(BF16), k.pb[3].bitcast(BF16)))
            tq_done[sl] = cx.now("V")
            cx.wait("sync", "V", tq_done[sl])
            od = wdst.rearrange("(s f) q -> f s q", f=P)[:, :, it * P:(it + 1) * P]
            cx.op("sync", "dma_start", out=od,
                  in_=tp.rearrange("f (s p) -> f s p", s=nstr), inc=("STP", 16))
            st_done[sl] = cx.now("STP")
            gi += 1
    cx.wait("sync", "STP")
    if k.debug:
        cx.wait("gpsimd", "STP")
        cx.op("gpsimd", "dma_start", out=k.dbg["wgqd"], in_=k.wgq_d, inc=("DBG", 16))
        for e_ in ("vector", "scalar", "tensor", "sync"):
            cx.wait(e_, "DBG")


def emit_attn(k, b, upto="full"):
    cx, view, sm, pb = k.cx, k.view, k.sm, k.pb
    dbgon = k.debug and b == 0
    x0 = b * T

    wkv_sb = view(O_W, P, [8, KVR], BF16)
    wq_sb = view(O_W + 4 * KB, P, [8, QR], BF16)
    wkrope_sb = view(O_W + 12 * KB, P, [8, QR], BF16)
    wvdec_sb = view(O_W, P, [2, C], BF16)
    wqrope_sb = view(O_W + 12 * KB, P, [4, QR], BF16)
    wkdec_sb = view(O_W + 16 * KB, P, [2, C], BF16)
    wqdec_sb = view(O_W + 4 * KB, P, [4, C], BF16)
    wo_sb = view(O_A + 16 * KB, P, [8, C], BF16)

    xn_tm = view(O_A, P, [TT_N, C], BF16)
    tabs = [view(O_A + i * 8 * KB, P, [T], F32) for i in range(2)]
    ksq = view(O_A, P, [T], BF16)
    p_sb = [view(O_A + 4 * KB + i * KB, P, [512], BF16) for i in range(4)]
    mrow = view(O_A + 8 * KB, 1, [T], BF16)
    R_f32 = view(O_A + 12 * KB, 1, [512], F32)
    pvs = view(O_A + 14 * KB, 64, [512], BF16)
    xn_fm = view(O_B, P, [TT_N, 8, P], BF16)
    attn_sb = view(O_B, P, [8, T], BF16)
    rtmp_a = view(O_B, P, [T], F32)
    rtmp_b = view(O_B + 8 * KB, P, [T], F32)
    rtmp_c = view(O_B + 16 * KB, P, [T], F32)
    rtmp_d = view(O_B + 24 * KB, P, [T], F32)
    kraw = view(O_H, P, [4, T], F32)
    vbuf = view(O_H, P, [TT_N, NH * 65], BF16)
    xq_tm = view(O_H, P, [TT_N, C], BF16)
    khat = view(O_F, P, [4, T], BF16)
    qhat = view(O_G, P, [4, T], BF16)
    kqb = [view(O_J + i * 4 * KB, P, [T], BF16) for i in range(4)]
    ep_x = view(O_J, P, [C], F32)
    ep_y1 = view(O_J + 4 * KB, P, [C], F32)
    ep_xn2 = view(O_J + 8 * KB, P, [C], F32)
    xstage = [view(O_K, P, [C], F32), view(O_K + 4 * KB, P, [C], F32)]
    xsq = view(O_J + 12 * KB, P, [C], F32)
    kvn_tm = view(O_K, P, [TT_N, KVR], BF16)
    kvn_fm = view(O_L, P, [TT_N, 2, P], BF16)
    qc_sb = view(O_M, P, [4, T], BF16)
    mask_sb = view(O_MASK, P, [512], BF16)
    ones64 = k.arena[0:1, O_ONES:O_ONES + 128].bitcast(BF16)
    ones96 = k.arena[0:96, 1184:1186].bitcast(BF16)
    R_bf = k.arena[0:1, O_RB:O_RB + 1024].bitcast(BF16)
    gkv_col = k.arena[0:P, O_GKV:O_GKV + 8].bitcast(F32)

    for wd_, sb_ in ((k.wkv_d, wkv_sb), (k.wq_d, wq_sb), (k.wkrope_d, wkrope_sb)):
        cx.op("gpsimd", "dma_start", out=sb_, in_=wd_.rearrange("(a p) n -> p a n", p=P),
              inc=("LW", 16))
    lw_a = cx.now("LW")

    tb23 = (pb[2].bitcast(BF16), pb[3].bitcast(BF16))
    ln_done = [0, 0]
    for m in range(TT_N):
        sl = m % 2
        cx.wait("sync", "V", ln_done[sl])
        cx.op("sync", "dma_start", out=xstage[sl][:],
              in_=k.x_d[x0 + m * P: x0 + (m + 1) * P, :], inc=("LX", 16))
        lx = cx.now("LX")
        k.ln(xstage[sl][:], xn_tm[:, m, :], C, xsq, vw=[("LX", lx)])
        ln_done[sl] = cx.now("V")
        cx.wait("tensor", "V", ln_done[sl])
        k.transpose_blocks(xn_tm[:, m, :],
                           xn_fm[:, m, :, :].rearrange("p a c -> p (a c)"), 8, tb23)
    v_xnf = cx.now("V")
    if upto == "ln":
        return
    if dbgon:
        cx.wait("gpsimd", "V", v_xnf)
        cx.op("gpsimd", "dma_start", out=k.dbg["xn_tm"],
              in_=view(O_A, P, [TT_N * C], BF16)[:], inc=("DBG", 16))
        for e_ in ("vector", "scalar", "tensor", "sync"):
            cx.wait(e_, "DBG")

    cx.wait("tensor", "LW", lw_a)
    cx.wait("tensor", "V", v_xnf)
    b01 = [0, 0]
    for m in range(TT_N):
        sl = m % 2
        bank = pb[sl]
        cx.wait("tensor", "V", b01[sl])
        for kk in range(8):
            cx.op("tensor", "matmul", bank[:, 0:KVR], xn_fm[:, m, kk, :],
                  wkv_sb[:, kk, :], start=(kk == 0), stop=(kk == 7),
                  inc=("T", 1) if kk == 7 else None)
        tmm = cx.now("T")
        cx.wait("vector", "T", tmm)
        cx.op("vector", "tensor_copy", xsq[:, 0:KVR], bank[:, 0:KVR])
        cx.op("vector", "tensor_tensor", xsq[:, 0:KVR], xsq[:, 0:KVR],
              xsq[:, 0:KVR], ALU.mult)
        cx.op("vector", "reduce_sum", sm(0), xsq[:, 0:KVR], axis=AX.X)
        cx.op("vector", "tensor_scalar", sm(1), sm(0), 1.0 / KVR, 1e-6, ALU.mult,
              ALU.add)
        v_var = cx.now("V")
        cx.wait("scalar", "V", v_var)
        cx.op("scalar", "activation", sm(1), sm(1), AF.Ln)
        cx.op("scalar", "activation", sm(2), sm(1), AF.Exp, scale=-0.5)
        s_rstd = cx.now("S")
        cx.wait("vector", "S", s_rstd)
        cx.op("vector", "tensor_scalar", kvn_tm[:, m, :], bank[:, 0:KVR], sm(2), None,
              ALU.mult)
        b01[sl] = cx.now("V")
    v_kv = cx.now("V")
    cx.wait("tensor", "V", v_kv)
    k.transpose_blocks(view(O_K, P, [TT_N * KVR], BF16)[:],
                       view(O_L, P, [TT_N * KVR], BF16)[:], TT_N * 2, tb23)
    v_kvf = cx.now("V")
    if dbgon:
        cx.wait("gpsimd", "V", v_kv)
        cx.op("gpsimd", "dma_start", out=k.dbg["kvn_tm"],
              in_=view(O_K, P, [TT_N * KVR], BF16)[:], inc=("DBG", 16))
        for e_ in ("vector", "scalar", "tensor", "sync"):
            cx.wait(e_, "DBG")

    def proj16(wsb, nk, src_qc, dst, scale, fold4):
        for mq in range(4):
            for nt in range(4):
                i = mq * 4 + nt
                bank = pb[i % 2]
                cx.wait("tensor", "V", b01[i % 2])
                for kk in range(nk):
                    if fold4:
                        o_ap = bank[:, 0:512].rearrange("p (a c) -> p a c", a=4)
                        mov = xn_fm[:, nt * 4:(nt + 1) * 4, kk, :]
                    else:
                        o_ap = bank[:, 0:512]
                        mov = src_qc[:, kk, nt * 512:(nt + 1) * 512]
                    cx.op("tensor", "matmul", o_ap, wsb[:, kk, mq * P:(mq + 1) * P],
                          mov, start=(kk == 0), stop=(kk == nk - 1),
                          inc=("T", 1) if kk == nk - 1 else None)
                tmm = cx.now("T")
                cx.wait("vector", "T", tmm)
                if scale is None:
                    cx.op("vector", "tensor_copy", dst[:, mq, nt * 512:(nt + 1) * 512],
                          bank[:, 0:512])
                else:
                    cx.op("vector", "tensor_scalar", dst[:, mq, nt * 512:(nt + 1) * 512],
                          bank[:, 0:512], scale, None, ALU.mult)
                b01[i % 2] = cx.now("V")

    proj16(wq_sb, 8, None, qc_sb, RSC, True)
    if dbgon:
        cx.wait("gpsimd", "V")
        cx.op("gpsimd", "dma_start", out=k.dbg["qc"], in_=view(O_M, P, [4 * T], BF16)[:],
              inc=("DBG", 16))
        for e_ in ("vector", "scalar", "tensor", "sync"):
            cx.wait(e_, "DBG")
    proj16(wkrope_sb, 8, None, kraw, None, True)
    t_kraw = cx.now("T")
    v_kraw = cx.now("V")
    if upto == "proj":
        return

    cx.wait("gpsimd", "T", t_kraw)
    cx.op("gpsimd", "dma_start", out=wqrope_sb,
          in_=k.wqrope_d.rearrange("(a p) n -> p a n", p=P), inc=("LW", 16))
    cx.op("gpsimd", "dma_start", out=wqdec_sb,
          in_=k.wqdec_d.rearrange("(a p) n -> p a n", p=P), inc=("LW", 16))
    lw_b = cx.now("LW")
    for wdsrc, wdsb in ((k.wvdec_d, wvdec_sb), (k.wkdec_d, wkdec_sb)):
        cx.wait("sync", "V")
        for kk in range(2):
            stg = ep_x if kk == 0 else ep_y1
            cx.op("sync", "dma_start", out=stg[:], in_=wdsrc[kk * P:(kk + 1) * P, :],
                  inc=("LX", 16))
        lx = cx.now("LX")
        cx.wait("vector", "LX", lx)
        for kk in range(2):
            stg = ep_x if kk == 0 else ep_y1
            cx.op("vector", "tensor_scalar", wdsb[:, kk, :], stg[:],
                  gkv_col[:, kk:kk + 1], None, ALU.mult)
    v_wdec = cx.now("V")

    def rope_pass(raw, hat, tmp_a, tmp_b, raw_ready_v, tmp_free_t):
        for second in (False, True):
            cd = k.cos2_d if second else k.cos1_d
            sd = k.sin2_d if second else k.sin1_d
            cx.wait("sync", "V")
            cx.wait("sync", "T", t_kraw)
            cx.op("sync", "dma_start", out=tabs[0][:], in_=cd, inc=("LX", 16))
            cx.op("sync", "dma_start", out=tabs[1][:], in_=sd, inc=("LX", 16))
            lx = cx.now("LX")
            cx.wait("vector", "LX", lx)
            cx.wait("vector", "V", raw_ready_v)
            cx.wait("vector", "T", tmp_free_t)
            for pt in range(2):
                o = (2 if second else 0) + pt
                ob = (0 if second else 2) + pt
                cx.op("vector", "tensor_tensor", tmp_a[:], raw[:, o, :], tabs[0][:],
                      ALU.mult)
                cx.op("vector", "tensor_tensor", tmp_b[:], raw[:, ob, :], tabs[1][:],
                      ALU.mult)
                cx.op("vector", "tensor_tensor", hat[:, o, :], tmp_a[:], tmp_b[:],
                      ALU.add if second else ALU.subtract)

    rope_pass(kraw, khat, rtmp_a, rtmp_b, v_kraw, t_kraw)
    v_khat = cx.now("V")

    qraw = kraw
    cx.wait("tensor", "V", v_khat)
    cx.wait("tensor", "LW", lw_b)
    proj16(wqrope_sb, 4, qc_sb, qraw, None, False)
    v_qraw = cx.now("V")
    rope_pass(qraw, qhat, rtmp_c, rtmp_d, v_qraw, t_kraw)
    v_qhat = cx.now("V")
    if upto == "rope":
        return
    if dbgon:
        cx.wait("gpsimd", "V", v_qhat)
        cx.op("gpsimd", "dma_start", out=k.dbg["khat"], in_=view(O_F, P, [4 * T], BF16)[:],
              inc=("DBG", 16))
        cx.op("gpsimd", "dma_start", out=k.dbg["qhat"], in_=view(O_G, P, [4 * T], BF16)[:],
              inc=("DBG", 16))
        for e_ in ("vector", "scalar", "tensor", "sync"):
            cx.wait(e_, "DBG")

    cx.wait("vector", "V", v_qhat)
    for m in range(TT_N):
        cx.op("vector", "memset",
              vbuf[:, m, :].rearrange("p (h c) -> p h c", c=65)[:, :, 64:65], 1.0)
    v_ones = cx.now("V")
    cx.wait("tensor", "V", v_ones)
    for m in range(TT_N):
        for nv in range(2):
            i = m * 2 + nv
            bank = pb[i % 2]
            cx.wait("tensor", "V", b01[i % 2])
            for kk in range(2):
                cx.op("tensor", "matmul", bank[:, 0:512], kvn_fm[:, m, kk, :],
                      wvdec_sb[:, kk, nv * 512:(nv + 1) * 512], start=(kk == 0),
                      stop=(kk == 1), inc=("T", 1) if kk == 1 else None)
            tmm = cx.now("T")
            cx.wait("vector", "T", tmm)
            cx.op("vector", "tensor_copy",
                  vbuf[:, m, nv * 8 * 65:(nv * 8 + 8) * 65].rearrange(
                      "p (h c) -> p h c", c=65)[:, :, 0:64],
                  bank[:, 0:512].rearrange("p (h c) -> p h c", c=64))
            b01[i % 2] = cx.now("V")
    v_vdone = cx.now("V")
    if upto == "vdec":
        return
    if dbgon:
        cx.wait("gpsimd", "V", v_vdone)
        cx.op("gpsimd", "dma_start", out=k.dbg["vbuf"],
              in_=view(O_H, P, [TT_N * NH * 65], BF16)[:], inc=("DBG", 16))
        for e_ in ("vector", "scalar", "tensor", "sync"):
            cx.wait(e_, "DBG")

    cx.wait("gpsimd", "T", t_kraw)
    cx.op("gpsimd", "dma_start", out=wo_sb, in_=k.wo_d.rearrange("(a p) n -> p a n", p=P),
          inc=("LW", 16))
    lw_o = cx.now("LW")

    sc_banks = [pb[2], pb[3], pb[6]]
    sc_slot3 = [0, 0, 0]
    headA = {}

    def emit_headA(h):
        kbx = kqb[(h % 2) * 2]
        qbx = kqb[(h % 2) * 2 + 1]
        st = {}
        free_t = headA.get(h - 2, {}).get("scoresT", 0)
        src_pt, src_off = h // 8, (h * 16) % P
        cx.wait("sync", "T", free_t)
        cx.wait("sync", "V", v_qhat)
        for half in range(2):
            o1 = src_off
            pt_ = src_pt + (2 if half else 0)
            r0 = 64 + 16 * half
            src2 = k.arena[o1:o1 + 16, O_F + pt_ * 4 * KB:O_F + pt_ * 4 * KB + 2 * T] \
                if False else None
            in3 = k.arena[o1:o1 + 16, O_F + pt_ * 4 * KB: O_G + pt_ * 4 * KB + 4 * KB]
            cx.op("sync", "dma_start", out=kbx[r0:r0 + 16, :],
                  in_=khat[o1:o1 + 16, pt_, :], inc=("AS", 16))
            cx.op("sync", "dma_start", out=qbx[r0:r0 + 16, :],
                  in_=qhat[o1:o1 + 16, pt_, :], inc=("AS", 16))
        as_rope = cx.now("AS")
        for which, (wsb, nk, bx) in enumerate([(wkdec_sb, 2, kbx), (wqdec_sb, 4, qbx)]):
            for nt in range(4):
                bank = pb[nt % 2]
                cx.wait("tensor", "V", b01[nt % 2])
                if which == 0 and nt == 0:
                    cx.wait("tensor", "T", free_t)
                for kk in range(nk):
                    if which == 0:
                        mov = kvn_fm[:, nt * 4:(nt + 1) * 4, kk, :]
                        o_ap = bank[0:64, 0:512].rearrange("p (a c) -> p a c", a=4)
                    else:
                        mov = qc_sb[:, kk, nt * 512:(nt + 1) * 512]
                        o_ap = bank[0:64, 0:512]
                    cx.op("tensor", "matmul", o_ap, wsb[:, kk, h * 64:(h + 1) * 64],
                          mov, start=(kk == 0), stop=(kk == nk - 1),
                          inc=("T", 1) if kk == nk - 1 else None)
                tmm = cx.now("T")
                cx.wait("vector", "T", tmm)
                cx.op("vector", "tensor_copy", bx[0:64, nt * 512:(nt + 1) * 512],
                      bank[0:64, 0:512])
                b01[nt % 2] = cx.now("V")
        cx.op("vector", "memset", kbx[96:97, :], 1.0)
        v_dec = cx.now("V")
        cx.wait("vector", "AS", as_rope)

        def norm_of(bx, outbase):
            cx.op("vector", "tensor_tensor", ksq[0:96, :], bx[0:96, :], bx[0:96, :],
                  ALU.mult)
            vsq = cx.now("V")
            cx.wait("tensor", "V", vsq)
            for ch in range(4):
                cx.op("tensor", "matmul", pb[7][32 * ch:32 * ch + 1, 0:512], ones96,
                      ksq[0:96, ch * 512:(ch + 1) * 512], start=True, stop=True,
                      tile_position=(0, 32 * ch), inc=("T", 1))
            tmm = cx.now("T")
            cx.wait("vector", "T", tmm)
            for ch in range(4):
                cx.op("vector", "reduce_max", sm(outbase + ch)[0:1, :],
                      pb[7][32 * ch:32 * ch + 1, 0:512], axis=AX.X)

        norm_of(kbx, 4)
        norm_of(qbx, 8)
        for (a_, b_) in ((4, 5), (6, 7), (4, 6), (8, 9), (10, 11), (8, 10)):
            cx.op("vector", "tensor_tensor", sm(a_)[0:1, :], sm(a_)[0:1, :],
                  sm(b_)[0:1, :], ALU.max)
        cx.op("vector", "tensor_scalar", sm(4)[0:1, :], sm(4)[0:1, :], 0.5 / AMGM_A,
              None, ALU.mult)
        cx.op("vector", "tensor_scalar", sm(8)[0:1, :], sm(8)[0:1, :], 0.5 * AMGM_A,
              None, ALU.mult)
        cx.op("vector", "tensor_tensor", sm(4)[0:1, :], sm(4)[0:1, :], sm(8)[0:1, :],
              ALU.add)
        cx.op("vector", "tensor_scalar", sm(4)[0:1, :], sm(4)[0:1, :], 1.0, -1.0,
              ALU.add, ALU.mult)
        cx.op("vector", "tensor_copy", mrow[0:1, :], sm(4)[0:1, :].broadcast_to([1, T]))
        v_mrow = cx.now("V")
        cx.wait("sync", "V", v_mrow)
        cx.op("sync", "dma_start", out=qbx[96:97, :], in_=mrow[0:1, :], inc=("AS", 16))
        st["asm"] = cx.now("AS")
        st["vdec"] = v_dec
        return st

    def emit_headB(h, st):
        kbx = kqb[(h % 2) * 2]
        qbx = kqb[(h % 2) * 2 + 1]
        cx.wait("tensor", "AS", st["asm"])
        cx.wait("tensor", "V", st["vdec"])

        def emit_epi(i):
            pvb = pb[4 + i % 2]
            cx.wait("vector", "T", pv_done[i])
            cx.op("vector", "reciprocal", R_f32[0:1, :], pvb[64:65, 0:512])
            cx.op("vector", "tensor_copy", R_bf[0:1, 0:512], R_f32[0:1, :])
            v_r = cx.now("V")
            cx.op("vector", "tensor_copy", pvs[:, :], pvb[0:64, 0:512])
            v_pvs = cx.now("V")
            cx.wait("tensor", "V", v_pvs)
            cx.op("tensor", "matmul", pvb[0:64, 0:512], ones64[0:1, 0:64],
                  R_bf[0:1, 0:512], start=True, stop=True, inc=("T", 1))
            t_rb = cx.now("T")
            cx.wait("vector", "T", t_rb)
            cx.op("vector", "tensor_tensor",
                  attn_sb[(h % 2) * 64:(h % 2) * 64 + 64, h // 2, i * 512:(i + 1) * 512],
                  pvs[:, :], pvb[0:64, 0:512], ALU.mult)
            epi_done[i] = cx.now("V")

        pv_done = {}
        epi_done = {-2: 0, -1: 0}
        for i in range(4):
            cx.wait("tensor", "V", epi_done[i - 2])
            jmax = 4 * i + 4
            pend = []

            def emit_pv(ent, i=i, jmax=jmax):
                j, c0, rsem, rval = ent
                cx.wait("tensor", rsem, rval)
                cx.op("tensor", "matmul", pb[4 + i % 2][0:65, c0:512],
                      vbuf[:, j, h * 65:(h + 1) * 65], p_sb[j % 4][:, c0:512],
                      start=(j == 0), stop=(j == jmax - 1), inc=("T", 1))

            for j in range(jmax):
                d = j - 4 * i
                c0 = 128 * d if d >= 0 else 0
                sl3 = j % 3
                sbank = sc_banks[sl3]
                cx.wait("tensor", "S", sc_slot3[sl3])
                cx.op("tensor", "matmul", sbank[:, c0:512], kbx[0:97, j * P:(j + 1) * P],
                      qbx[0:97, i * 512 + c0:(i + 1) * 512], start=True, stop=True,
                      inc=("T", 1))
                t_sc = cx.now("T")
                cx.wait("scalar", "T", t_sc)
                cx.op("scalar", "activation", p_sb[j % 4][:, c0:512], sbank[:, c0:512],
                      AF.Exp)
                sc_slot3[sl3] = cx.now("S")
                if d >= 0:
                    cx.wait("vector", "S", sc_slot3[sl3])
                    cx.op("vector", "tensor_tensor", p_sb[j % 4][:, c0:512],
                          p_sb[j % 4][:, c0:512], mask_sb[:, 0:512 - c0], ALU.mult)
                    pend.append((j, c0, "V", cx.now("V")))
                else:
                    pend.append((j, c0, "S", sc_slot3[sl3]))
                if len(pend) > 2:
                    emit_pv(pend.pop(0))
            for ent in pend:
                emit_pv(ent)
            pv_done[i] = cx.now("T")
            if i >= 1:
                emit_epi(i - 1)
        emit_epi(3)
        cx.wait("tensor", "V", epi_done[3])
        return cx.now("T")

    headA[0] = emit_headA(0)
    for h in range(NH):
        if h + 1 < NH:
            headA[h + 1] = emit_headA(h + 1)
        headA[h]["scoresT"] = emit_headB(h, headA[h])
        if dbgon and h == 0:
            cx.wait("gpsimd", "T", headA[0]["scoresT"])
            cx.op("gpsimd", "dma_start", out=k.dbg["kb0"], in_=kqb[0][0:97, :],
                  inc=("DBG", 16))
            cx.op("gpsimd", "dma_start", out=k.dbg["qb0"], in_=kqb[1][0:97, :],
                  inc=("DBG", 16))
            for e_ in ("vector", "scalar", "tensor", "sync"):
                cx.wait(e_, "DBG")
    v_heads = cx.now("V")
    if upto == "heads":
        return
    if dbgon:
        cx.wait("gpsimd", "V", v_heads)
        cx.op("gpsimd", "dma_start", out=k.dbg["attn"], in_=view(O_B, P, [8 * T], BF16)[:],
              inc=("DBG", 16))
        for e_ in ("vector", "scalar", "tensor", "sync"):
            cx.wait(e_, "DBG")

    cx.wait("tensor", "LW", lw_o)
    t_h14 = headA[NH - 2]["scoresT"]
    t_h15 = headA[NH - 1]["scoresT"]
    cx.wait("sync", "T", t_h14)
    cx.wait("vector", "T", t_h15)
    ep_free = 0
    sy_done = 0
    for m in range(TT_N):
        cx.wait("sync", "V", ep_free)
        cx.op("sync", "dma_start", out=ep_x[:], in_=k.x_d[x0 + m * P:x0 + (m + 1) * P, :],
              inc=("LX", 16))
        lx_m = cx.now("LX")
        for nf in range(2):
            bank = pb[nf]
            cx.wait("tensor", "V", b01[nf])
            for kk in range(8):
                cx.op("tensor", "matmul", bank[:, 0:512], attn_sb[:, kk, m * P:(m + 1) * P],
                      wo_sb[:, kk, nf * 512:(nf + 1) * 512], start=(kk == 0),
                      stop=(kk == 7), inc=("T", 1) if kk == 7 else None)
        t_wo = cx.now("T")
        cx.wait("vector", "T", t_wo)
        cx.wait("vector", "LX", lx_m)
        cx.wait("vector", "SY", sy_done)
        for nf in range(2):
            cx.op("vector", "tensor_tensor", ep_y1[:, nf * 512:(nf + 1) * 512],
                  pb[nf][:, 0:512], ep_x[:, nf * 512:(nf + 1) * 512], ALU.add)
        v_y1 = cx.now("V")
        b01[0] = b01[1] = v_y1
        cx.wait("sync", "V", v_y1)
        cx.op("sync", "dma_start", out=k.y1_d[m * P:(m + 1) * P, :], in_=ep_y1[:],
              inc=("SY", 16))
        sy_done = cx.now("SY")
        k.ln(ep_y1[:], ep_xn2[:], C, xsq)
        vln2 = cx.now("V")
        k.quant(ep_xn2[:], xq_tm[:, m, :], C, O_F, sw=[("V", vln2)])
        ep_free = cx.now("V")
    if dbgon:
        cx.wait("gpsimd", "V")
        cx.op("gpsimd", "dma_start", out=k.dbg["xq"], in_=view(O_H, P, [TT_N * C], BF16)[:],
              inc=("DBG", 16))
        for e_ in ("vector", "scalar", "tensor", "sync"):
            cx.wait(e_, "DBG")


def emit_ffn(k, b):
    cx, view, pb = k.cx, k.view, k.pb
    dbgon = k.debug and b == 0
    xq_tm = view(O_H, P, [TT_N, C], BF16)
    wdT = view(F_WDT, P, [22, KB], BF16)
    xq_fm = view(F_XQ, P, [TT_N, 8, P], BF16)
    wgn = [view(F_STR + i * 8 * KB, P, [8, 512], BF16) for i in range(2)]
    wun = [view(F_STR + 16 * KB + i * 8 * KB, P, [8, 512], BF16) for i in range(2)]
    hq = [view(F_HQ + m * 5632, P, [IF], BF16) for m in range(8)]
    sG = [view(F_TR + i * 2 * KB, P, [512], F32) for i in range(2)]
    hbuf = [view(F_TR + 4 * KB + i * 2 * KB, P, [512], F32) for i in range(2)]
    hqf = [view(F_TR + 8 * KB + i * 5632, P, [22, P], BF16) for i in range(2)]
    Q_SCR = F_TR + 20 * KB
    y1t = [view(F_TR + 30 * KB + i * 4 * KB, P, [C], F32) for i in range(2)]
    outt = [view(F_TR + 38 * KB + i * 4 * KB, P, [C], F32) for i in range(2)]

    tb23 = (pb[2].bitcast(BF16), pb[3].bitcast(BF16))
    k.transpose_blocks(view(O_H, P, [TT_N * C], BF16)[:],
                       view(F_XQ, P, [TT_N * C], BF16)[:], TT_N * 8, tb23)
    tp_xq = cx.now("V")
    cx.op("gpsimd", "dma_start", out=wdT[:], in_=k.wdq_d.rearrange("(a p) n -> p a n", p=P),
          inc=("LW", 16))
    lw_d = cx.now("LW")

    for th in range(2):
        m0 = th * 8
        silu_done = [0, 0]
        hmul_done = [0, 0]
        hbuf_guard = [0, 0]
        quant_last = {}
        t_chunk = [0, 0]
        for n in range(6):
            nb = 0
            ncols = FFN_CHUNK[n]
            noff = FFN_OFF[n]
            cx.wait("sync", "T", t_chunk[nb])
            cx.op("sync", "dma_start", out=wgn[nb][:, :, 0:ncols],
                  in_=k.wgq_d[:, noff:noff + ncols].rearrange("(a p) c -> p a c", p=P),
                  inc=("LC", 16))
            cx.op("sync", "dma_start", out=wun[nb][:, :, 0:ncols],
                  in_=k.wuq_d[:, noff:noff + ncols].rearrange("(a p) c -> p a c", p=P),
                  inc=("LC", 16))
            lc = cx.now("LC")
            cx.wait("tensor", "LC", lc)
            cx.wait("tensor", "V", tp_xq)
            for mi in range(8):
                m = m0 + mi
                gbank, ubank = pb[mi % 2], pb[2 + mi % 2]
                cx.wait("tensor", "S", silu_done[mi % 2])
                cx.wait("tensor", "V", hmul_done[mi % 2])
                for kk in range(8):
                    cx.op("tensor", "matmul", gbank[:, 0:ncols],
                          xq_fm[:, m, kk, :], wgn[nb][:, kk, 0:ncols],
                          start=(kk == 0), stop=(kk == 7),
                          inc=("T", 1) if kk == 7 else None)
                for kk in range(8):
                    cx.op("tensor", "matmul", ubank[:, 0:ncols],
                          xq_fm[:, m, kk, :], wun[nb][:, kk, 0:ncols],
                          start=(kk == 0), stop=(kk == 7),
                          inc=("T", 1) if kk == 7 else None)
                t_mm = cx.now("T")
                cx.wait("scalar", "T", t_mm)
                cx.wait("scalar", "V", hmul_done[mi % 2])
                cx.op("scalar", "activation", sG[mi % 2][:, 0:ncols], gbank[:, 0:ncols],
                      AF.Silu)
                s_silu = cx.now("S")
                silu_done[mi % 2] = s_silu
                cx.wait("vector", "S", s_silu)
                cx.wait("vector", "V", hbuf_guard[mi % 2])
                cx.op("vector", "tensor_tensor", hbuf[mi % 2][:, 0:ncols],
                      sG[mi % 2][:, 0:ncols], ubank[:, 0:ncols], ALU.mult)
                hmul_done[mi % 2] = cx.now("V")
                k.quant(hbuf[mi % 2][:, 0:ncols], hq[mi][:, noff:noff + ncols], ncols,
                        Q_SCR, sw=[("V", hmul_done[mi % 2])])
                quant_last[mi] = hbuf_guard[mi % 2] = cx.now("V")
            t_chunk[nb] = cx.now("T")
        dn_guard = 0
        st_guard = [0, 0]
        for mi in range(8):
            m = m0 + mi
            mb = mi % 2
            cx.wait("tensor", "V", quant_last[mi])
            k.transpose_blocks(hq[mi][:], hqf[mb][:, :, :].rearrange("p a c -> p (a c)"),
                               22, (pb[6].bitcast(BF16), pb[7].bitcast(BF16)))
            tp_h = cx.now("V")
            cx.wait("sync", "V", max(dn_guard, tp_xq))
            cx.op("sync", "dma_start", out=y1t[mb][:], in_=k.y1_d[m * P:(m + 1) * P, :],
                  inc=("LX", 16))
            lx = cx.now("LX")
            cx.wait("tensor", "V", tp_h)
            cx.wait("tensor", "LW", lw_d)
            cx.wait("tensor", "V", dn_guard)
            for nf in range(2):
                bank = pb[4 + nf]
                for kk in range(22):
                    cx.op("tensor", "matmul", bank[:, 0:512], hqf[mb][:, kk, :],
                          wdT[:, kk, nf * 512:(nf + 1) * 512], start=(kk == 0),
                          stop=(kk == 21), inc=("T", 1) if kk == 21 else None)
            t_dn = cx.now("T")
            cx.wait("vector", "T", t_dn)
            cx.wait("vector", "LX", lx)
            cx.wait("vector", "ST", st_guard[mb])
            for nf in range(2):
                cx.op("vector", "tensor_tensor", outt[mb][:, nf * 512:(nf + 1) * 512],
                      pb[4 + nf][:, 0:512], y1t[mb][:, nf * 512:(nf + 1) * 512], ALU.add)
            vout = cx.now("V")
            dn_guard = vout
            cx.wait("gpsimd", "V", vout)
            cx.op("gpsimd", "dma_start", out=k.out_d[b * T + m * P:b * T + (m + 1) * P, :],
                  in_=outt[mb][:], inc=("ST", 16))
            st_guard[mb] = cx.now("ST")
        if dbgon and th == 0:
            cx.wait("gpsimd", "V")
            cx.op("gpsimd", "dma_start", out=k.dbg["hq"], in_=hq[0][:], inc=("DBG", 16))
            for e_ in ("vector", "scalar", "tensor", "sync"):
                cx.wait(e_, "DBG")


def build_program(debug=False, upto="full"):
    k = K(debug=debug)
    cx = k.cx
    emit_prep(k)
    cx.barrier()
    if upto != "prep":
        for b in range(NB):
            emit_attn(k, b, upto=upto if upto.startswith("a_") is False else upto[2:])
            cx.barrier()
            if upto.startswith("a_") or upto == "attn0":
                break
            emit_ffn(k, b)
            cx.barrier()
            if upto == "ffn0":
                break
    cx.wait("gpsimd", "ST")
    cx.wait("gpsimd", "DBG")
    cx.wait("gpsimd", "STP")
    cx.wait("gpsimd", "SY")

    cx.finalize()
    with k.nc.Block() as block:
        @block.sync
        def _(e):
            cx.emit_engine("sync", e)

        @block.scalar
        def _(e):
            cx.emit_engine("scalar", e)

        @block.vector
        def _(e):
            cx.emit_engine("vector", e)

        @block.tensor
        def _(e):
            cx.emit_engine("tensor", e)

        @block.gpsimd
        def _(e):
            cx.emit_engine("gpsimd", e)

    return k


_PROG_CACHE = {}


def _get_prog(debug=False):
    key = bool(debug)
    if key not in _PROG_CACHE:
        _PROG_CACHE[key] = build_program(debug=key)
    return _PROG_CACHE[key]


def _host_inputs(inputs):
    f32 = np.float32
    perm = np.zeros(QR, np.int64)
    for h in range(NH):
        for d_ in range(16):
            perm[h * 16 + d_] = h * 32 + d_
            perm[256 + h * 16 + d_] = h * 32 + 16 + d_
    com = {
        "wkv": np.ascontiguousarray(inputs["L0_attn_W_kv"], f32),
        "gkv": np.ascontiguousarray(np.asarray(inputs["L0_attn_g_kv"]).reshape(2, P).T, f32),
        "wkdec": np.ascontiguousarray(inputs["L0_attn_W_kdec"], f32),
        "wvdec": np.ascontiguousarray(inputs["L0_attn_W_vdec"], f32),
        "wq": np.ascontiguousarray(inputs["L0_attn_W_q"], f32),
        "wqdec": np.ascontiguousarray(inputs["L0_attn_W_qdec"], f32),
        "wkrope": np.ascontiguousarray(np.asarray(inputs["L0_attn_W_krope"])[:, perm], f32),
        "wqrope": np.ascontiguousarray(np.asarray(inputs["L0_attn_W_qrope"])[:, perm], f32),
        "wo": np.ascontiguousarray(inputs["L0_attn_W_o"], f32),
        "wg": np.ascontiguousarray(inputs["L0_ffn_w_gate"], f32),
        "wu": np.ascontiguousarray(inputs["L0_ffn_w_up"], f32),
        "wd": np.ascontiguousarray(inputs["L0_ffn_w_down"], f32),
    }
    inv_freq = 1.0 / 10000.0 ** (np.arange(0, RD, 2, dtype=f32) / RD)
    tpos = np.arange(T, dtype=f32)
    freqs = np.outer(tpos, inv_freq)
    emb = np.concatenate([freqs, freqs], -1)
    cos = np.cos(emb).astype(f32)
    sin = np.sin(emb).astype(f32)
    com["cos1"] = np.ascontiguousarray(np.tile(cos[:, :16].T, (8, 1)))
    com["sin1"] = np.ascontiguousarray(np.tile(sin[:, :16].T, (8, 1)))
    com["cos2"] = np.ascontiguousarray(np.tile(cos[:, 16:].T, (8, 1)))
    com["sin2"] = np.ascontiguousarray(np.tile(sin[:, 16:].T, (8, 1)))
    r = np.arange(P)[:, None]
    c = np.arange(512)[None, :]
    com["maskc"] = (r <= c).astype(ml_dtypes.bfloat16)
    com["ident"] = np.eye(P, dtype=ml_dtypes.bfloat16)
    return com


def kernel(**inputs):
    com = _host_inputs(inputs)
    x = np.ascontiguousarray(np.asarray(inputs["L0_attn_x"]), np.float32)
    k = _get_prog(debug=False)
    in_maps = []
    for core in range(8):
        m = dict(com)
        m["x"] = np.ascontiguousarray(x[core * 2:(core + 1) * 2].reshape(NB * T, C))
        in_maps.append(m)
    res = run_bass_kernel_spmd(k.nc, in_maps, core_ids=list(range(8)))
    out = np.empty((16, T, C), np.float32)
    for core in range(8):
        out[core * 2:(core + 1) * 2] = res.results[core]["out"].reshape(NB, T, C)
    return out
